# version 12
# baseline (speedup 1.0000x reference)
import contextlib
import numpy as np

import concourse.bass as bass
import concourse.mybir as mybir
from concourse.bass_utils import run_bass_kernel_spmd

F32 = mybir.dt.float32
BF16 = mybir.dt.bfloat16
Alu = mybir.AluOpType
Act = mybir.ActivationFunctionType
AX = mybir.AxisListType

MAGIC = 12582912.0
NCORE = 8
T = 2048
NT = 16384
D = 1024
FT = 8
IM = 2048
IMT = 16
CAP = 2560.0
E = 8
HALF = 1024
QTR = 512
NG = 16
NGA = 128
NQ = 24

MEAN_N = {i: 4096 * 1024 for i in range(20)}
MEAN_N[1] = 2048 * 1024
for _e in range(8):
    MEAN_N[10 + _e] = 2048 * 1024


class Sched:
    NDMA = 8
    NST = 4

    def __init__(self, sems):
        self.stopped = False
        self.items = []
        self.cnt = {e: 0 for e in "tvag"}
        for i in range(self.NDMA):
            self.cnt[f"d{i}"] = 0
        for i in range(self.NST):
            self.cnt[f"w{i}"] = 0
        self.sem = sems
        self.dseq = 0
        self.dseq_f = 0
        self.dseq_b = 0
        self.wseq = 0
        self.seen = {e: {} for e in "stvag"}

    def mark(self, eng, builder, sem_key=None, dma=False):
        if self.stopped:
            return (sem_key or eng, self.cnt[sem_key or eng])
        se = sem_key or eng
        n = 16 if dma else 1
        self.cnt[se] += n
        sem = self.sem

        def fn(E, b=builder, s=se, k=n):
            b(E).then_inc(sem[s], k)
        self.items.append((eng, fn))
        return (se, self.cnt[se])

    def wait_tok(self, eng, tok):
        if self.stopped:
            return
        key, c = tok
        if c <= 0 or self.seen[eng].get(key, 0) >= c:
            return
        self.seen[eng][key] = c
        sem = self.sem
        self.items.append((eng, lambda E, p=key, cc=c: E.wait_ge(sem[p], cc)))

    def wait(self, eng, prod, count=None):
        if prod == "s":
            for i in range(self.NDMA):
                self.wait_tok(eng, (f"d{i}", self.cnt[f"d{i}"]))
            return
        if prod == "w":
            for i in range(self.NST):
                self.wait_tok(eng, (f"w{i}", self.cnt[f"w{i}"]))
            return
        c = self.cnt[prod] if count is None else count
        self.wait_tok(eng, (prod, c))

    def dma_sem(self, pool="any"):
        if pool == "fast":
            key = f"d{self.dseq_f % 4}"
            self.dseq_f += 1
        elif pool == "bulk":
            key = f"d{4 + self.dseq_b % 4}"
            self.dseq_b += 1
        else:
            key = f"d{self.dseq % self.NDMA}"
            self.dseq += 1
        return key

    def st_sem(self):
        key = f"w{self.wseq % self.NST}"
        self.wseq += 1
        return key

    def barrier(self, engs="stvag", prods="stvag"):
        for e in engs:
            for p in prods:
                if p == e:
                    continue
                if p == "s":
                    continue
                self.wait(e, p)
        for e in engs:
            self.wait(e, "s")
            self.wait(e, "w")

    def replay(self, eng, engobj):
        for e, fn in self.items:
            if e == eng:
                fn(engobj)


def build_nc(taps=(), stop_at=None):
    nc = bass.Bass("TRN2", target_bir_lowering=False, detect_race_conditions=False)

    def din(name, shape, dt=F32):
        return nc.dram_tensor(name, shape, dt, kind="ExternalInput").ap()

    x_own_d = din("x_own", [T, D])
    x_all_d = din("x_all", [NT, D])
    maskrow_d = din("maskrow", [1, NGA])
    bsel_d = din("bselrow", [1, 2])
    cin_d = din("cin", [2, D])
    wmod_d = din("wmod", [48, 128, 1024])
    gwt_d = din("gwt", [D, E])
    gnwt_d = din("gnwt", [D, 1])
    sgw_d = din("sgw_t", [32, 128, 1024])
    sdw_d = din("sdw_t", [8, 128, 2048])
    egw_d = din("egw_t", [E, 32, 128, 1024])
    edw_d = din("edw_t", [E, 8, 128, 2048])
    w1_d = din("w1_t", [32, 128, 1024])
    w2_d = din("w2_t", [8, 128, 4096])

    out_d = nc.dram_tensor("out", [T, D], F32, kind="ExternalOutput").ap()
    dbg = {}
    for name, shape in taps:
        dbg[name] = nc.dram_tensor(name, shape, F32, kind="ExternalOutput").ap()

    scr_g_d = nc.dram_tensor("scr_g", [9, 16, 128, 2048], BF16, kind="Internal").ap()
    scr_d_d = nc.dram_tensor("scr_d", [9, 8, 128, 2048], BF16, kind="Internal").ap()
    scr_w1_d = nc.dram_tensor("scr_w1", [32, 128, 1024], BF16, kind="Internal").ap()
    scr_w2_d = nc.dram_tensor("scr_w2", [8, 128, 4096], BF16, kind="Internal").ap()

    es = contextlib.ExitStack()

    def sb(n, shape, d=F32):
        return es.enter_context(nc.sbuf_tensor(n, shape, d))

    def psb(n, shape, d=F32):
        return es.enter_context(nc.psum_tensor(n, shape, d))

    id128 = sb("id128", [128, 128])
    idbf = sb("idbf", [128, 128], BF16)
    tribf = sb("tribf", [128, 128], BF16)
    onesP = sb("onesP", [128, 128], BF16)
    ones_m1b = sb("ones_m1b", [128, 1], BF16)
    ones_k1b = sb("ones_k1b", [1, 128], BF16)
    e8sel = sb("e8sel", [8, 128], BF16)
    vecs = sb("vecs", [128, 48])
    mod_sb = sb("mod_sb", [128, 96])
    csT = sb("csT", [128, 16])
    bsel_bc = sb("bsel_bc", [128, 2])
    bsb = sb("bsb", [1, 2], BF16)
    gw2 = sb("gw2", [128, FT * E], BF16)
    sxa = sb("sxa", [128, 2 * FT * 11], BF16)
    sx2 = sb("sx2", [128, 2 * FT * 2], BF16)
    crow = sb("crow", [1, 8])
    agrow = sb("agrow", [1, E])
    c4row = sb("c4row", [1, E])
    cb_src = sb("cb_src", [1, 32], BF16)
    cbc = sb("cbc", [128, 64])
    offrow = sb("offrow", [1, E])
    maskb = sb("maskb", [1, NGA], BF16)
    rows_tm = sb("rows_tm", [128, NG * NQ])
    magic_col = sb("magicc", [128, 1])
    rowst = sb("rowst", [16, 512])
    past = sb("past", [16, 512])
    diagt = sb("diagt", [128, 128], BF16)
    mwcol = sb("mwcol", [128, 32])
    mwrow = sb("mwrow", [1, 32])
    mwc32 = sb("mwc32", [32, 1])
    mwbc = sb("mwbc", [128, 32])
    rmwbc = sb("rmwbc", [128, 32])
    cntinv = sb("cntinv", [1, 32])
    mwrow_b = sb("mwrow_b", [128, 32], BF16)
    xq1 = sb("xq1", [128, FT * T], BF16)
    acc = sb("acc", [128, FT * T])
    wkT = sb("wkT", [8, T], BF16)
    selB = sb("selB", [128, NG * E])
    selBb = sb("selBb", [128, NG * E], BF16)
    wselB = sb("wselB", [128, NG * E])
    wkeep = sb("wkeep", [128, NG * E])
    wkeepb = sb("wkeepb", [128, NG * E], BF16)
    offBrow = sb("offBrow", [1, NG * E])
    offBb = sb("offBb", [1, NG * E], BF16)
    cumsb = sb("cumsb", [128, NG * E])
    Bs1F = sb("Bs1F", [128, T], BF16)
    tA = sb("tA", [128, 1024])
    tB = sb("tB", [128, 1024])
    tC = sb("tC", [128, 1024], BF16)
    tD = sb("tD", [128, 1024], BF16)
    paxb = tD
    tE_ = sb("tE_", [128, 1024], BF16)
    tF = sb("tF", [128, 1024], BF16)
    trow = sb("trow", [1, 512])
    wst = [sb("wstA", [128, 2048]), sb("wstB", [128, 2048])]
    wqb = [sb("wqbA", [128, 2048], BF16), sb("wqbB", [128, 2048], BF16)]
    Bu = sb("Bu", [128, HALF], BF16)
    Bv = sb("Bv", [128, HALF], BF16)
    Bw = Bu
    Bx = Bv
    xq3 = xq1
    es1 = contextlib.ExitStack()

    def sb1(n, shape, d=F32):
        return es1.enter_context(nc.sbuf_tensor(n, shape, d))
    pafa = sb1("pafa", [128, FT * 512], BF16)
    paf2 = sb1("paf2", [128, 512], BF16)
    paflo = sb1("paflo", [128, FT * 512], BF16)
    statsB = sb1("statsB", [128, NG * 13])
    sxo = sb1("sxo", [128, FT * 11], BF16)
    sx2o = sb1("sx2o", [128, FT * 2], BF16)
    cbco = sb1("cbco", [128, 32])
    statsA = sb1("statsA", [128, NGA * 13], BF16)
    probsA = sb1("probsA", [128, NGA * E])
    selAb = sb1("selAb", [128, NGA * E], BF16)
    cntrow = sb1("cntrow", [1, NGA * E], BF16)
    tmA = sb1("tmA", [128, NGA])
    tmB = sb1("tmB", [128, NGA])
    tmC = sb1("tmC", [128, NGA])
    tmD = sb1("tmD", [128, NGA])
    tmE = sb1("tmE", [128, NGA])
    es1.close()
    es2 = contextlib.ExitStack()
    xbh = es2.enter_context(nc.sbuf_tensor("xbh", [128, FT * HALF], BF16))
    h1h = es2.enter_context(nc.sbuf_tensor("h1h", [128, FT * HALF], BF16))
    es2.close()
    es3 = contextlib.ExitStack()
    zt = es3.enter_context(nc.sbuf_tensor("zt", [128, IMT * HALF], BF16))
    es3.close()
    es4 = contextlib.ExitStack()
    xbh2 = es4.enter_context(nc.sbuf_tensor("xbh2", [128, FT * HALF], BF16))
    h1h2 = es4.enter_context(nc.sbuf_tensor("h1h2", [128, FT * HALF], BF16))
    es4.close()
    _wq20 = es.enter_context(nc.sbuf_tensor("wq20", [128, 4096], BF16))
    wq2 = [_wq20, _wq20]
    hft = es.enter_context(nc.sbuf_tensor("hft", [128, 32 * QTR], BF16))

    psR = [psb(f"psR{i}", [128, 1024]) for i in range(3)]
    ps_stat = psb("ps_stat", [16, 512])
    ps_misc = psb("ps_misc", [128, 512])

    sems = {}
    for key, nm in [("t", "Stn"), ("v", "Sve"), ("a", "Sac"), ("g", "Sgp")]:
        sems[key] = es.enter_context(nc.semaphore(nm))
    for i in range(Sched.NDMA):
        sems[f"d{i}"] = es.enter_context(nc.semaphore(f"Sd{i}"))
    for i in range(Sched.NST):
        sems[f"w{i}"] = es.enter_context(nc.semaphore(f"Sw{i}"))
    sems["dbg"] = es.enter_context(nc.semaphore("Sdbg"))

    S = Sched(sems)
    S.v_need_drain = False

    def ckpt(name):
        if stop_at is not None and name == stop_at:
            S.stopped = True
    guard = {}

    def dguard(buf):
        for tok in guard.get(buf, []):
            S.wait_tok("s", tok)

    def gset(buf, *toks):
        guard[buf] = list(toks)

    def gadd(buf, tok):
        guard.setdefault(buf, []).append(tok)

    def dma(dst, src, gbuf=None, slow=False, pool="any"):
        if gbuf is not None:
            dguard(gbuf)
        key = S.dma_sem(pool)
        S.wait_tok("s", (key, S.cnt[key]))
        return S.mark("s", lambda E_: E_.dma_start(
            dst, src, allow_slow_non_contiguous=slow), sem_key=key, dma=True)

    def dma_w(dst, src):
        key = S.st_sem()
        S.wait_tok("a", (key, S.cnt[key]))
        return S.mark("a", lambda E_: E_.dma_start(dst, src), sem_key=key, dma=True)

    def mm(out, lhsT, rhs, start=True, stop=True):
        return S.mark("t", lambda E_: E_.matmul(out, lhsT, rhs, start=start, stop=stop))

    def tr(out, in_, ident):
        return S.mark("t", lambda E_: E_.transpose(out, in_, ident))

    def v_(b):
        return S.mark("v", b)

    def a_(b):
        return S.mark("a", b)

    def g_(b):
        S.wait("g", "g")
        return S.mark("g", b)

    def _free_elems(ap):
        try:
            shp = list(ap.shape)
        except Exception:
            return 1 << 30
        n = 1
        for s in shp[1:]:
            n *= s
        return n

    def EM(eng, method, *args, **kwargs):
        if eng == "g":
            S.wait("g", "g")
        if eng == "v":
            if S.v_need_drain:
                S.wait("v", "v")
                S.v_need_drain = False
            if _free_elems(args[0]) < 128:
                S.v_need_drain = True

        def fn(E_, m=method, a=args, k=kwargs):
            return getattr(E_, m)(*a, **k)
        return S.mark(eng, fn)

    def rnd(out, in_):
        return EM("v", "tensor_scalar", out, in_, MAGIC, MAGIC, Alu.add,
                                              Alu.subtract)

    def tmcol(q, g0=0, ng=NG):
        return rows_tm[:, q * NG + g0: q * NG + g0 + ng]

    tmview = rows_tm[:].rearrange("p (q g) -> p q g", g=NG)

    def build_bcast(tmslice, dst, ng):
        for g in range(ng):
            S.wait("v", "t")
            EM("v", "tensor_scalar", diagt[:], idbf[:],
                                                tmslice[:, g:g + 1], None, Alu.mult)
            S.wait("t", "v")
            S.wait("t", "a")
            mm(ps_misc[:, 0:128], onesP[:], diagt[:])
            S.wait("a", "t")
            EM("a", "activation", dst[:, g * 128:(g + 1) * 128],
                                             ps_misc[:, 0:128], Act.Identity)
        S.wait("v", "a")

    def rows_to_tm(k, qglist, g0):
        for s in range(4):
            S.wait("t", "a")
            S.wait("t", "v")
            tr(ps_misc[0:128, 0:k], rowst[0:k, s * 128:(s + 1) * 128],
               id128[0:k, 0:k])
            S.wait("a", "t")
            for r, (q, gd) in enumerate(qglist):
                EM("a", "activation", 
                    tmview[:, q, g0 + gd + s: g0 + gd + s + 1],
                    ps_misc[0:128, r:r + 1], Act.Identity)

    def two_rows_to_tm(qa, g0a, qb, g0b):
        S.wait("a", "t")
        EM("a", "activation", rowst[0:1, :], ps_stat[0:1, 0:512],
                                    Act.Identity)
        EM("a", "activation", past[0:1, :], ps_misc[0:1, 0:512],
                                    Act.Identity)
        rows_to_tm(1, [(qa, 0)], g0a)
        S.wait("v", "a")
        EM("v", "tensor_copy", rowst[0:1, 0:512], past[0:1, 0:512])
        S.wait("a", "v")
        rows_to_tm(1, [(qb, 0)], g0b)

    def quant_strip(wstrip, wq_out, mi, ncol):
        EM("a", "activation", wstrip[:, 0:ncol], wstrip[:, 0:ncol],
                                    Act.Identity, scale=rmwbc[:, mi:mi + 1],
                                    bias=magic_col[:, 0:1])
        S.wait("v", "a")
        EM("v", "tensor_scalar", wstrip[:, 0:ncol], wstrip[:, 0:ncol],
                                       MAGIC, -1.0, Alu.subtract, Alu.max)
        EM("v", "tensor_scalar", wq_out[:, 0:ncol], wstrip[:, 0:ncol],
                                       1.0, None, Alu.min)

    def maxtree_to_tm(buf, width, ntile, qdst, g0, ng):
        n = ntile // 2
        while n >= 1:
            for j in range(n):
                EM("v", "tensor_tensor",
                    buf[:, j * width:(j + 1) * width],
                    buf[:, j * width:(j + 1) * width],
                    buf[:, (j + n) * width:(j + n + 1) * width], Alu.max)
            n //= 2
        S.wait("v", "t")
        EM("v", "tensor_copy", tC[0:64, 0:width], buf[64:128, 0:width])
        EM("v", "tensor_tensor", buf[0:64, 0:width], buf[0:64, 0:width],
                                       tC[0:64, 0:width], Alu.max)
        EM("v", "tensor_copy", tC[0:32, 0:width], buf[32:64, 0:width])
        EM("v", "tensor_tensor", buf[0:32, 0:width], buf[0:32, 0:width],
                                       tC[0:32, 0:width], Alu.max)
        nsub = width // 128
        for s in range(nsub):
            S.wait("t", "v")
            S.wait("t", "a")
            tr(ps_misc[:, 0:128].bitcast(BF16)[:, 0:32],
               buf[0:32, s * 128:(s + 1) * 128], idbf[0:32, 0:32])
            S.wait("a", "t")
            EM("a", "activation", tA[:, s * 32:(s + 1) * 32],
                                             ps_misc[:, 0:128].bitcast(BF16)[:, 0:32],
                                             Act.Identity)
        S.wait("v", "a")
        gadd("tAx", ("a", S.cnt["a"]))
        EM("v", "reduce_max", 
            tmcol(qdst, g0, ng),
            tA[:, 0:nsub * 32].rearrange("p (g k) -> p g k", k=32), axis=AX.X)

    EM("g", "memset", id128[:], 1.0)
    S.wait("g", "g")
    EM("g", "affine_select", id128[:], id128[:], [[1, 128]], Alu.is_equal,
                                   0.0, base=0, channel_multiplier=-1)
    EM("g", "memset", tribf[:], 1.0)
    S.wait("g", "g")
    EM("g", "affine_select", tribf[:], tribf[:], [[1, 128]], Alu.is_ge,
                                   0.0, base=0, channel_multiplier=-1)
    EM("g", "memset", onesP[:], 1.0)
    EM("g", "memset", ones_m1b[:], 1.0)
    EM("g", "memset", ones_k1b[:], 1.0)
    EM("g", "memset", mwcol[:], 0.0)
    EM("g", "memset", magic_col[:], MAGIC)

    EM("g", "memset", cntinv[:], 0.0)
    for mi, n in MEAN_N.items():
        EM("g", "memset", cntinv[0:1, mi:mi + 1], 1.0 / n)
    if taps:
        _tap_init = {
            "t_pafa": pafa, "t_past": past, "t_rows": rows_tm, "t_statsA": statsA,
            "t_zt": zt, "t_h1": h1h, "t_xbh": xbh, "t_xq1": xq1, "t_acc": acc,
            "t_wkeep": wkeep, "t_selB": selB, "t_wselB": wselB, "t_cum": cumsb,
        }
        for nm, _shape in taps:
            if nm in _tap_init:
                EM("g", "memset", _tap_init[nm][:], 0.0)
    S.wait("v", "g")
    EM("v", "tensor_copy", idbf[:], id128[:])

    d_c = dma(tA[0:2, 0:1024], cin_d[:])
    d_b = dma(trow[0:1, 0:2], bsel_d[:])
    d_m = dma(trow[0:1, 16:16 + NGA], maskrow_d[:])
    S.wait("a", "s")
    EM("a", "activation", tA[0:2, 0:1024], tA[0:2, 0:1024], Act.Silu)
    for k in range(8):
        S.wait("t", "a")
        tr(ps_misc[:, 0:2], tA[0:2, k * 128:(k + 1) * 128], id128[0:2, 0:2])
        S.wait("a", "t")
        EM("a", "activation", csT[:, 2 * k:2 * k + 2], ps_misc[:, 0:2],
                                         Act.Identity)
    S.wait("v", "s")
    EM("v", "tensor_copy", maskb[:], trow[0:1, 16:16 + NGA])
    for m in range(48):
        dd = dma(wst[m % 2][:, 0:1024], wmod_d[m], gbuf=f"mst{m % 2}")
        S.wait("t", "s")
        S.wait("t", "a")
        for k in range(8):
            mm(ps_misc[:, 0:2], wst[m % 2][:, k * 128:(k + 1) * 128],
               csT[:, 2 * k:2 * k + 2], start=(k == 0), stop=(k == 7))
        gset(f"mst{m % 2}", ("t", S.cnt["t"]))
        S.wait("a", "t")
        EM("a", "activation", mod_sb[:, 2 * m:2 * m + 2], ps_misc[:, 0:2],
                                         Act.Identity)
    S.wait("v", "s")
    EM("v", "tensor_copy", bsb[:], trow[0:1, 0:2])
    S.wait("t", "v")
    S.wait("t", "a")
    mm(ps_misc[:, 0:2], ones_k1b[0:1, :], bsb[:])
    S.wait("a", "t")
    EM("a", "activation", bsel_bc[:], ps_misc[:, 0:2], Act.Identity)
    S.wait("v", "a")
    mod3 = mod_sb[:].rearrange("p (m two) -> p m two", two=2)
    EM("v", "tensor_scalar", tA[:, 0:48], mod3[:, :, 1], bsel_bc[:, 1:2],
                                   None, Alu.mult)
    EM("v", "scalar_tensor_tensor", vecs[:], mod3[:, :, 0], bsel_bc[:, 0:1],
                                          tA[:, 0:48], Alu.mult, Alu.add)
    EM("v", "tensor_scalar", vecs[:, 8:16], vecs[:, 8:16], 1.0, None, Alu.add)
    EM("v", "tensor_scalar", vecs[:, 32:40], vecs[:, 32:40], 1.0, None,
                                   Alu.add)
    for c0 in (16, 40):
        S.wait("a", "v")
        EM("a", "activation", tA[:, 0:8], vecs[:, c0:c0 + 8], Act.Sign)
        EM("a", "activation", tA[:, 8:16], vecs[:, c0:c0 + 8], Act.Abs)
        S.wait("v", "a")
        EM("v", "tensor_scalar", tA[:, 8:16], tA[:, 8:16], 0.25, None, Alu.max)
        EM("v", "tensor_tensor", vecs[:, c0:c0 + 8], tA[:, 0:8],
                                              tA[:, 8:16], Alu.mult)
    S.wait("s", "v")
    S.wait("s", "a")
    d_gw = dma(tA[:, 0:64].rearrange("p (ft e) -> p ft e", e=8),
               gwt_d[:].rearrange("(ft p) e -> p ft e", p=128))
    d_gn = dma(tA[:, 64:72].rearrange("p (ft o) -> p ft o", o=1),
               gnwt_d[:].rearrange("(ft p) o -> p ft o", p=128), slow=True)
    S.wait("v", "s")
    for ft in range(8):
        EM("v", "tensor_scalar", gw2[:, ft * 8:ft * 8 + 8],
                                              tA[:, ft * 8:ft * 8 + 8],
                                              tA[:, 64 + ft:65 + ft], None, Alu.mult)
    for b in range(3):
        if b < 2:
            sxT, sxoff = sxa, b * 88
            sx2T, sx2off = sx2, b * 16
            cbcT, cbcoff = cbc, b * 32
        else:
            sxT, sxoff = sxo, 0
            sx2T, sx2off = sx2o, 0
            cbcT, cbcoff = cbco, 0
        for ft in range(8):
            if b < 2:
                av = tB[:, 40 + ft:41 + ft]
                EM("v", "tensor_scalar", av, mod3[:, 8 + ft, b:b + 1], 1.0, None,
                   Alu.add)
                bv = mod3[:, 0 + ft, b:b + 1]
            else:
                av = vecs[:, 8 + ft:9 + ft]
                bv = vecs[:, 0 + ft:1 + ft]
            EM("v", "tensor_scalar", 
                sxT[:, sxoff + ft * 11:sxoff + ft * 11 + 8],
                gw2[:, ft * 8:ft * 8 + 8], av, None, Alu.mult)
            EM("v", "memset", sxT[:, sxoff + ft * 11 + 8:sxoff + ft * 11 + 9], 1.0)
            EM("v", "tensor_tensor", 
                sxT[:, sxoff + ft * 11 + 9:sxoff + ft * 11 + 10], av, av, Alu.mult)
            EM("v", "tensor_tensor", 
                sxT[:, sxoff + ft * 11 + 10:sxoff + ft * 11 + 11], av, bv, Alu.mult)
            EM("v", "memset", sx2T[:, sx2off + ft * 2:sx2off + ft * 2 + 1], 1.0)
            EM("v", "tensor_copy", 
                sx2T[:, sx2off + ft * 2 + 1:sx2off + ft * 2 + 2],
                sxT[:, sxoff + ft * 11 + 9:sxoff + ft * 11 + 10])
        for ft in range(8):
            S.wait("v", "t")
            EM("v", "tensor_copy", tC[:, 0:1],
                sxT[:, sxoff + ft * 11 + 9:sxoff + ft * 11 + 10])
            EM("v", "tensor_copy", tC[:, 1:2],
                sxT[:, sxoff + ft * 11 + 10:sxoff + ft * 11 + 11])
            bv = (mod3[:, 0 + ft, b:b + 1] if b < 2 else vecs[:, 0 + ft:1 + ft])
            EM("v", "tensor_tensor", tA[:, 0:1], bv, bv, Alu.mult)
            EM("v", "tensor_copy", tC[:, 2:3], tA[:, 0:1])
            S.wait("t", "v")
            mm(ps_stat[0:3, 0:1], tC[:, 0:3], ones_m1b[:], start=(ft == 0),
               stop=(ft == 7))
        for ft in range(8):
            mm(ps_stat[0:8, 1:2], sxT[:, sxoff + ft * 11:sxoff + ft * 11 + 8],
               ones_m1b[:], start=(ft == 0), stop=(ft == 7))
        for ft in range(8):
            bv = (mod3[:, 0 + ft, b:b + 1] if b < 2 else vecs[:, 0 + ft:1 + ft])
            S.wait("v", "t")
            EM("v", "tensor_scalar",
                tC[:, 8:16], gw2[:, ft * 8:ft * 8 + 8], bv, None, Alu.mult)
            S.wait("t", "v")
            mm(ps_stat[0:8, 2:3], tC[:, 8:16], ones_m1b[:], start=(ft == 0),
               stop=(ft == 7))
        S.wait("a", "t")
        EM("a", "activation", rowst[0:3, 0:1], ps_stat[0:3, 0:1], Act.Identity)
        EM("a", "activation", rowst[0:8, 1:2], ps_stat[0:8, 1:2], Act.Identity)
        EM("a", "activation", rowst[0:8, 2:3], ps_stat[0:8, 2:3], Act.Identity)
        for col, nr, dst in ((0, 3, crow), (1, 8, agrow), (2, 8, c4row)):
            S.wait("t", "a")
            tr(ps_misc[0:1, 0:nr], rowst[0:nr, col:col + 1], id128[0:nr, 0:nr])
            S.wait("a", "t")
            EM("a", "activation", dst[0:1, 0:nr], ps_misc[0:1, 0:nr], Act.Identity)
        S.wait("v", "a")
        EM("v", "memset", cb_src[:], 0.0)
        EM("v", "tensor_copy", cb_src[0:1, 0:3], crow[0:1, 0:3])
        EM("v", "tensor_copy", cb_src[0:1, 8:16], agrow[0:1, :])
        EM("v", "tensor_copy", cb_src[0:1, 16:24], c4row[0:1, :])
        S.wait("t", "v")
        S.wait("t", "a")
        mm(ps_misc[:, 0:32], ones_k1b[0:1, :], cb_src[0:1, :])
        S.wait("a", "t")
        EM("a", "activation", cbcT[:, cbcoff:cbcoff + 32], ps_misc[:, 0:32],
           Act.Identity)

    ckpt("p1")
    flat_srcs = [(0, sgw_d.flatten()), (1, sdw_d.flatten())]
    for e in range(8):
        flat_srcs.append((2 + e, egw_d[e].flatten()))
    for e in range(8):
        flat_srcs.append((10 + e, edw_d[e].flatten()))
    flat_srcs.append((18, w1_d.flatten()))
    flat_srcs.append((19, w2_d.flatten()))

    CH = 128 * 2048
    meanitems = []
    for mi, flat in flat_srcs:
        for c in range(MEAN_N[mi] // CH):
            meanitems.append((mi, flat, c))

    meanbufs = [wst[0][:], wst[1][:]] + [
        acc[:, i * 2048:(i + 1) * 2048] for i in range(6)]

    def mean_chunk(tg):
        mi, flat, c = meanitems[tg]
        bi = tg % 8
        buf = meanbufs[bi]
        dd = dma(buf, flat[c * CH:(c + 1) * CH].rearrange("(p c) -> p c", p=128),
                 gbuf=f"mst{bi}", pool="bulk")
        if tg % 2 == 0:
            S.wait_tok("v", dd)
            EM("v", "reduce_sum", rows_tm[:, 376:377], buf, axis=AX.X,
                                                 apply_absolute_value=True)
            gset(f"mst{bi}", ("v", S.cnt["v"]))
            EM("v", "tensor_tensor",
                mwcol[:, mi:mi + 1], mwcol[:, mi:mi + 1], rows_tm[:, 376:377],
                Alu.add)
        else:
            S.wait_tok("a", dd)
            EM("a", "activation", wqb[0][:], buf, Act.Abs,
                                                 accum_out=rows_tm[:, 377:378])
            gset(f"mst{bi}", ("a", S.cnt["a"]))
            S.wait("v", "a")
            EM("v", "tensor_tensor",
                mwcol[:, mi:mi + 1], mwcol[:, mi:mi + 1], rows_tm[:, 377:378],
                Alu.add)

    def mean_final():
        S.wait("v", "a")
        EM("v", "tensor_copy", mwrow_b[:], mwcol[:])
        S.wait("t", "v")
        S.wait("t", "a")
        mm(ps_misc[0:32, 0:1], mwrow_b[:], ones_m1b[:])
        S.wait("a", "t")
        EM("a", "activation", mwc32[0:32, 0:1], ps_misc[0:32, 0:1], Act.Identity)
        S.wait("t", "a")
        tr(ps_misc[0:1, 0:32], mwc32[0:32, 0:1], id128[0:32, 0:32])
        S.wait("a", "t")
        EM("a", "activation", mwrow[0:1, :], ps_misc[0:1, 0:32], Act.Identity)
        S.wait("v", "a")
        EM("v", "tensor_tensor", mwrow[0:1, :], mwrow[0:1, :], cntinv[0:1, :],
                                       Alu.mult)
        EM("v", "tensor_scalar", mwrow[0:1, :], mwrow[0:1, :], 1e-5, None,
                                       Alu.max)
        EM("v", "tensor_copy", cb_src[0:1, 0:32], mwrow[0:1, :])
        S.wait("t", "v")
        S.wait("t", "a")
        mm(ps_misc[:, 0:32], ones_k1b[0:1, :], cb_src[0:1, :])
        S.wait("a", "t")
        EM("a", "activation", mwbc[:], ps_misc[:, 0:32], Act.Identity)
        S.wait("v", "a")
        EM("v", "reciprocal", rmwbc[:], mwbc[:])

    ckpt("p2")
    NCH = NT // 512 + T // 512
    mt = 0
    for ci in range(NCH):
        own = ci >= NT // 512
        ch = ci if not own else ci - NT // 512
        src_d = x_all_d if not own else x_own_d
        mgoal = (len(meanitems) * (ci + 1)) // NCH
        while mt < mgoal:
            mean_chunk(mt)
            mt += 1
        for s4 in range(4):
            g = ch * 4 + s4
            xb_ = g % 4
            xstg = [tA[:, 0:1024], tB[:, 0:1024],
                    acc[:, 12288:13312], acc[:, 13312:14336]][xb_]
            dd = dma(xstg, src_d[g * 128:(g + 1) * 128, :],
                     gbuf=f"xst{xb_}", pool="fast")
            S.wait_tok("v", dd)
            S.wait("v", "t")
            EM("v", "tensor_copy", paxb[:], xstg)
            EM("v", "tensor_tensor", tC[:, 0:1024], xstg, paxb[:],
               Alu.subtract)
            gset(f"xst{xb_}", ("v", S.cnt["v"]))
            pmb = ps_misc[:].bitcast(BF16)
            for src, dst in ((paxb, pafa), (tC, paflo)):
                S.wait("t", "v")
                S.wait("t", "a")
                for ft in range(8):
                    tr(pmb[:, ft * 128:(ft + 1) * 128],
                       src[:, ft * 128:(ft + 1) * 128], idbf[:])
                S.wait("a", "t")
                EM("a", "activation",
                    dst[:].rearrange("p (ft s c) -> p ft s c", s=4,
                                     c=128)[:, :, s4, :],
                    pmb[:, 0:1024].rearrange("p (ft c) -> p ft c", c=128),
                    Act.Identity)
        if own:
            sxU, sx2U = sxo, sx2o
            soff, s2off = 0, 0
        else:
            bx = ch // 16
            sxU, sx2U = sxa, sx2
            soff, s2off = bx * 88, bx * 16
        S.wait("v", "a")
        S.wait("t", "a")
        for ft in range(8):
            mm(ps_stat[0:11, 0:512], sxU[:, soff + ft * 11:soff + (ft + 1) * 11],
               pafa[:, ft * 512:(ft + 1) * 512], start=(ft == 0), stop=False)
        for ft in range(8):
            mm(ps_stat[0:11, 0:512], sxU[:, soff + ft * 11:soff + (ft + 1) * 11],
               paflo[:, ft * 512:(ft + 1) * 512], start=False, stop=(ft == 7))
        for ft in range(8):
            EM("v", "tensor_tensor", 
                paf2[:], pafa[:, ft * 512:(ft + 1) * 512],
                pafa[:, ft * 512:(ft + 1) * 512], Alu.mult)
            S.wait("t", "v")
            mm(ps_misc[0:2, 0:512], sx2U[:, s2off + ft * 2:s2off + (ft + 1) * 2],
               paf2[:], start=(ft == 0), stop=(ft == 7))
            if ft < 7:
                S.wait("v", "t")
        S.wait("a", "t")
        EM("a", "activation", past[0:11, :], ps_stat[0:11, 0:512], Act.Identity)
        EM("a", "activation", rowst[0:2, 0:512], ps_misc[0:2, 0:512], Act.Identity)
        stT = (statsA if not own else statsB)[:].rearrange("p (g q) -> p g q", q=13)
        for s4 in range(4):
            g = ch * 4 + s4
            S.wait("t", "a")
            tr(ps_misc[0:128, 0:11], past[0:11, s4 * 128:(s4 + 1) * 128],
               id128[0:11, 0:11])
            S.wait("a", "t")
            EM("a", "activation", stT[:, g, 0:11], ps_misc[0:128, 0:11],
               Act.Identity)
            S.wait("t", "a")
            tr(ps_misc[0:128, 0:2], rowst[0:2, s4 * 128:(s4 + 1) * 128],
               id128[0:2, 0:2])
            S.wait("a", "t")
            EM("a", "activation", stT[:, g, 11:13], ps_misc[0:128, 0:2],
               Act.Identity)
    mean_final()
    ckpt("pAstats")
    stA = statsA[:].rearrange("p (g q) -> p g q", q=13)
    stO = statsB[:].rearrange("p (g q) -> p g q", q=13)
    pA3 = probsA[:].rearrange("p (g e) -> p g e", e=E)
    pO3 = wselB[:].rearrange("p (g e) -> p g e", e=E)
    S.wait("v", "a")
    for b in range(3):
        if b < 2:
            stB = stA[:, b * 64:(b + 1) * 64, :]
            out3 = pA3[:, b * 64:(b + 1) * 64, :]
            cb_ap = cbc
            co = b * 32
            ngz = 64
        else:
            stB = stO[:, 0:16, :]
            out3 = pO3
            cb_ap = cbco
            co = 0
            ngz = 16
        tmAs = tmA[:, 0:ngz]; tmBs = tmB[:, 0:ngz]; tmCs = tmC[:, 0:ngz]
        tmDs = tmD[:, 0:ngz]; tmEs = tmE[:, 0:ngz]
        EM("v", "tensor_scalar", tmAs, stB[:, :, 8], 1.0 / D, None, Alu.mult)
        EM("v", "tensor_tensor", tmBs, tmAs, tmAs, Alu.mult)
        EM("v", "scalar_tensor_tensor", tmCs, stB[:, :, 11], 1.0 / D, tmBs,
            Alu.mult, Alu.subtract)
        EM("v", "tensor_scalar", tmCs, tmCs, 1e-5, None, Alu.add)
        EM("v", "reciprocal", tmCs, tmCs)
        S.wait("a", "v")
        EM("a", "activation", tmCs, tmCs, Act.Sqrt)
        S.wait("v", "a")
        EM("v", "tensor_tensor", tmDs, tmAs, stB[:, :, 9], Alu.mult)
        EM("v", "scalar_tensor_tensor", tmDs, tmDs, -2.0, stB[:, :, 12],
            Alu.mult, Alu.add)
        EM("v", "tensor_scalar", tmEs, tmBs, cb_ap[:, co + 0:co + 1], None, Alu.mult)
        EM("v", "tensor_tensor", tmDs, tmDs, tmEs, Alu.add)
        EM("v", "tensor_tensor", tmEs, tmCs, tmCs, Alu.mult)
        EM("v", "tensor_tensor", tmDs, tmDs, tmEs, Alu.mult)
        EM("v", "tensor_scalar", tmEs, tmAs, cb_ap[:, co + 1:co + 2], None, Alu.mult)
        EM("v", "tensor_tensor", tmEs, tmEs, stB[:, :, 10], Alu.subtract)
        EM("v", "tensor_tensor", tmEs, tmEs, tmCs, Alu.mult)
        EM("v", "scalar_tensor_tensor", tmDs, tmEs, -2.0, tmDs, Alu.mult, Alu.add)
        EM("v", "tensor_scalar", tmDs, tmDs, cb_ap[:, co + 2:co + 3], None, Alu.add)
        EM("v", "tensor_scalar", tmDs, tmDs, 1.0 / D, 1e-6, Alu.mult, Alu.add)
        EM("v", "reciprocal", tmDs, tmDs)
        S.wait("a", "v")
        EM("a", "activation", tmDs, tmDs, Act.Sqrt)
        S.wait("v", "a")
        for e in range(E):
            EM("v", "tensor_scalar", tmEs, tmAs, cb_ap[:, co + 8 + e:co + 9 + e],
                None, Alu.mult)
            EM("v", "tensor_tensor", tmEs, stB[:, :, e], tmEs, Alu.subtract)
            EM("v", "tensor_tensor", tmEs, tmEs, tmCs, Alu.mult)
            EM("v", "tensor_scalar", tmEs, tmEs, cb_ap[:, co + 16 + e:co + 17 + e],
                None, Alu.add)
            EM("v", "tensor_tensor", out3[:, :, e], tmEs, tmDs, Alu.mult)
    ckpt("p3logits")
    def soft_top2(p3, ng_, sel_dst):
        tm1 = tmA[:, 0:ng_]
        tm2 = tmB[:, 0:ng_]
        EM("v", "reduce_max", tm1, p3, axis=AX.X)
        EM("v", "tensor_tensor", p3, p3, tm1.unsqueeze(2).broadcast_to(
            [128, ng_, E]), Alu.subtract)
        S.wait("a", "v")
        EM("a", "activation", p3, p3, Act.Exp)
        S.wait("v", "a")
        EM("v", "reduce_sum", tm1, p3, axis=AX.X)
        EM("v", "reciprocal", tm1, tm1)
        EM("v", "tensor_tensor", p3, p3, tm1.unsqueeze(2).broadcast_to(
            [128, ng_, E]), Alu.mult)
        EM("v", "reduce_max", tm1, p3, axis=AX.X)
        EM("v", "tensor_tensor", sel_dst, p3, tm1.unsqueeze(2).broadcast_to(
            [128, ng_, E]), Alu.is_equal)
        EM("v", "scalar_tensor_tensor", sel_dst, sel_dst, -1e30, p3, Alu.mult,
            Alu.add)
        EM("v", "reduce_max", tm2, sel_dst, axis=AX.X)
        EM("v", "tensor_tensor", sel_dst, p3, tm2.unsqueeze(2).broadcast_to(
            [128, ng_, E]), Alu.is_ge)
    scrA = statsA[:, 0:NGA * E].rearrange("p (g e) -> p g e", e=E)
    soft_top2(pA3, NGA, scrA)
    EM("v", "tensor_copy", selAb[:], statsA[:, 0:NGA * E])
    sB3o = selB[:].rearrange("p (g e) -> p g e", e=E)
    soft_top2(pO3, NG, sB3o)
    S.wait("t", "v")
    S.wait("t", "a")
    mm(ps_misc[0:1, 0:512], ones_m1b[:], selAb[:, 0:512])
    mm(ps_stat[0:1, 0:512], ones_m1b[:], selAb[:, 512:1024])
    S.wait("a", "t")
    EM("a", "activation", cntrow[0:1, 0:512], ps_misc[0:1, 0:512], Act.Identity)
    EM("a", "activation", cntrow[0:1, 512:1024], ps_stat[0:1, 0:512], Act.Identity)
    S.wait("v", "a")
    EM("v", "tensor_tensor", 
        cntrow[0:1, :].rearrange("o (g e) -> o g e", e=E),
        cntrow[0:1, :].rearrange("o (g e) -> o g e", e=E),
        maskb[0:1, :].unsqueeze(2).broadcast_to([1, NGA, E]), Alu.mult)
    EM("v", "reduce_sum", offrow[0:1, :],
        cntrow[0:1, :].rearrange("o (g e) -> o e g", e=E), axis=AX.X)

    ckpt("p3")
    S.barrier()
    for half in range(2):
        g0 = half * 8
        t0 = half * HALF
        for s8 in range(8):
            dd = dma(tA[:, 0:1024], x_own_d[t0 + s8 * 128: t0 + (s8 + 1) * 128, :],
                     gbuf="tAx")
            S.wait("v", "s")
            S.wait("v", "t")
            EM("v", "tensor_copy", paxb[:], tA[:, 0:1024])
            gset("tAx", ("v", S.cnt["v"]))
            S.wait("t", "v")
            S.wait("t", "a")
            pmb = ps_misc[:].bitcast(BF16)
            for ft in range(8):
                tr(pmb[:, ft * 128:(ft + 1) * 128],
                   paxb[:, ft * 128:(ft + 1) * 128], idbf[:])
            S.wait("a", "t")
            EM("a", "activation",
                xbh[:].rearrange("p (ft s c) -> p ft s c", s=8,
                                 c=128)[:, :, s8, :],
                pmb[:, 0:1024].rearrange("p (ft c) -> p ft c", c=128),
                Act.Identity)
        S.wait("v", "a")
        S.wait("v", "t")
        EM("a", "activation", h1h[:], xbh[:], Act.Square)
        for c in range(2):
            S.wait("t", "v")
            S.wait("t", "a")
            for ft in range(8):
                mm(ps_stat[0:1, 0:512], ones_m1b[:],
                   xbh[:, ft * HALF + c * 512: ft * HALF + (c + 1) * 512],
                   start=(ft == 0), stop=(ft == 7))
            for ft in range(8):
                mm(ps_misc[0:1, 0:512], ones_m1b[:],
                   h1h[:, ft * HALF + c * 512: ft * HALF + (c + 1) * 512],
                   start=(ft == 0), stop=(ft == 7))
            two_rows_to_tm(0, g0 + c * 4, 1, g0 + c * 4)
        S.wait("v", "a")
        EM("v", "tensor_scalar", tmcol(2, g0, 8), tmcol(0, g0, 8), 1.0 / D,
                                       None, Alu.mult)
        EM("v", "tensor_tensor", tmcol(3, g0, 8), tmcol(2, g0, 8),
                                       tmcol(2, g0, 8), Alu.mult)
        EM("v", "scalar_tensor_tensor", tmcol(1, g0, 8), tmcol(1, g0, 8),
                                              1.0 / D, tmcol(3, g0, 8), Alu.mult,
                                              Alu.subtract)
        EM("v", "tensor_scalar", tmcol(1, g0, 8), tmcol(1, g0, 8), 1e-5,
                                       None, Alu.add)
        EM("v", "reciprocal", tmcol(1, g0, 8), tmcol(1, g0, 8))
        S.wait("a", "v")
        EM("a", "activation", tmcol(1, g0, 8), tmcol(1, g0, 8), Act.Sqrt)
        S.wait("v", "a")
        EM("v", "tensor_tensor", tmcol(3, g0, 8), tmcol(2, g0, 8),
                                       tmcol(1, g0, 8), Alu.mult)
        ckpt(f"p4stats{half}")
        build_bcast(tmcol(1, g0, 8), Bu, 8)
        build_bcast(tmcol(3, g0, 8), Bv, 8)
        S.wait("v", "t")
        for ft in range(8):
            av = vecs[:, 8 + ft:9 + ft]
            bv = vecs[:, 0 + ft:1 + ft]
            EM("v", "tensor_tensor", 
                tC[:, 0:HALF], xbh[:, ft * HALF:(ft + 1) * HALF], Bu[:], Alu.mult)
            EM("v", "tensor_tensor", tC[:, 0:HALF], tC[:, 0:HALF], Bv[:],
                                           Alu.subtract)
            EM("v", "tensor_scalar", 
                h1h[:, ft * HALF:(ft + 1) * HALF], tC[:, 0:HALF], av, bv,
                Alu.mult, Alu.add)
        S.wait("a", "t")
        S.wait("a", "v")
        EM("a", "activation", xbh[:], h1h[:], Act.Square)
        for c in range(2):
            S.wait("t", "v")
            S.wait("t", "a")
            for ft in range(8):
                mm(ps_stat[0:1, 0:512], ones_m1b[:],
                   xbh[:, ft * HALF + c * 512: ft * HALF + (c + 1) * 512],
                   start=(ft == 0), stop=(ft == 7))
            S.wait("a", "t")
            EM("a", "activation", rowst[0:1, :], ps_stat[0:1, 0:512],
                                        Act.Identity)
            rows_to_tm(1, [(4, 0)], g0 + c * 4)
        S.wait("v", "a")
        maxtree_to_tm(xbh, HALF, 8, 5, g0, 8)
        EM("v", "tensor_scalar", tmcol(6, g0, 8), tmcol(4, g0, 8), 1.0 / D,
                                       1e-8, Alu.mult, Alu.add)
        EM("v", "reciprocal", tmcol(6, g0, 8), tmcol(6, g0, 8))
        S.wait("a", "v")
        EM("a", "activation", tmcol(6, g0, 8), tmcol(6, g0, 8), Act.Sqrt)
        EM("a", "activation", tmcol(5, g0, 8), tmcol(5, g0, 8), Act.Sqrt)
        S.wait("v", "a")
        EM("v", "tensor_tensor", tmcol(5, g0, 8), tmcol(5, g0, 8),
                                       tmcol(6, g0, 8), Alu.mult)
        EM("v", "tensor_scalar", tmcol(5, g0, 8), tmcol(5, g0, 8), 1e-5,
                                       None, Alu.max)
        EM("v", "reciprocal", tmcol(7, g0, 8), tmcol(5, g0, 8))
        EM("v", "tensor_scalar", tmcol(7, g0, 8), tmcol(7, g0, 8), 127.0,
                                       None, Alu.mult)
        EM("v", "tensor_tensor", tmcol(6, g0, 8), tmcol(6, g0, 8),
                                       tmcol(7, g0, 8), Alu.mult)
        EM("v", "reciprocal", tmcol(7, g0, 8), tmcol(7, g0, 8))
        build_bcast(tmcol(6, g0, 8), Bw, 8)
        for ft in range(8):
            EM("v", "tensor_tensor", 
                tA[:, 0:HALF], h1h[:, ft * HALF:(ft + 1) * HALF], Bw[:], Alu.mult)
            rnd(xq1[:, ft * T + t0: ft * T + t0 + HALF], tA[:, 0:HALF])
        gadd("tAx", ("v", S.cnt["v"]))
    ckpt("p4logits")
    S.wait("v", "a")
    EM("v", "tensor_tensor", wselB[:], wselB[:], selB[:], Alu.mult)
    EM("v", "tensor_copy", selBb[:], selB[:])
    S.wait("t", "v")
    S.wait("t", "a")
    mm(ps_misc[0:1, 0:128], ones_m1b[:], selBb[:])
    S.wait("a", "t")
    EM("a", "activation", offBrow[0:1, :], ps_misc[0:1, 0:128], Act.Identity)
    S.wait("v", "a")
    oB3 = offBrow[0:1, :].rearrange("o (g e) -> o e g", e=E)
    scanrow = trow[0:1, 256:256 + NG * E]
    cB3 = scanrow.rearrange("o (g e) -> o e g", e=E)
    for e in range(E):
        EM("v", "tensor_tensor_scan", cB3[:, e, :], oB3[:, e, :],
                                                 oB3[:, e, :], 0.0, Alu.add,
                                                 Alu.bypass)
    EM("v", "tensor_tensor", scanrow, scanrow, offBrow[0:1, :],
                                   Alu.subtract)
    EM("v", "tensor_tensor", 
        scanrow.rearrange("o (g e) -> o g e", e=E),
        scanrow.rearrange("o (g e) -> o g e", e=E),
        offrow[0:1, :].unsqueeze(1).broadcast_to([1, NG, E]), Alu.add)
    EM("v", "tensor_copy", offBb[0:1, :], scanrow)
    S.wait("t", "v")
    S.wait("t", "a")
    for g in range(NG):
        mm(ps_misc[0:128, g * 8:(g + 1) * 8], tribf[:], selBb[:, g * 8:(g + 1) * 8],
           start=(g == 0), stop=False)
    mm(ps_misc[0:128, 0:NG * E], ones_k1b[0:1, :], offBb[0:1, :], start=False,
       stop=True)
    S.wait("v", "t")
    EM("v", "tensor_scalar", cumsb[:], ps_misc[0:128, 0:NG * E], CAP, None,
                                   Alu.is_le)
    EM("v", "tensor_tensor", wkeep[:], cumsb[:], wselB[:], Alu.mult)
    EM("v", "tensor_copy", wkeepb[:], wkeep[:])
    for c in range(4):
        S.wait("t", "v")
        S.wait("t", "a")
        for s4 in range(4):
            g = c * 4 + s4
            tr(ps_stat[0:8, 0:512].bitcast(BF16)[:, s4 * 128:(s4 + 1) * 128],
               wkeepb[:, g * 8:(g + 1) * 8], idbf[:])
        S.wait("a", "t")
        EM("a", "activation", 
            wkT[:, c * 512:(c + 1) * 512],
            ps_stat[0:8, 0:512].bitcast(BF16)[:, 0:512], Act.Identity)
    build_bcast(tmcol(7, 0, 8), Bu, 8)
    build_bcast(tmcol(7, 8, 8), Bv, 8)
    S.wait("v", "a")
    EM("v", "tensor_copy", Bs1F[:, 0:HALF], Bu[:])
    EM("v", "tensor_copy", Bs1F[:, HALF:T], Bv[:])

    ckpt("p4")
    S.barrier()
    for e in range(9):
        gsrc = sgw_d if e == 0 else egw_d[e - 1]
        dsrc = sdw_d if e == 0 else edw_d[e - 1]
        mi_g = 0 if e == 0 else 2 + (e - 1)
        mi_d = 1 if e == 0 else 10 + (e - 1)
        for half in range(2):
            g0 = half * 8
            t0 = half * HALF
            def b1_load_quant(j):
                bi = j % 2
                if half == 0:
                    d1 = dma(wst[bi][:, 0:1024], gsrc[j], gbuf=f"wst{bi}")
                    d2 = dma(wst[bi][:, 1024:2048], gsrc[j + 16])
                    S.wait_tok("a", d1)
                    S.wait_tok("a", d2)
                    for tok in guard.get(f"wqs{bi}", []):
                        S.wait_tok("v", tok)
                    for tok in guard.get(f"wqb{bi}", []):
                        S.wait_tok("v", tok)
                    quant_strip(wst[bi], wqb[bi], mi_g, 2048)
                    gset(f"wst{bi}", ("v", S.cnt["v"]))
                    qd = S.cnt["v"]
                    S.wait("a", "v", qd)
                    gset(f"wqs{bi}", dma_w(scr_g_d[e, j], wqb[bi][:]))
                    return ("v", qd)
                d1 = dma(wqb[bi][:, 0:1024], scr_g_d[e, j][:, 0:1024],
                         gbuf=f"wqb{bi}")
                d2 = dma(wqb[bi][:, 1024:2048], scr_g_d[e, j][:, 1024:2048])
                return [d1, d2]

            def b1_mms(j, qtok):
                bi = j % 2
                r0 = (2 * j) % 3
                r1 = (2 * j + 1) % 3
                for tk in (qtok if isinstance(qtok, list) else [qtok]):
                    S.wait_tok("t", tk)
                for r in (r0, r1):
                    for tok in guard.get(f"psa{r}", []):
                        S.wait_tok("t", tok)
                    for tok in guard.get(f"psv{r}", []):
                        S.wait_tok("t", tok)
                for k in range(8):
                    for c in range(2):
                        mm(psR[r0][:, c * 512:(c + 1) * 512],
                           wqb[bi][:, k * 128:(k + 1) * 128],
                           xq1[:, k * T + t0 + c * 512: k * T + t0 + (c + 1) * 512],
                           start=(k == 0), stop=(k == 7))
                m0 = ("t", S.cnt["t"])
                for k in range(8):
                    for c in range(2):
                        mm(psR[r1][:, c * 512:(c + 1) * 512],
                           wqb[bi][:, 1024 + k * 128:1024 + (k + 1) * 128],
                           xq1[:, k * T + t0 + c * 512: k * T + t0 + (c + 1) * 512],
                           start=(k == 0), stop=(k == 7))
                gset(f"wqb{bi}", ("t", S.cnt["t"]))
                return (m0, ("t", S.cnt["t"]))

            def b1_evac(j, mtoks):
                r0 = (2 * j) % 3
                r1 = (2 * j + 1) % 3
                S.wait_tok("v", mtoks[0])
                EM("v", "scalar_tensor_tensor",
                    tA[:, 0:HALF], psR[r0][:], mwbc[:, mi_g:mi_g + 1],
                    Bs1F[:, t0:t0 + HALF], Alu.mult, Alu.mult)
                gset(f"psv{r0}", ("v", S.cnt["v"]))
                S.wait("a", "v")
                EM("a", "activation", tC[:, 0:HALF], tA[:, 0:HALF], Act.Silu)
                S.wait_tok("a", mtoks[1])
                EM("a", "activation", tB[:, 0:HALF], psR[r1][:], Act.Identity)
                gset(f"psa{r1}", ("a", S.cnt["a"]))
                S.wait("v", "a")
                EM("v", "tensor_tensor", zt[:, j * HALF:(j + 1) * HALF],
                                                    tC[:, 0:HALF], tB[:, 0:HALF],
                                                    Alu.mult)

            pend = None
            for j in range(17):
                if j < 16:
                    qtok = b1_load_quant(j)
                if pend is not None:
                    b1_evac(*pend)
                if j < 16:
                    pend = (j, b1_mms(j, qtok))
                else:
                    pend = None
            if e == 0 and half == 0:
                ckpt("p5b1")
            for j in range(8):
                S.wait("a", "t")
                S.wait("a", "v")
                EM("a", "activation",
                    tC[:, 0:HALF], zt[:, 2 * j * HALF:(2 * j + 1) * HALF],
                    Act.Square)
                EM("a", "activation",
                    tD[:, 0:HALF], zt[:, (2 * j + 1) * HALF:(2 * j + 2) * HALF],
                    Act.Square)
                S.wait("t", "a")
                mm(ps_stat[0:1, 0:512], ones_m1b[:], tC[:, 0:512],
                   start=(j == 0), stop=False)
                mm(ps_misc[0:1, 0:512], ones_m1b[:], tC[:, 512:1024],
                   start=(j == 0), stop=False)
                mm(ps_stat[0:1, 0:512], ones_m1b[:], tD[:, 0:512],
                   start=False, stop=(j == 7))
                mm(ps_misc[0:1, 0:512], ones_m1b[:], tD[:, 512:1024],
                   start=False, stop=(j == 7))
                S.wait("v", "a")
                S.wait("v", "t")
                EM("v", "tensor_tensor", tD[:, 0:HALF], tC[:, 0:HALF],
                                               tD[:, 0:HALF], Alu.max)
                if j == 0:
                    EM("v", "tensor_copy", tE_[:, 0:HALF], tD[:, 0:HALF])
                else:
                    EM("v", "tensor_tensor", tE_[:, 0:HALF], tE_[:, 0:HALF],
                                                   tD[:, 0:HALF], Alu.max)
            two_rows_to_tm(10, g0, 10, g0 + 4)
            S.wait("v", "a")
            maxtree_to_tm(tE_, HALF, 1, 11, g0, 8)
            EM("v", "tensor_scalar", tmcol(9, g0, 8), tmcol(7, g0, 8),
                                           mwbc[:, mi_g:mi_g + 1], None, Alu.mult)
            EM("v", "tensor_scalar", tmcol(12, g0, 8), tmcol(10, g0, 8),
                                           1.0 / IM, None, Alu.mult)
            EM("v", "tensor_tensor", tmcol(12, g0, 8), tmcol(12, g0, 8),
                                           tmcol(9, g0, 8), Alu.mult)
            EM("v", "tensor_tensor", tmcol(12, g0, 8), tmcol(12, g0, 8),
                                           tmcol(9, g0, 8), Alu.mult)
            EM("v", "tensor_scalar", tmcol(12, g0, 8), tmcol(12, g0, 8),
                                           1e-8, None, Alu.add)
            EM("v", "reciprocal", tmcol(12, g0, 8), tmcol(12, g0, 8))
            S.wait("a", "v")
            EM("a", "activation", tmcol(12, g0, 8), tmcol(12, g0, 8),
                                        Act.Sqrt)
            EM("a", "activation", tmcol(11, g0, 8), tmcol(11, g0, 8),
                                        Act.Sqrt)
            S.wait("v", "a")
            EM("v", "tensor_tensor", tmcol(11, g0, 8), tmcol(11, g0, 8),
                                           tmcol(9, g0, 8), Alu.mult)
            EM("v", "tensor_tensor", tmcol(11, g0, 8), tmcol(11, g0, 8),
                                           tmcol(12, g0, 8), Alu.mult)
            EM("v", "tensor_scalar", tmcol(11, g0, 8), tmcol(11, g0, 8),
                                           1e-5, None, Alu.max)
            EM("v", "reciprocal", tmcol(13, g0, 8), tmcol(11, g0, 8))
            EM("v", "tensor_scalar", tmcol(13, g0, 8), tmcol(13, g0, 8),
                                           127.0, None, Alu.mult)
            EM("v", "tensor_tensor", tmcol(12, g0, 8), tmcol(12, g0, 8),
                                           tmcol(13, g0, 8), Alu.mult)
            EM("v", "reciprocal", tmcol(13, g0, 8), tmcol(13, g0, 8))
            EM("v", "tensor_tensor", tmcol(12, g0, 8), tmcol(12, g0, 8),
                                           tmcol(9, g0, 8), Alu.mult)
            build_bcast(tmcol(12, g0, 8), Bw, 8)
            build_bcast(tmcol(13, g0, 8), Bx, 8)
            if e == 0:
                EM("v", "tensor_copy", tF[:, 0:HALF], Bx[:])
            else:
                S.wait("g", "t")
                EM("g", "memset", e8sel[:], 1.0)
                S.wait("g", "g")
                EM("g", "affine_select",
                    e8sel[:], e8sel[:], [[0, 128]], Alu.is_equal, 0.0,
                    base=-(e - 1), channel_multiplier=1)
                S.wait("t", "g")
                for c in range(2):
                    S.wait("t", "v")
                    S.wait("t", "a")
                    mm(ps_misc[0:128, 0:512], e8sel[:],
                       wkT[:, t0 + c * 512: t0 + (c + 1) * 512])
                    S.wait("v", "t")
                    EM("v", "tensor_tensor",
                        tF[:, c * 512:(c + 1) * 512], ps_misc[0:128, 0:512],
                        Bx[:, c * 512:(c + 1) * 512], Alu.mult)
            S.wait("v", "t")
            for j in range(16):
                EM("v", "tensor_tensor", tA[:, 0:HALF],
                                                    zt[:, j * HALF:(j + 1) * HALF],
                                                    Bw[:], Alu.mult)
                rnd(zt[:, j * HALF:(j + 1) * HALF], tA[:, 0:HALF])
            zq_tok = ("v", S.cnt["v"])
            def b2_load_quant(m):
                bi = m % 2
                if half == 0:
                    d1 = dma(wst[bi][:], dsrc[m], gbuf=f"wst{bi}")
                    S.wait_tok("a", d1)
                    for tok in guard.get(f"wqs{bi}", []):
                        S.wait_tok("v", tok)
                    for tok in guard.get(f"wqb{bi}", []):
                        S.wait_tok("v", tok)
                    quant_strip(wst[bi], wqb[bi], mi_d, 2048)
                    gset(f"wst{bi}", ("v", S.cnt["v"]))
                    qd = S.cnt["v"]
                    S.wait("a", "v", qd)
                    gset(f"wqs{bi}", dma_w(scr_d_d[e, m], wqb[bi][:]))
                    return ("v", qd)
                d1 = dma(wqb[bi][:, 0:1024], scr_d_d[e, m][:, 0:1024],
                         gbuf=f"wqb{bi}")
                d2 = dma(wqb[bi][:, 1024:2048], scr_d_d[e, m][:, 1024:2048])
                return [d1, d2]

            def b2_mms(m, qtok):
                bi = m % 2
                r0 = m % 3
                for tk in (qtok if isinstance(qtok, list) else [qtok]):
                    S.wait_tok("t", tk)
                S.wait_tok("t", zq_tok)
                for tok in guard.get(f"psa{r0}", []):
                    S.wait_tok("t", tok)
                for tok in guard.get(f"psv{r0}", []):
                    S.wait_tok("t", tok)
                for k in range(16):
                    for c in range(2):
                        mm(psR[r0][:, c * 512:(c + 1) * 512],
                           wqb[bi][:, k * 128:(k + 1) * 128],
                           zt[:, k * HALF + c * 512: k * HALF + (c + 1) * 512],
                           start=(k == 0), stop=(k == 15))
                gset(f"wqb{bi}", ("t", S.cnt["t"]))
                return ("t", S.cnt["t"])

            def b2_evac(m, mtok):
                r0 = m % 3
                S.wait_tok("v", mtok)
                EM("v", "scalar_tensor_tensor",
                    tA[:, 0:HALF], psR[r0][:], mwbc[:, mi_d:mi_d + 1],
                    tF[:, 0:HALF], Alu.mult, Alu.mult)
                gset(f"psv{r0}", ("v", S.cnt["v"]))
                if e == 0:
                    EM("v", "tensor_copy",
                        acc[:, m * T + t0: m * T + t0 + HALF],
                        tA[:, 0:HALF])
                else:
                    EM("v", "tensor_tensor",
                        acc[:, m * T + t0: m * T + t0 + HALF],
                        acc[:, m * T + t0: m * T + t0 + HALF],
                        tA[:, 0:HALF], Alu.add)

            pend = None
            for m in range(9):
                if m < 8:
                    qtok = b2_load_quant(m)
                if pend is not None:
                    b2_evac(*pend)
                if m < 8:
                    pend = (m, b2_mms(m, qtok))
                else:
                    pend = None
            if half == 1:
                ckpt(f"p5e{e}")

    ckpt("p5")
    S.barrier()
    for half in range(2):
        t0 = half * HALF
        for s8 in range(8):
            dd = dma(tB[:, 0:1024], x_own_d[t0 + s8 * 128: t0 + (s8 + 1) * 128, :],
                     gbuf="tBx")
            S.wait("t", "s")
            S.wait("t", "a")
            for ft in range(4):
                tr(ps_misc[:, ft * 128:(ft + 1) * 128],
                   tB[:, ft * 128:(ft + 1) * 128], id128[:])
            S.wait("a", "t")
            EM("a", "activation", tA[:, 0:512], ps_misc[:, 0:512],
                                        Act.Identity)
            S.wait("t", "a")
            for ft in range(4):
                tr(ps_misc[:, ft * 128:(ft + 1) * 128],
                   tB[:, 512 + ft * 128: 512 + (ft + 1) * 128], id128[:])
            gset("tBx", ("t", S.cnt["t"]))
            S.wait("a", "t")
            EM("a", "activation", tA[:, 512:1024], ps_misc[:, 0:512],
                                        Act.Identity)
            S.wait("v", "a")
            for ft in range(8):
                av = vecs[:, 16 + ft:17 + ft]
                EM("v", "scalar_tensor_tensor", 
                    acc[:, ft * T + t0 + s8 * 128: ft * T + t0 + s8 * 128 + 128],
                    acc[:, ft * T + t0 + s8 * 128: ft * T + t0 + s8 * 128 + 128],
                    av, tA[:, ft * 128:(ft + 1) * 128], Alu.mult, Alu.add)

    ckpt("p6")
    for half in range(2):
        g0 = half * 8
        t0 = half * HALF
        S.wait("v", "t")
        for ft in range(8):
            EM("v", "tensor_copy", 
                xbh2[:, ft * HALF:(ft + 1) * HALF],
                acc[:, ft * T + t0: ft * T + t0 + HALF])
        S.wait("a", "v")
        S.wait("a", "t")
        EM("a", "activation", h1h2[:], xbh2[:], Act.Square)
        for c in range(2):
            S.wait("t", "v")
            S.wait("t", "a")
            for ft in range(8):
                mm(ps_stat[0:1, 0:512], ones_m1b[:],
                   xbh2[:, ft * HALF + c * 512: ft * HALF + (c + 1) * 512],
                   start=(ft == 0), stop=(ft == 7))
            for ft in range(8):
                mm(ps_misc[0:1, 0:512], ones_m1b[:],
                   h1h2[:, ft * HALF + c * 512: ft * HALF + (c + 1) * 512],
                   start=(ft == 0), stop=(ft == 7))
            two_rows_to_tm(14, g0 + c * 4, 15, g0 + c * 4)
        S.wait("v", "a")
        EM("v", "tensor_scalar", tmcol(16, g0, 8), tmcol(14, g0, 8), 1.0 / D,
                                       None, Alu.mult)
        EM("v", "tensor_tensor", tmcol(17, g0, 8), tmcol(16, g0, 8),
                                       tmcol(16, g0, 8), Alu.mult)
        EM("v", "scalar_tensor_tensor", tmcol(15, g0, 8), tmcol(15, g0, 8),
                                              1.0 / D, tmcol(17, g0, 8), Alu.mult,
                                              Alu.subtract)
        EM("v", "tensor_scalar", tmcol(15, g0, 8), tmcol(15, g0, 8), 1e-5,
                                       None, Alu.add)
        EM("v", "reciprocal", tmcol(15, g0, 8), tmcol(15, g0, 8))
        S.wait("a", "v")
        EM("a", "activation", tmcol(15, g0, 8), tmcol(15, g0, 8), Act.Sqrt)
        S.wait("v", "a")
        EM("v", "tensor_tensor", tmcol(16, g0, 8), tmcol(16, g0, 8),
                                       tmcol(15, g0, 8), Alu.mult)
        build_bcast(tmcol(15, g0, 8), Bu, 8)
        build_bcast(tmcol(16, g0, 8), Bv, 8)
        S.wait("v", "t")
        for ft in range(8):
            av = vecs[:, 32 + ft:33 + ft]
            bv = vecs[:, 24 + ft:25 + ft]
            EM("v", "tensor_tensor", 
                tC[:, 0:HALF], xbh2[:, ft * HALF:(ft + 1) * HALF], Bu[:], Alu.mult)
            EM("v", "tensor_tensor", tC[:, 0:HALF], tC[:, 0:HALF], Bv[:],
                                           Alu.subtract)
            EM("v", "tensor_scalar", 
                h1h2[:, ft * HALF:(ft + 1) * HALF], tC[:, 0:HALF], av, bv,
                Alu.mult, Alu.add)
        S.wait("a", "v")
        S.wait("a", "t")
        EM("a", "activation", xbh2[:], h1h2[:], Act.Square)
        S.wait("v", "a")
        maxtree_to_tm(xbh2, HALF, 8, 17, g0, 8)
        S.wait("a", "v")
        EM("a", "activation", tmcol(17, g0, 8), tmcol(17, g0, 8), Act.Sqrt)
        S.wait("v", "a")
        EM("v", "tensor_scalar", tmcol(17, g0, 8), tmcol(17, g0, 8), 1e-5,
                                       None, Alu.max)
        EM("v", "reciprocal", tmcol(18, g0, 8), tmcol(17, g0, 8))
        EM("v", "tensor_scalar", tmcol(18, g0, 8), tmcol(18, g0, 8), 127.0,
                                       None, Alu.mult)
        EM("v", "reciprocal", tmcol(19, g0, 8), tmcol(18, g0, 8))
        build_bcast(tmcol(18, g0, 8), Bw, 8)
        for ft in range(8):
            EM("v", "tensor_tensor", 
                tA[:, 0:HALF], h1h2[:, ft * HALF:(ft + 1) * HALF], Bw[:], Alu.mult)
            rnd(xq3[:, ft * T + t0: ft * T + t0 + HALF], tA[:, 0:HALF])
        build_bcast(tmcol(19, g0, 8), Bu, 8)
        EM("v", "tensor_copy", 
            Bs1F[:, half * HALF:(half + 1) * HALF], Bu[:])

    ckpt("p7")
    S.barrier()
    for q in range(4):
        t0 = q * QTR
        g0q = q * 4
        half = t0 // HALF
        hoff = t0 % HALF
        if q > 0:
            S.wait("a", "w")
        def w1_load_quant(m):
            bi = m % 2
            if q == 0:
                d1 = dma(wst[bi][:, 0:1024], w1_d[m], gbuf=f"wst{bi}")
                S.wait_tok("a", d1)
                for tok in guard.get(f"wqs{bi}", []):
                    S.wait_tok("v", tok)
                for tok in guard.get(f"wqb{bi}", []):
                    S.wait_tok("v", tok)
                quant_strip(wst[bi], wqb[bi], 18, 1024)
                gset(f"wst{bi}", ("v", S.cnt["v"]))
                qd = S.cnt["v"]
                S.wait("a", "v", qd)
                gset(f"wqs{bi}", dma_w(scr_w1_d[m], wqb[bi][:, 0:1024]))
                return ("v", qd)
            return dma(wqb[bi][:, 0:1024], scr_w1_d[m], gbuf=f"wqb{bi}")

        def w1_mms(m, qtok):
            bi = m % 2
            r0 = m % 3
            S.wait_tok("t", qtok)
            for tok in guard.get(f"psa{r0}", []):
                S.wait_tok("t", tok)
            for tok in guard.get(f"psv{r0}", []):
                S.wait_tok("t", tok)
            for k in range(8):
                mm(psR[r0][:, 0:512], wqb[bi][:, k * 128:(k + 1) * 128],
                   xq3[:, k * T + t0: k * T + t0 + QTR], start=(k == 0),
                   stop=(k == 7))
            gset(f"wqb{bi}", ("t", S.cnt["t"]))
            return ("t", S.cnt["t"])

        def w1_evac(m, mtok):
            r0 = m % 3
            S.wait_tok("v", mtok)
            EM("v", "scalar_tensor_tensor",
                tA[:, 0:QTR], psR[r0][:, 0:512], mwbc[:, 18:19],
                Bs1F[:, half * HALF + hoff: half * HALF + hoff + QTR],
                Alu.mult, Alu.mult)
            gset(f"psv{r0}", ("v", S.cnt["v"]))
            S.wait("a", "v")
            EM("a", "activation",
                hft[:, m * QTR:(m + 1) * QTR], tA[:, 0:QTR], Act.Gelu)

        pend = None
        for m in range(33):
            if m < 32:
                qtok = w1_load_quant(m)
            if pend is not None:
                w1_evac(*pend)
            if m < 32:
                pend = (m, w1_mms(m, qtok))
            else:
                pend = None
        S.wait("v", "a")
        for j in range(16):
            S.wait("a", "v")
            EM("a", "activation",
                tC[:, 0:QTR], hft[:, 2 * j * QTR:(2 * j + 1) * QTR],
                Act.Square)
            EM("a", "activation",
                tD[:, 0:QTR], hft[:, (2 * j + 1) * QTR:(2 * j + 2) * QTR],
                Act.Square)
            S.wait("v", "a")
            EM("v", "tensor_tensor", tD[:, 0:QTR], tC[:, 0:QTR],
                                           tD[:, 0:QTR], Alu.max)
            if j == 0:
                EM("v", "tensor_copy", tE_[:, 0:QTR], tD[:, 0:QTR])
            else:
                EM("v", "tensor_tensor", tE_[:, 0:QTR], tE_[:, 0:QTR],
                                               tD[:, 0:QTR], Alu.max)
        maxtree_to_tm(tE_, QTR, 1, 20, g0q, 4)
        S.wait("a", "v")
        EM("a", "activation", tmcol(20, g0q, 4), tmcol(20, g0q, 4), Act.Sqrt)
        S.wait("v", "a")
        EM("v", "tensor_scalar", tmcol(20, g0q, 4), tmcol(20, g0q, 4), 1e-5,
                                       None, Alu.max)
        EM("v", "reciprocal", tmcol(21, g0q, 4), tmcol(20, g0q, 4))
        EM("v", "tensor_scalar", tmcol(21, g0q, 4), tmcol(21, g0q, 4), 127.0,
                                       None, Alu.mult)
        EM("v", "reciprocal", tmcol(22, g0q, 4), tmcol(21, g0q, 4))
        build_bcast(tmcol(21, g0q, 4), Bw, 4)
        build_bcast(tmcol(22, g0q, 4), Bx, 4)
        for j in range(32):
            EM("v", "tensor_tensor", tA[:, 0:QTR],
                                                hft[:, j * QTR:(j + 1) * QTR],
                                                Bw[:, 0:QTR], Alu.mult)
            rnd(hft[:, j * QTR:(j + 1) * QTR], tA[:, 0:QTR])
        hq_tok = ("v", S.cnt["v"])

        def w2_load_quant(m):
            if q == 0:
                for hk in range(2):
                    d1 = dma(wst[hk][:],
                             w2_d[m][:, hk * 2048:(hk + 1) * 2048],
                             gbuf=f"wst{hk}")
                    S.wait_tok("a", d1)
                    for tok in guard.get("wq2s", []):
                        S.wait_tok("v", tok)
                    for tok in guard.get("wq20", []):
                        S.wait_tok("v", tok)
                    EM("a", "activation", wst[hk][:], wst[hk][:],
                                                Act.Identity,
                                                scale=rmwbc[:, 19:20],
                                                bias=magic_col[:, 0:1])
                    S.wait("v", "a")
                    EM("v", "tensor_scalar",
                        wst[hk][:], wst[hk][:], MAGIC, -1.0, Alu.subtract,
                        Alu.max)
                    EM("v", "tensor_scalar",
                        wq2[0][:, hk * 2048:(hk + 1) * 2048], wst[hk][:],
                        1.0, None, Alu.min)
                    gset(f"wst{hk}", ("v", S.cnt["v"]))
                qd = S.cnt["v"]
                S.wait("a", "v", qd)
                gset("wq2s", dma_w(scr_w2_d[m], wq2[0][:]))
                return ("v", qd)
            dd = [dma(wq2[0][:, i * 1024:(i + 1) * 1024],
                      scr_w2_d[m][:, i * 1024:(i + 1) * 1024],
                      gbuf=("wq20" if i == 0 else None)) for i in range(4)]
            return dd

        def w2_mms(m, qtok):
            r0 = m % 3
            for tk in (qtok if isinstance(qtok, list) else [qtok]):
                S.wait_tok("t", tk)
            S.wait_tok("t", hq_tok)
            for tok in guard.get(f"psa{r0}", []):
                S.wait_tok("t", tok)
            for tok in guard.get(f"psv{r0}", []):
                S.wait_tok("t", tok)
            for k in range(32):
                mm(psR[r0][:, 0:512], wq2[0][:, k * 128:(k + 1) * 128],
                   hft[:, k * QTR:(k + 1) * QTR], start=(k == 0), stop=(k == 31))
            gset("wq20", ("t", S.cnt["t"]))
            return ("t", S.cnt["t"])

        def w2_evac(m, mtok):
            r0 = m % 3
            S.wait_tok("v", mtok)
            EM("v", "scalar_tensor_tensor",
                tA[:, 0:QTR], psR[r0][:, 0:512], mwbc[:, 19:20], Bx[:, 0:QTR],
                Alu.mult, Alu.mult)
            gset(f"psv{r0}", ("v", S.cnt["v"]))
            gv = vecs[:, 40 + m:41 + m]
            EM("v", "scalar_tensor_tensor",
                tB[:, 0:QTR], tA[:, 0:QTR], gv,
                acc[:, m * T + t0: m * T + t0 + QTR], Alu.mult, Alu.add)
            obuf = tA[:, 512:1024] if m % 2 == 0 else tB[:, 512:1024]
            for tok in guard.get(f"ob{m % 2}", []):
                S.wait_tok("a", tok)
            for s4 in range(4):
                S.wait("t", "v")
                S.wait("t", "a")
                tr(ps_misc[:, 0:128], tB[:, s4 * 128:(s4 + 1) * 128], id128[:])
                S.wait("a", "t")
                EM("a", "activation",
                    obuf[:, s4 * 128:(s4 + 1) * 128],
                    ps_misc[:, 0:128], Act.Identity)
            wtok = dma_w(
                out_d[t0:t0 + QTR, m * 128:(m + 1) * 128]
                .rearrange("(s p) d -> p s d", p=128),
                obuf[:].rearrange("p (s d) -> p s d", d=128))
            gset(f"ob{m % 2}", wtok)

        pend = None
        for m in range(9):
            if m < 8:
                qtok = w2_load_quant(m)
            if pend is not None:
                w2_evac(*pend)
            if m < 8:
                pend = (m, w2_mms(m, qtok))
            else:
                pend = None

    S.stopped = False
    tap_srcs = {
        "t_vecs": (vecs, [128, 48]), "t_offrow": (offrow, [1, 8]),
        "t_wkeep": (wkeep, [128, 128]), "t_mwrow": (mwrow, [1, 32]),
        "t_rows": (rows_tm, [128, NG * NQ]), "t_acc": (acc, [128, FT * T]),
        "t_selB": (selB, [128, 128]), "t_wselB": (wselB, [128, 128]),
        "t_cum": (cumsb, [128, 128]),
        "t_xq1": (xq1, [128, FT * T]),
        "t_h1": (h1h, [128, FT * HALF]),
        "t_xbh": (xbh, [128, FT * HALF]),
        "t_cbc": (cbc, [128, 32]),
        "t_statsA": (statsA, [128, NGA * 13]),
        "t_statsB": (statsB, [128, NG * 13]),
        "t_probsA": (probsA, [128, NGA * E]),
        "t_cbco": (cbco, [128, 32]),
        "t_tmA": (tmA, [128, NGA]), "t_tmB": (tmB, [128, NGA]),
        "t_tmC": (tmC, [128, NGA]), "t_tmD": (tmD, [128, NGA]),
        "t_offB": (offBrow, [1, NG * E]),
        "t_zt": (zt, [128, IMT * HALF]),
        "t_pafa": (pafa, [128, FT * 512]),
        "t_past": (past, [16, 512]),
        "t_sxa": (sxa, [128, FT * 11]),
        "t_vecsd": (vecs, [128, 48]),
        "t_idbf": (idbf, [128, 128]),
        "t_id128": (id128, [128, 128]),
        "t_tB": (tB, [128, 1024]),
    }
    S.barrier(engs="ag")
    for name in dbg:
        tsrc, _shape = tap_srcs[name]
        shape = list(dbg[name].shape)
        ap = tsrc[0:shape[0], 0:shape[1]]
        S.cnt.setdefault("dbg", 0)
        S.mark("g", lambda E_, d=dbg[name], a=ap: E_.dma_start(d, a),
               sem_key="dbg", dma=True)
    if dbg:
        S.items.append(("a", lambda E_: E_.wait_ge(sems["dbg"], S.cnt["dbg"])))

    for i in range(Sched.NST):
        fw = S.cnt[f"w{i}"]
        if fw > 0:
            S.items.append(
                ("a", lambda E_, i=i, fw=fw: E_.wait_ge(sems[f"w{i}"], fw)))

    with es:
        with nc.Block() as block:
            @block.sync
            def _(sync):
                S.replay("s", sync)

            @block.tensor
            def _(t_):
                S.replay("t", t_)

            @block.vector
            def _(v):
                S.replay("v", v)

            @block.scalar
            def _(a):
                S.replay("a", a)

            @block.gpsimd
            def _(g):
                S.replay("g", g)

    return nc, list(dbg.keys())


def _strips(w, nstrip, nk):
    a = np.asarray(w, dtype=np.float32)
    lead = a.shape[:-2]
    a = a.reshape(lead + (nstrip, 128, nk, 128))
    ax = list(range(len(lead))) + [len(lead), len(lead) + 3, len(lead) + 2,
                                   len(lead) + 1]
    a = np.ascontiguousarray(a.transpose(ax))
    return a.reshape(lead + (nstrip, 128, nk * 128))


def _host_common(inputs):
    x = np.ascontiguousarray(np.asarray(inputs["L0_attn_x"], dtype=np.float32))
    xf = x.reshape(NT, D)
    common = {
        "x_all": xf,
        "cin": np.ascontiguousarray(np.asarray(inputs["L0_c"], dtype=np.float32)),
        "wmod": _strips(np.asarray(inputs["L0_wmod"],
                                   dtype=np.float32).T, 48, 8),
        "gwt": np.ascontiguousarray(np.asarray(inputs["L0_attn_gate_w"],
                                               dtype=np.float32).T),
        "gnwt": np.ascontiguousarray(np.asarray(inputs["L0_attn_gate_norm_w"],
                                                dtype=np.float32).reshape(D, 1)),
        "sgw_t": _strips(inputs["L0_attn_shared_gate_w"], 32, 8),
        "sdw_t": _strips(inputs["L0_attn_shared_down_w"], 8, 16),
        "egw_t": _strips(inputs["L0_attn_expert_gate_w"], 32, 8),
        "edw_t": _strips(inputs["L0_attn_expert_down_w"], 8, 16),
        "w1_t": _strips(inputs["L0_ffn_w1"], 32, 8),
        "w2_t": _strips(inputs["L0_ffn_w2"], 8, 32),
    }
    return common, xf


def kernel(**inputs):
    common, xf = _host_common(inputs)
    in_maps = []
    for core in range(NCORE):
        m = dict(common)
        m["x_own"] = np.ascontiguousarray(xf[core * T:(core + 1) * T])
        mk = np.zeros((1, NGA), dtype=np.float32)
        mk[0, : core * NG] = 1.0
        m["maskrow"] = mk
        bs = np.zeros((1, 2), dtype=np.float32)
        bs[0, core // 4] = 1.0
        m["bselrow"] = bs
        in_maps.append(m)

    nc, _ = build_nc()
    res = run_bass_kernel_spmd(nc, in_maps, core_ids=list(range(NCORE)))
    out = np.concatenate([np.asarray(res.results[i]["out"]) for i in range(NCORE)],
                         axis=0)
    return out.reshape(2, 8192, D)


# version 13
# speedup vs baseline: 1.0073x; 1.0073x over previous
import contextlib
import numpy as np

import concourse.bass as bass
import concourse.mybir as mybir
from concourse.bass_utils import run_bass_kernel_spmd

F32 = mybir.dt.float32
BF16 = mybir.dt.bfloat16
Alu = mybir.AluOpType
Act = mybir.ActivationFunctionType
AX = mybir.AxisListType

MAGIC = 12582912.0
NCORE = 8
T = 2048
NT = 16384
D = 1024
FT = 8
IM = 2048
IMT = 16
CAP = 2560.0
E = 8
HALF = 1024
QTR = 512
NG = 16
NGA = 128
NQ = 24

MEAN_N = {i: 4096 * 1024 for i in range(20)}
MEAN_N[1] = 2048 * 1024
for _e in range(8):
    MEAN_N[10 + _e] = 2048 * 1024


class Sched:
    NDMA = 8
    NST = 4

    def __init__(self, sems):
        self.stopped = False
        self.items = []
        self.cnt = {e: 0 for e in "tvag"}
        for i in range(self.NDMA):
            self.cnt[f"d{i}"] = 0
        for i in range(self.NST):
            self.cnt[f"w{i}"] = 0
        self.sem = sems
        self.dseq = 0
        self.dseq_f = 0
        self.dseq_b = 0
        self.wseq = 0
        self.seen = {e: {} for e in "stvag"}

    def mark(self, eng, builder, sem_key=None, dma=False):
        if self.stopped:
            return (sem_key or eng, self.cnt[sem_key or eng])
        se = sem_key or eng
        n = 16 if dma else 1
        self.cnt[se] += n
        sem = self.sem

        def fn(E, b=builder, s=se, k=n):
            b(E).then_inc(sem[s], k)
        self.items.append((eng, fn))
        return (se, self.cnt[se])

    def wait_tok(self, eng, tok):
        if self.stopped:
            return
        key, c = tok
        if c <= 0 or self.seen[eng].get(key, 0) >= c:
            return
        self.seen[eng][key] = c
        sem = self.sem
        self.items.append((eng, lambda E, p=key, cc=c: E.wait_ge(sem[p], cc)))

    def wait(self, eng, prod, count=None):
        if prod == "s":
            for i in range(self.NDMA):
                self.wait_tok(eng, (f"d{i}", self.cnt[f"d{i}"]))
            return
        if prod == "w":
            for i in range(self.NST):
                self.wait_tok(eng, (f"w{i}", self.cnt[f"w{i}"]))
            return
        c = self.cnt[prod] if count is None else count
        self.wait_tok(eng, (prod, c))

    def dma_sem(self, pool="any"):
        if pool == "fast":
            key = f"d{self.dseq_f % 4}"
            self.dseq_f += 1
        elif pool == "bulk":
            key = f"d{4 + self.dseq_b % 4}"
            self.dseq_b += 1
        else:
            key = f"d{self.dseq % self.NDMA}"
            self.dseq += 1
        return key

    def st_sem(self):
        key = f"w{self.wseq % self.NST}"
        self.wseq += 1
        return key

    def barrier(self, engs="stvag", prods="stvag"):
        for e in engs:
            for p in prods:
                if p == e:
                    continue
                if p == "s":
                    continue
                self.wait(e, p)
        for e in engs:
            self.wait(e, "s")
            self.wait(e, "w")

    def replay(self, eng, engobj):
        for e, fn in self.items:
            if e == eng:
                fn(engobj)


def build_nc(taps=(), stop_at=None):
    nc = bass.Bass("TRN2", target_bir_lowering=False, detect_race_conditions=False)

    def din(name, shape, dt=F32):
        return nc.dram_tensor(name, shape, dt, kind="ExternalInput").ap()

    x_own_d = din("x_own", [T, D])
    x_all_d = din("x_all", [NT, D])
    maskrow_d = din("maskrow", [1, NGA])
    bsel_d = din("bselrow", [1, 2])
    cin_d = din("cin", [2, D])
    wmod_d = din("wmod", [48, 128, 1024])
    gwt_d = din("gwt", [D, E])
    gnwt_d = din("gnwt", [D, 1])
    sgw_d = din("sgw_t", [32, 128, 1024])
    sdw_d = din("sdw_t", [8, 128, 2048])
    egw_d = din("egw_t", [E, 32, 128, 1024])
    edw_d = din("edw_t", [E, 8, 128, 2048])
    w1_d = din("w1_t", [32, 128, 1024])
    w2_d = din("w2_t", [8, 128, 4096])

    out_d = nc.dram_tensor("out", [T, D], F32, kind="ExternalOutput").ap()
    dbg = {}
    for name, shape in taps:
        dbg[name] = nc.dram_tensor(name, shape, F32, kind="ExternalOutput").ap()

    scr_g_d = nc.dram_tensor("scr_g", [9, 16, 128, 2048], BF16, kind="Internal").ap()
    scr_d_d = nc.dram_tensor("scr_d", [9, 8, 128, 2048], BF16, kind="Internal").ap()
    scr_w1_d = nc.dram_tensor("scr_w1", [32, 128, 1024], BF16, kind="Internal").ap()
    scr_w2_d = nc.dram_tensor("scr_w2", [8, 128, 4096], BF16, kind="Internal").ap()

    es = contextlib.ExitStack()

    def sb(n, shape, d=F32):
        return es.enter_context(nc.sbuf_tensor(n, shape, d))

    def psb(n, shape, d=F32):
        return es.enter_context(nc.psum_tensor(n, shape, d))

    id128 = sb("id128", [128, 128])
    idbf = sb("idbf", [128, 128], BF16)
    tribf = sb("tribf", [128, 128], BF16)
    onesP = sb("onesP", [128, 128], BF16)
    ones_m1b = sb("ones_m1b", [128, 1], BF16)
    ones_k1b = sb("ones_k1b", [1, 128], BF16)
    e8sel = sb("e8sel", [8, 128], BF16)
    vecs = sb("vecs", [128, 48])
    mod_sb = sb("mod_sb", [128, 96])
    csT = sb("csT", [128, 16])
    bsel_bc = sb("bsel_bc", [128, 2])
    bsb = sb("bsb", [1, 2], BF16)
    gw2 = sb("gw2", [128, FT * E], BF16)
    sxa = sb("sxa", [128, 2 * FT * 11], BF16)
    sx2 = sb("sx2", [128, 2 * FT * 2], BF16)
    crow = sb("crow", [1, 8])
    agrow = sb("agrow", [1, E])
    c4row = sb("c4row", [1, E])
    cb_src = sb("cb_src", [1, 32], BF16)
    cbc = sb("cbc", [128, 64])
    offrow = sb("offrow", [1, E])
    maskb = sb("maskb", [1, NGA], BF16)
    rows_tm = sb("rows_tm", [128, NG * NQ])
    magic_col = sb("magicc", [128, 1])
    rowst = sb("rowst", [16, 512])
    past = sb("past", [16, 512])
    diagt = sb("diagt", [128, 128], BF16)
    mwcol = sb("mwcol", [128, 32])
    mwrow = sb("mwrow", [1, 32])
    mwc32 = sb("mwc32", [32, 1])
    mwbc = sb("mwbc", [128, 32])
    rmwbc = sb("rmwbc", [128, 32])
    cntinv = sb("cntinv", [1, 32])
    mwrow_b = sb("mwrow_b", [128, 32], BF16)
    xq1 = sb("xq1", [128, FT * T], BF16)
    acc = sb("acc", [128, FT * T])
    wkT = sb("wkT", [8, T], BF16)
    selB = sb("selB", [128, NG * E])
    selBb = sb("selBb", [128, NG * E], BF16)
    wselB = sb("wselB", [128, NG * E])
    wkeep = sb("wkeep", [128, NG * E])
    wkeepb = sb("wkeepb", [128, NG * E], BF16)
    offBrow = sb("offBrow", [1, NG * E])
    offBb = sb("offBb", [1, NG * E], BF16)
    cumsb = sb("cumsb", [128, NG * E])
    Bs1F = sb("Bs1F", [128, T], BF16)
    tA = sb("tA", [128, 1024])
    tB = sb("tB", [128, 1024])
    tC = sb("tC", [128, 1024], BF16)
    tD = sb("tD", [128, 1024], BF16)
    paxb = tD
    tE_ = sb("tE_", [128, 1024], BF16)
    tF = sb("tF", [128, 1024], BF16)
    trow = sb("trow", [1, 512])
    wst = [sb("wstA", [128, 2048]), sb("wstB", [128, 2048])]
    wqb = [sb("wqbA", [128, 2048], BF16), sb("wqbB", [128, 2048], BF16)]
    Bu = sb("Bu", [128, HALF], BF16)
    Bv = sb("Bv", [128, HALF], BF16)
    Bw = Bu
    Bx = Bv
    xq3 = xq1
    es1 = contextlib.ExitStack()

    def sb1(n, shape, d=F32):
        return es1.enter_context(nc.sbuf_tensor(n, shape, d))
    pafa = sb1("pafa", [128, FT * 512], BF16)
    paf2 = sb1("paf2", [128, 512], BF16)
    paflo = sb1("paflo", [128, FT * 512], BF16)
    statsB = sb1("statsB", [128, NG * 13])
    sxo = sb1("sxo", [128, FT * 11], BF16)
    sx2o = sb1("sx2o", [128, FT * 2], BF16)
    cbco = sb1("cbco", [128, 32])
    statsA = sb1("statsA", [128, NGA * 13], BF16)
    probsA = sb1("probsA", [128, NGA * E])
    selAb = sb1("selAb", [128, NGA * E], BF16)
    cntrow = sb1("cntrow", [1, NGA * E], BF16)
    tmA = sb1("tmA", [128, NGA])
    tmB = sb1("tmB", [128, NGA])
    tmC = sb1("tmC", [128, NGA])
    tmD = sb1("tmD", [128, NGA])
    tmE = sb1("tmE", [128, NGA])
    es1.close()
    es2 = contextlib.ExitStack()
    xbh = es2.enter_context(nc.sbuf_tensor("xbh", [128, FT * HALF], BF16))
    h1h = es2.enter_context(nc.sbuf_tensor("h1h", [128, FT * HALF], BF16))
    es2.close()
    es3 = contextlib.ExitStack()
    zt = es3.enter_context(nc.sbuf_tensor("zt", [128, IMT * HALF], BF16))
    es3.close()
    es4 = contextlib.ExitStack()
    xbh2 = es4.enter_context(nc.sbuf_tensor("xbh2", [128, FT * HALF], BF16))
    h1h2 = es4.enter_context(nc.sbuf_tensor("h1h2", [128, FT * HALF], BF16))
    es4.close()
    _wq20 = es.enter_context(nc.sbuf_tensor("wq20", [128, 4096], BF16))
    wq2 = [_wq20, _wq20]
    hft = es.enter_context(nc.sbuf_tensor("hft", [128, 32 * QTR], BF16))

    psR = [psb(f"psR{i}", [128, 1024]) for i in range(3)]
    ps_stat = psb("ps_stat", [16, 512])
    ps_misc = psb("ps_misc", [128, 512])

    sems = {}
    for key, nm in [("t", "Stn"), ("v", "Sve"), ("a", "Sac"), ("g", "Sgp")]:
        sems[key] = es.enter_context(nc.semaphore(nm))
    for i in range(Sched.NDMA):
        sems[f"d{i}"] = es.enter_context(nc.semaphore(f"Sd{i}"))
    for i in range(Sched.NST):
        sems[f"w{i}"] = es.enter_context(nc.semaphore(f"Sw{i}"))
    sems["dbg"] = es.enter_context(nc.semaphore("Sdbg"))

    S = Sched(sems)
    S.v_need_drain = False

    def ckpt(name):
        if stop_at is not None and name == stop_at:
            S.stopped = True
    guard = {}

    def dguard(buf):
        for tok in guard.get(buf, []):
            S.wait_tok("s", tok)

    def gset(buf, *toks):
        guard[buf] = list(toks)

    def gadd(buf, tok):
        guard.setdefault(buf, []).append(tok)

    def dma(dst, src, gbuf=None, slow=False, pool="any"):
        if gbuf is not None:
            dguard(gbuf)
        key = S.dma_sem(pool)
        S.wait_tok("s", (key, S.cnt[key]))
        return S.mark("s", lambda E_: E_.dma_start(
            dst, src, allow_slow_non_contiguous=slow), sem_key=key, dma=True)

    def dma_w(dst, src):
        key = S.st_sem()
        S.wait_tok("a", (key, S.cnt[key]))
        return S.mark("a", lambda E_: E_.dma_start(dst, src), sem_key=key, dma=True)

    def mm(out, lhsT, rhs, start=True, stop=True):
        return S.mark("t", lambda E_: E_.matmul(out, lhsT, rhs, start=start, stop=stop))

    def tr(out, in_, ident):
        return S.mark("t", lambda E_: E_.transpose(out, in_, ident))

    def v_(b):
        return S.mark("v", b)

    def a_(b):
        return S.mark("a", b)

    def g_(b):
        S.wait("g", "g")
        return S.mark("g", b)

    def _free_elems(ap):
        try:
            shp = list(ap.shape)
        except Exception:
            return 1 << 30
        n = 1
        for s in shp[1:]:
            n *= s
        return n

    def EM(eng, method, *args, **kwargs):
        if eng == "g":
            S.wait("g", "g")
        if eng == "v":
            if S.v_need_drain:
                S.wait("v", "v")
                S.v_need_drain = False
            if _free_elems(args[0]) < 128:
                S.v_need_drain = True

        def fn(E_, m=method, a=args, k=kwargs):
            return getattr(E_, m)(*a, **k)
        return S.mark(eng, fn)

    def rnd(out, in_):
        return EM("v", "tensor_scalar", out, in_, MAGIC, MAGIC, Alu.add,
                                              Alu.subtract)

    def tmcol(q, g0=0, ng=NG):
        return rows_tm[:, q * NG + g0: q * NG + g0 + ng]

    tmview = rows_tm[:].rearrange("p (q g) -> p q g", g=NG)

    def build_bcast(tmslice, dst, ng):
        for g in range(ng):
            S.wait("v", "t")
            EM("v", "tensor_scalar", diagt[:], idbf[:],
                                                tmslice[:, g:g + 1], None, Alu.mult)
            S.wait("t", "v")
            S.wait("t", "a")
            mm(ps_misc[:, 0:128], onesP[:], diagt[:])
            S.wait("a", "t")
            EM("a", "activation", dst[:, g * 128:(g + 1) * 128],
                                             ps_misc[:, 0:128], Act.Identity)
        S.wait("v", "a")

    def rows_to_tm(k, qglist, g0):
        for s in range(4):
            S.wait("t", "a")
            S.wait("t", "v")
            tr(ps_misc[0:128, 0:k], rowst[0:k, s * 128:(s + 1) * 128],
               id128[0:k, 0:k])
            S.wait("a", "t")
            for r, (q, gd) in enumerate(qglist):
                EM("a", "activation", 
                    tmview[:, q, g0 + gd + s: g0 + gd + s + 1],
                    ps_misc[0:128, r:r + 1], Act.Identity)

    def two_rows_to_tm(qa, g0a, qb, g0b):
        S.wait("a", "t")
        EM("a", "activation", rowst[0:1, :], ps_stat[0:1, 0:512],
                                    Act.Identity)
        EM("a", "activation", past[0:1, :], ps_misc[0:1, 0:512],
                                    Act.Identity)
        rows_to_tm(1, [(qa, 0)], g0a)
        S.wait("v", "a")
        EM("v", "tensor_copy", rowst[0:1, 0:512], past[0:1, 0:512])
        S.wait("a", "v")
        rows_to_tm(1, [(qb, 0)], g0b)

    def quant_strip(wstrip, wq_out, mi, ncol):
        EM("a", "activation", wstrip[:, 0:ncol], wstrip[:, 0:ncol],
                                    Act.Identity, scale=rmwbc[:, mi:mi + 1],
                                    bias=magic_col[:, 0:1])
        S.wait("v", "a")
        EM("v", "tensor_scalar", wstrip[:, 0:ncol], wstrip[:, 0:ncol],
                                       MAGIC, -1.0, Alu.subtract, Alu.max)
        EM("v", "tensor_scalar", wq_out[:, 0:ncol], wstrip[:, 0:ncol],
                                       1.0, None, Alu.min)

    def maxtree_to_tm(buf, width, ntile, qdst, g0, ng):
        n = ntile // 2
        while n >= 1:
            for j in range(n):
                EM("v", "tensor_tensor",
                    buf[:, j * width:(j + 1) * width],
                    buf[:, j * width:(j + 1) * width],
                    buf[:, (j + n) * width:(j + n + 1) * width], Alu.max)
            n //= 2
        S.wait("v", "t")
        EM("v", "tensor_copy", tC[0:64, 0:width], buf[64:128, 0:width])
        EM("v", "tensor_tensor", buf[0:64, 0:width], buf[0:64, 0:width],
                                       tC[0:64, 0:width], Alu.max)
        EM("v", "tensor_copy", tC[0:32, 0:width], buf[32:64, 0:width])
        EM("v", "tensor_tensor", buf[0:32, 0:width], buf[0:32, 0:width],
                                       tC[0:32, 0:width], Alu.max)
        nsub = width // 128
        for s in range(nsub):
            S.wait("t", "v")
            S.wait("t", "a")
            tr(ps_misc[:, 0:128].bitcast(BF16)[:, 0:32],
               buf[0:32, s * 128:(s + 1) * 128], idbf[0:32, 0:32])
            S.wait("a", "t")
            EM("a", "activation", tA[:, s * 32:(s + 1) * 32],
                                             ps_misc[:, 0:128].bitcast(BF16)[:, 0:32],
                                             Act.Identity)
        S.wait("v", "a")
        gadd("tAx", ("a", S.cnt["a"]))
        EM("v", "reduce_max", 
            tmcol(qdst, g0, ng),
            tA[:, 0:nsub * 32].rearrange("p (g k) -> p g k", k=32), axis=AX.X)

    EM("g", "memset", id128[:], 1.0)
    S.wait("g", "g")
    EM("g", "affine_select", id128[:], id128[:], [[1, 128]], Alu.is_equal,
                                   0.0, base=0, channel_multiplier=-1)
    EM("g", "memset", tribf[:], 1.0)
    S.wait("g", "g")
    EM("g", "affine_select", tribf[:], tribf[:], [[1, 128]], Alu.is_ge,
                                   0.0, base=0, channel_multiplier=-1)
    EM("g", "memset", onesP[:], 1.0)
    EM("g", "memset", ones_m1b[:], 1.0)
    EM("g", "memset", ones_k1b[:], 1.0)
    EM("g", "memset", mwcol[:], 0.0)
    EM("g", "memset", magic_col[:], MAGIC)

    EM("g", "memset", cntinv[:], 0.0)
    for mi, n in MEAN_N.items():
        EM("g", "memset", cntinv[0:1, mi:mi + 1], 1.0 / n)
    if taps:
        _tap_init = {
            "t_pafa": pafa, "t_past": past, "t_rows": rows_tm, "t_statsA": statsA,
            "t_zt": zt, "t_h1": h1h, "t_xbh": xbh, "t_xq1": xq1, "t_acc": acc,
            "t_wkeep": wkeep, "t_selB": selB, "t_wselB": wselB, "t_cum": cumsb,
        }
        for nm, _shape in taps:
            if nm in _tap_init:
                EM("g", "memset", _tap_init[nm][:], 0.0)
    S.wait("v", "g")
    EM("v", "tensor_copy", idbf[:], id128[:])

    d_c = dma(tA[0:2, 0:1024], cin_d[:])
    d_b = dma(trow[0:1, 0:2], bsel_d[:])
    d_m = dma(trow[0:1, 16:16 + NGA], maskrow_d[:])
    S.wait("a", "s")
    EM("a", "activation", tA[0:2, 0:1024], tA[0:2, 0:1024], Act.Silu)
    for k in range(8):
        S.wait("t", "a")
        tr(ps_misc[:, 0:2], tA[0:2, k * 128:(k + 1) * 128], id128[0:2, 0:2])
        S.wait("a", "t")
        EM("a", "activation", csT[:, 2 * k:2 * k + 2], ps_misc[:, 0:2],
                                         Act.Identity)
    S.wait("v", "s")
    EM("v", "tensor_copy", maskb[:], trow[0:1, 16:16 + NGA])
    for m in range(48):
        dd = dma(wst[m % 2][:, 0:1024], wmod_d[m], gbuf=f"mst{m % 2}")
        S.wait("t", "s")
        S.wait("t", "a")
        for k in range(8):
            mm(ps_misc[:, 0:2], wst[m % 2][:, k * 128:(k + 1) * 128],
               csT[:, 2 * k:2 * k + 2], start=(k == 0), stop=(k == 7))
        gset(f"mst{m % 2}", ("t", S.cnt["t"]))
        S.wait("a", "t")
        EM("a", "activation", mod_sb[:, 2 * m:2 * m + 2], ps_misc[:, 0:2],
                                         Act.Identity)
    S.wait("v", "s")
    EM("v", "tensor_copy", bsb[:], trow[0:1, 0:2])
    S.wait("t", "v")
    S.wait("t", "a")
    mm(ps_misc[:, 0:2], ones_k1b[0:1, :], bsb[:])
    S.wait("a", "t")
    EM("a", "activation", bsel_bc[:], ps_misc[:, 0:2], Act.Identity)
    S.wait("v", "a")
    mod3 = mod_sb[:].rearrange("p (m two) -> p m two", two=2)
    EM("v", "tensor_scalar", tA[:, 0:48], mod3[:, :, 1], bsel_bc[:, 1:2],
                                   None, Alu.mult)
    EM("v", "scalar_tensor_tensor", vecs[:], mod3[:, :, 0], bsel_bc[:, 0:1],
                                          tA[:, 0:48], Alu.mult, Alu.add)
    EM("v", "tensor_scalar", vecs[:, 8:16], vecs[:, 8:16], 1.0, None, Alu.add)
    EM("v", "tensor_scalar", vecs[:, 32:40], vecs[:, 32:40], 1.0, None,
                                   Alu.add)
    for c0 in (16, 40):
        S.wait("a", "v")
        EM("a", "activation", tA[:, 0:8], vecs[:, c0:c0 + 8], Act.Sign)
        EM("a", "activation", tA[:, 8:16], vecs[:, c0:c0 + 8], Act.Abs)
        S.wait("v", "a")
        EM("v", "tensor_scalar", tA[:, 8:16], tA[:, 8:16], 0.25, None, Alu.max)
        EM("v", "tensor_tensor", vecs[:, c0:c0 + 8], tA[:, 0:8],
                                              tA[:, 8:16], Alu.mult)
    S.wait("s", "v")
    S.wait("s", "a")
    d_gw = dma(tA[:, 0:64].rearrange("p (ft e) -> p ft e", e=8),
               gwt_d[:].rearrange("(ft p) e -> p ft e", p=128))
    d_gn = dma(tA[:, 64:72].rearrange("p (ft o) -> p ft o", o=1),
               gnwt_d[:].rearrange("(ft p) o -> p ft o", p=128), slow=True)
    S.wait("v", "s")
    for ft in range(8):
        EM("v", "tensor_scalar", gw2[:, ft * 8:ft * 8 + 8],
                                              tA[:, ft * 8:ft * 8 + 8],
                                              tA[:, 64 + ft:65 + ft], None, Alu.mult)
    for b in range(3):
        if b < 2:
            sxT, sxoff = sxa, b * 88
            sx2T, sx2off = sx2, b * 16
            cbcT, cbcoff = cbc, b * 32
        else:
            sxT, sxoff = sxo, 0
            sx2T, sx2off = sx2o, 0
            cbcT, cbcoff = cbco, 0
        for ft in range(8):
            if b < 2:
                av = tB[:, 40 + ft:41 + ft]
                EM("v", "tensor_scalar", av, mod3[:, 8 + ft, b:b + 1], 1.0, None,
                   Alu.add)
                bv = mod3[:, 0 + ft, b:b + 1]
            else:
                av = vecs[:, 8 + ft:9 + ft]
                bv = vecs[:, 0 + ft:1 + ft]
            EM("v", "tensor_scalar", 
                sxT[:, sxoff + ft * 11:sxoff + ft * 11 + 8],
                gw2[:, ft * 8:ft * 8 + 8], av, None, Alu.mult)
            EM("v", "memset", sxT[:, sxoff + ft * 11 + 8:sxoff + ft * 11 + 9], 1.0)
            EM("v", "tensor_tensor", 
                sxT[:, sxoff + ft * 11 + 9:sxoff + ft * 11 + 10], av, av, Alu.mult)
            EM("v", "tensor_tensor", 
                sxT[:, sxoff + ft * 11 + 10:sxoff + ft * 11 + 11], av, bv, Alu.mult)
            EM("v", "memset", sx2T[:, sx2off + ft * 2:sx2off + ft * 2 + 1], 1.0)
            EM("v", "tensor_copy", 
                sx2T[:, sx2off + ft * 2 + 1:sx2off + ft * 2 + 2],
                sxT[:, sxoff + ft * 11 + 9:sxoff + ft * 11 + 10])
        for ft in range(8):
            S.wait("v", "t")
            EM("v", "tensor_copy", tC[:, 0:1],
                sxT[:, sxoff + ft * 11 + 9:sxoff + ft * 11 + 10])
            EM("v", "tensor_copy", tC[:, 1:2],
                sxT[:, sxoff + ft * 11 + 10:sxoff + ft * 11 + 11])
            bv = (mod3[:, 0 + ft, b:b + 1] if b < 2 else vecs[:, 0 + ft:1 + ft])
            EM("v", "tensor_tensor", tA[:, 0:1], bv, bv, Alu.mult)
            EM("v", "tensor_copy", tC[:, 2:3], tA[:, 0:1])
            S.wait("t", "v")
            mm(ps_stat[0:3, 0:1], tC[:, 0:3], ones_m1b[:], start=(ft == 0),
               stop=(ft == 7))
        for ft in range(8):
            mm(ps_stat[0:8, 1:2], sxT[:, sxoff + ft * 11:sxoff + ft * 11 + 8],
               ones_m1b[:], start=(ft == 0), stop=(ft == 7))
        for ft in range(8):
            bv = (mod3[:, 0 + ft, b:b + 1] if b < 2 else vecs[:, 0 + ft:1 + ft])
            S.wait("v", "t")
            EM("v", "tensor_scalar",
                tC[:, 8:16], gw2[:, ft * 8:ft * 8 + 8], bv, None, Alu.mult)
            S.wait("t", "v")
            mm(ps_stat[0:8, 2:3], tC[:, 8:16], ones_m1b[:], start=(ft == 0),
               stop=(ft == 7))
        S.wait("a", "t")
        EM("a", "activation", rowst[0:3, 0:1], ps_stat[0:3, 0:1], Act.Identity)
        EM("a", "activation", rowst[0:8, 1:2], ps_stat[0:8, 1:2], Act.Identity)
        EM("a", "activation", rowst[0:8, 2:3], ps_stat[0:8, 2:3], Act.Identity)
        for col, nr, dst in ((0, 3, crow), (1, 8, agrow), (2, 8, c4row)):
            S.wait("t", "a")
            tr(ps_misc[0:1, 0:nr], rowst[0:nr, col:col + 1], id128[0:nr, 0:nr])
            S.wait("a", "t")
            EM("a", "activation", dst[0:1, 0:nr], ps_misc[0:1, 0:nr], Act.Identity)
        S.wait("v", "a")
        EM("v", "memset", cb_src[:], 0.0)
        EM("v", "tensor_copy", cb_src[0:1, 0:3], crow[0:1, 0:3])
        EM("v", "tensor_copy", cb_src[0:1, 8:16], agrow[0:1, :])
        EM("v", "tensor_copy", cb_src[0:1, 16:24], c4row[0:1, :])
        S.wait("t", "v")
        S.wait("t", "a")
        mm(ps_misc[:, 0:32], ones_k1b[0:1, :], cb_src[0:1, :])
        S.wait("a", "t")
        EM("a", "activation", cbcT[:, cbcoff:cbcoff + 32], ps_misc[:, 0:32],
           Act.Identity)

    ckpt("p1")
    flat_srcs = [(0, sgw_d.flatten()), (1, sdw_d.flatten())]
    for e in range(8):
        flat_srcs.append((2 + e, egw_d[e].flatten()))
    for e in range(8):
        flat_srcs.append((10 + e, edw_d[e].flatten()))
    flat_srcs.append((18, w1_d.flatten()))
    flat_srcs.append((19, w2_d.flatten()))

    CH = 128 * 2048
    meanitems = []
    for mi, flat in flat_srcs:
        for c in range(MEAN_N[mi] // CH):
            meanitems.append((mi, flat, c))

    meanbufs = [wst[0][:], wst[1][:]] + [
        acc[:, i * 2048:(i + 1) * 2048] for i in range(6)]

    def mean_chunk(tg):
        mi, flat, c = meanitems[tg]
        bi = tg % 8
        buf = meanbufs[bi]
        dd = dma(buf, flat[c * CH:(c + 1) * CH].rearrange("(p c) -> p c", p=128),
                 gbuf=f"mst{bi}", pool="bulk")
        if tg % 2 == 0:
            S.wait_tok("v", dd)
            EM("v", "reduce_sum", rows_tm[:, 376:377], buf, axis=AX.X,
                                                 apply_absolute_value=True)
            gset(f"mst{bi}", ("v", S.cnt["v"]))
            EM("v", "tensor_tensor",
                mwcol[:, mi:mi + 1], mwcol[:, mi:mi + 1], rows_tm[:, 376:377],
                Alu.add)
        else:
            S.wait_tok("a", dd)
            EM("a", "activation", wqb[0][:], buf, Act.Abs,
                                                 accum_out=rows_tm[:, 377:378])
            gset(f"mst{bi}", ("a", S.cnt["a"]))
            S.wait("v", "a")
            EM("v", "tensor_tensor",
                mwcol[:, mi:mi + 1], mwcol[:, mi:mi + 1], rows_tm[:, 377:378],
                Alu.add)

    def mean_final():
        S.wait("v", "a")
        EM("v", "tensor_copy", mwrow_b[:], mwcol[:])
        S.wait("t", "v")
        S.wait("t", "a")
        mm(ps_misc[0:32, 0:1], mwrow_b[:], ones_m1b[:])
        S.wait("a", "t")
        EM("a", "activation", mwc32[0:32, 0:1], ps_misc[0:32, 0:1], Act.Identity)
        S.wait("t", "a")
        tr(ps_misc[0:1, 0:32], mwc32[0:32, 0:1], id128[0:32, 0:32])
        S.wait("a", "t")
        EM("a", "activation", mwrow[0:1, :], ps_misc[0:1, 0:32], Act.Identity)
        S.wait("v", "a")
        EM("v", "tensor_tensor", mwrow[0:1, :], mwrow[0:1, :], cntinv[0:1, :],
                                       Alu.mult)
        EM("v", "tensor_scalar", mwrow[0:1, :], mwrow[0:1, :], 1e-5, None,
                                       Alu.max)
        EM("v", "tensor_copy", cb_src[0:1, 0:32], mwrow[0:1, :])
        S.wait("t", "v")
        S.wait("t", "a")
        mm(ps_misc[:, 0:32], ones_k1b[0:1, :], cb_src[0:1, :])
        S.wait("a", "t")
        EM("a", "activation", mwbc[:], ps_misc[:, 0:32], Act.Identity)
        S.wait("v", "a")
        EM("v", "reciprocal", rmwbc[:], mwbc[:])

    ckpt("p2")
    NCH = NT // 512 + T // 512
    mt = 0
    for ci in range(NCH):
        own = ci >= NT // 512
        ch = ci if not own else ci - NT // 512
        src_d = x_all_d if not own else x_own_d
        mgoal = (len(meanitems) * (ci + 1)) // NCH
        while mt < mgoal:
            mean_chunk(mt)
            mt += 1
        for s4 in range(4):
            g = ch * 4 + s4
            xb_ = g % 4
            xstg = [tA[:, 0:1024], tB[:, 0:1024],
                    acc[:, 12288:13312], acc[:, 13312:14336]][xb_]
            dd = dma(xstg, src_d[g * 128:(g + 1) * 128, :],
                     gbuf=f"xst{xb_}", pool="fast")
            S.wait_tok("v", dd)
            S.wait("v", "t")
            EM("v", "tensor_copy", paxb[:], xstg)
            EM("v", "tensor_tensor", tC[:, 0:1024], xstg, paxb[:],
               Alu.subtract)
            gset(f"xst{xb_}", ("v", S.cnt["v"]))
            pmb = ps_misc[:].bitcast(BF16)
            for src, dst in ((paxb, pafa), (tC, paflo)):
                S.wait("t", "v")
                S.wait("t", "a")
                for ft in range(8):
                    tr(pmb[:, ft * 128:(ft + 1) * 128],
                       src[:, ft * 128:(ft + 1) * 128], idbf[:])
                S.wait("a", "t")
                EM("a", "activation",
                    dst[:].rearrange("p (ft s c) -> p ft s c", s=4,
                                     c=128)[:, :, s4, :],
                    pmb[:, 0:1024].rearrange("p (ft c) -> p ft c", c=128),
                    Act.Identity)
        if own:
            sxU, sx2U = sxo, sx2o
            soff, s2off = 0, 0
        else:
            bx = ch // 16
            sxU, sx2U = sxa, sx2
            soff, s2off = bx * 88, bx * 16
        S.wait("v", "a")
        S.wait("t", "a")
        for ft in range(8):
            mm(ps_stat[0:11, 0:512], sxU[:, soff + ft * 11:soff + (ft + 1) * 11],
               pafa[:, ft * 512:(ft + 1) * 512], start=(ft == 0), stop=False)
        for ft in range(8):
            mm(ps_stat[0:11, 0:512], sxU[:, soff + ft * 11:soff + (ft + 1) * 11],
               paflo[:, ft * 512:(ft + 1) * 512], start=False, stop=(ft == 7))
        for ft in range(8):
            EM("v", "tensor_tensor", 
                paf2[:], pafa[:, ft * 512:(ft + 1) * 512],
                pafa[:, ft * 512:(ft + 1) * 512], Alu.mult)
            S.wait("t", "v")
            mm(ps_misc[0:2, 0:512], sx2U[:, s2off + ft * 2:s2off + (ft + 1) * 2],
               paf2[:], start=(ft == 0), stop=(ft == 7))
            if ft < 7:
                S.wait("v", "t")
        S.wait("a", "t")
        EM("a", "activation", past[0:11, :], ps_stat[0:11, 0:512], Act.Identity)
        EM("a", "activation", rowst[0:2, 0:512], ps_misc[0:2, 0:512], Act.Identity)
        stT = (statsA if not own else statsB)[:].rearrange("p (g q) -> p g q", q=13)
        for s4 in range(4):
            g = ch * 4 + s4
            S.wait("t", "a")
            tr(ps_misc[0:128, 0:11], past[0:11, s4 * 128:(s4 + 1) * 128],
               id128[0:11, 0:11])
            S.wait("a", "t")
            EM("a", "activation", stT[:, g, 0:11], ps_misc[0:128, 0:11],
               Act.Identity)
            S.wait("t", "a")
            tr(ps_misc[0:128, 0:2], rowst[0:2, s4 * 128:(s4 + 1) * 128],
               id128[0:2, 0:2])
            S.wait("a", "t")
            EM("a", "activation", stT[:, g, 11:13], ps_misc[0:128, 0:2],
               Act.Identity)
    mean_final()
    ckpt("pAstats")
    stA = statsA[:].rearrange("p (g q) -> p g q", q=13)
    stO = statsB[:].rearrange("p (g q) -> p g q", q=13)
    pA3 = probsA[:].rearrange("p (g e) -> p g e", e=E)
    pO3 = wselB[:].rearrange("p (g e) -> p g e", e=E)
    S.wait("v", "a")
    for b in range(3):
        if b < 2:
            stB = stA[:, b * 64:(b + 1) * 64, :]
            out3 = pA3[:, b * 64:(b + 1) * 64, :]
            cb_ap = cbc
            co = b * 32
            ngz = 64
        else:
            stB = stO[:, 0:16, :]
            out3 = pO3
            cb_ap = cbco
            co = 0
            ngz = 16
        tmAs = tmA[:, 0:ngz]; tmBs = tmB[:, 0:ngz]; tmCs = tmC[:, 0:ngz]
        tmDs = tmD[:, 0:ngz]; tmEs = tmE[:, 0:ngz]
        EM("v", "tensor_scalar", tmAs, stB[:, :, 8], 1.0 / D, None, Alu.mult)
        EM("v", "tensor_tensor", tmBs, tmAs, tmAs, Alu.mult)
        EM("v", "scalar_tensor_tensor", tmCs, stB[:, :, 11], 1.0 / D, tmBs,
            Alu.mult, Alu.subtract)
        EM("v", "tensor_scalar", tmCs, tmCs, 1e-5, None, Alu.add)
        EM("v", "reciprocal", tmCs, tmCs)
        S.wait("a", "v")
        EM("a", "activation", tmCs, tmCs, Act.Sqrt)
        S.wait("v", "a")
        EM("v", "tensor_tensor", tmDs, tmAs, stB[:, :, 9], Alu.mult)
        EM("v", "scalar_tensor_tensor", tmDs, tmDs, -2.0, stB[:, :, 12],
            Alu.mult, Alu.add)
        EM("v", "tensor_scalar", tmEs, tmBs, cb_ap[:, co + 0:co + 1], None, Alu.mult)
        EM("v", "tensor_tensor", tmDs, tmDs, tmEs, Alu.add)
        EM("v", "tensor_tensor", tmEs, tmCs, tmCs, Alu.mult)
        EM("v", "tensor_tensor", tmDs, tmDs, tmEs, Alu.mult)
        EM("v", "tensor_scalar", tmEs, tmAs, cb_ap[:, co + 1:co + 2], None, Alu.mult)
        EM("v", "tensor_tensor", tmEs, tmEs, stB[:, :, 10], Alu.subtract)
        EM("v", "tensor_tensor", tmEs, tmEs, tmCs, Alu.mult)
        EM("v", "scalar_tensor_tensor", tmDs, tmEs, -2.0, tmDs, Alu.mult, Alu.add)
        EM("v", "tensor_scalar", tmDs, tmDs, cb_ap[:, co + 2:co + 3], None, Alu.add)
        EM("v", "tensor_scalar", tmDs, tmDs, 1.0 / D, 1e-6, Alu.mult, Alu.add)
        EM("v", "reciprocal", tmDs, tmDs)
        S.wait("a", "v")
        EM("a", "activation", tmDs, tmDs, Act.Sqrt)
        S.wait("v", "a")
        for e in range(E):
            EM("v", "tensor_scalar", tmEs, tmAs, cb_ap[:, co + 8 + e:co + 9 + e],
                None, Alu.mult)
            EM("v", "tensor_tensor", tmEs, stB[:, :, e], tmEs, Alu.subtract)
            EM("v", "tensor_tensor", tmEs, tmEs, tmCs, Alu.mult)
            EM("v", "tensor_scalar", tmEs, tmEs, cb_ap[:, co + 16 + e:co + 17 + e],
                None, Alu.add)
            EM("v", "tensor_tensor", out3[:, :, e], tmEs, tmDs, Alu.mult)
    ckpt("p3logits")
    def soft_top2(p3, ng_, sel_dst):
        tm1 = tmA[:, 0:ng_]
        tm2 = tmB[:, 0:ng_]
        EM("v", "reduce_max", tm1, p3, axis=AX.X)
        EM("v", "tensor_tensor", p3, p3, tm1.unsqueeze(2).broadcast_to(
            [128, ng_, E]), Alu.subtract)
        S.wait("a", "v")
        EM("a", "activation", p3, p3, Act.Exp)
        S.wait("v", "a")
        EM("v", "reduce_sum", tm1, p3, axis=AX.X)
        EM("v", "reciprocal", tm1, tm1)
        EM("v", "tensor_tensor", p3, p3, tm1.unsqueeze(2).broadcast_to(
            [128, ng_, E]), Alu.mult)
        EM("v", "reduce_max", tm1, p3, axis=AX.X)
        EM("v", "tensor_tensor", sel_dst, p3, tm1.unsqueeze(2).broadcast_to(
            [128, ng_, E]), Alu.is_equal)
        EM("v", "scalar_tensor_tensor", sel_dst, sel_dst, -1e30, p3, Alu.mult,
            Alu.add)
        EM("v", "reduce_max", tm2, sel_dst, axis=AX.X)
        EM("v", "tensor_tensor", sel_dst, p3, tm2.unsqueeze(2).broadcast_to(
            [128, ng_, E]), Alu.is_ge)
    scrA = statsA[:, 0:NGA * E].rearrange("p (g e) -> p g e", e=E)
    soft_top2(pA3, NGA, scrA)
    EM("v", "tensor_copy", selAb[:], statsA[:, 0:NGA * E])
    sB3o = selB[:].rearrange("p (g e) -> p g e", e=E)
    soft_top2(pO3, NG, sB3o)
    S.wait("t", "v")
    S.wait("t", "a")
    mm(ps_misc[0:1, 0:512], ones_m1b[:], selAb[:, 0:512])
    mm(ps_stat[0:1, 0:512], ones_m1b[:], selAb[:, 512:1024])
    S.wait("a", "t")
    EM("a", "activation", cntrow[0:1, 0:512], ps_misc[0:1, 0:512], Act.Identity)
    EM("a", "activation", cntrow[0:1, 512:1024], ps_stat[0:1, 0:512], Act.Identity)
    S.wait("v", "a")
    EM("v", "tensor_tensor", 
        cntrow[0:1, :].rearrange("o (g e) -> o g e", e=E),
        cntrow[0:1, :].rearrange("o (g e) -> o g e", e=E),
        maskb[0:1, :].unsqueeze(2).broadcast_to([1, NGA, E]), Alu.mult)
    EM("v", "reduce_sum", offrow[0:1, :],
        cntrow[0:1, :].rearrange("o (g e) -> o e g", e=E), axis=AX.X)

    ckpt("p3")
    S.barrier()
    for half in range(2):
        g0 = half * 8
        t0 = half * HALF
        for s8 in range(8):
            xb_ = s8 % 4
            xstg = [tA[:, 0:1024], tB[:, 0:1024],
                    acc[:, 12288:13312], acc[:, 13312:14336]][xb_]
            dd = dma(xstg, x_own_d[t0 + s8 * 128: t0 + (s8 + 1) * 128, :],
                     gbuf=("tAx" if xb_ == 0 else f"p4x{xb_}"), pool="fast")
            S.wait_tok("v", dd)
            S.wait("v", "t")
            EM("v", "tensor_copy", paxb[:], xstg)
            gset("tAx" if xb_ == 0 else f"p4x{xb_}", ("v", S.cnt["v"]))
            S.wait("t", "v")
            S.wait("t", "a")
            pmb = ps_misc[:].bitcast(BF16)
            for ft in range(8):
                tr(pmb[:, ft * 128:(ft + 1) * 128],
                   paxb[:, ft * 128:(ft + 1) * 128], idbf[:])
            S.wait("a", "t")
            EM("a", "activation",
                xbh[:].rearrange("p (ft s c) -> p ft s c", s=8,
                                 c=128)[:, :, s8, :],
                pmb[:, 0:1024].rearrange("p (ft c) -> p ft c", c=128),
                Act.Identity)
        S.wait("v", "a")
        S.wait("v", "t")
        EM("a", "activation", h1h[:], xbh[:], Act.Square)
        for c in range(2):
            S.wait("t", "v")
            S.wait("t", "a")
            for ft in range(8):
                mm(ps_stat[0:1, 0:512], ones_m1b[:],
                   xbh[:, ft * HALF + c * 512: ft * HALF + (c + 1) * 512],
                   start=(ft == 0), stop=(ft == 7))
            for ft in range(8):
                mm(ps_misc[0:1, 0:512], ones_m1b[:],
                   h1h[:, ft * HALF + c * 512: ft * HALF + (c + 1) * 512],
                   start=(ft == 0), stop=(ft == 7))
            two_rows_to_tm(0, g0 + c * 4, 1, g0 + c * 4)
        S.wait("v", "a")
        EM("v", "tensor_scalar", tmcol(2, g0, 8), tmcol(0, g0, 8), 1.0 / D,
                                       None, Alu.mult)
        EM("v", "tensor_tensor", tmcol(3, g0, 8), tmcol(2, g0, 8),
                                       tmcol(2, g0, 8), Alu.mult)
        EM("v", "scalar_tensor_tensor", tmcol(1, g0, 8), tmcol(1, g0, 8),
                                              1.0 / D, tmcol(3, g0, 8), Alu.mult,
                                              Alu.subtract)
        EM("v", "tensor_scalar", tmcol(1, g0, 8), tmcol(1, g0, 8), 1e-5,
                                       None, Alu.add)
        EM("v", "reciprocal", tmcol(1, g0, 8), tmcol(1, g0, 8))
        S.wait("a", "v")
        EM("a", "activation", tmcol(1, g0, 8), tmcol(1, g0, 8), Act.Sqrt)
        S.wait("v", "a")
        EM("v", "tensor_tensor", tmcol(3, g0, 8), tmcol(2, g0, 8),
                                       tmcol(1, g0, 8), Alu.mult)
        ckpt(f"p4stats{half}")
        build_bcast(tmcol(1, g0, 8), Bu, 8)
        build_bcast(tmcol(3, g0, 8), Bv, 8)
        S.wait("v", "t")
        for ft in range(8):
            av = vecs[:, 8 + ft:9 + ft]
            bv = vecs[:, 0 + ft:1 + ft]
            EM("v", "tensor_tensor", 
                tC[:, 0:HALF], xbh[:, ft * HALF:(ft + 1) * HALF], Bu[:], Alu.mult)
            EM("v", "tensor_tensor", tC[:, 0:HALF], tC[:, 0:HALF], Bv[:],
                                           Alu.subtract)
            EM("v", "tensor_scalar", 
                h1h[:, ft * HALF:(ft + 1) * HALF], tC[:, 0:HALF], av, bv,
                Alu.mult, Alu.add)
        S.wait("a", "t")
        S.wait("a", "v")
        EM("a", "activation", xbh[:], h1h[:], Act.Square)
        for c in range(2):
            S.wait("t", "v")
            S.wait("t", "a")
            for ft in range(8):
                mm(ps_stat[0:1, 0:512], ones_m1b[:],
                   xbh[:, ft * HALF + c * 512: ft * HALF + (c + 1) * 512],
                   start=(ft == 0), stop=(ft == 7))
            S.wait("a", "t")
            EM("a", "activation", rowst[0:1, :], ps_stat[0:1, 0:512],
                                        Act.Identity)
            rows_to_tm(1, [(4, 0)], g0 + c * 4)
        S.wait("v", "a")
        maxtree_to_tm(xbh, HALF, 8, 5, g0, 8)
        EM("v", "tensor_scalar", tmcol(6, g0, 8), tmcol(4, g0, 8), 1.0 / D,
                                       1e-8, Alu.mult, Alu.add)
        EM("v", "reciprocal", tmcol(6, g0, 8), tmcol(6, g0, 8))
        S.wait("a", "v")
        EM("a", "activation", tmcol(6, g0, 8), tmcol(6, g0, 8), Act.Sqrt)
        EM("a", "activation", tmcol(5, g0, 8), tmcol(5, g0, 8), Act.Sqrt)
        S.wait("v", "a")
        EM("v", "tensor_tensor", tmcol(5, g0, 8), tmcol(5, g0, 8),
                                       tmcol(6, g0, 8), Alu.mult)
        EM("v", "tensor_scalar", tmcol(5, g0, 8), tmcol(5, g0, 8), 1e-5,
                                       None, Alu.max)
        EM("v", "reciprocal", tmcol(7, g0, 8), tmcol(5, g0, 8))
        EM("v", "tensor_scalar", tmcol(7, g0, 8), tmcol(7, g0, 8), 127.0,
                                       None, Alu.mult)
        EM("v", "tensor_tensor", tmcol(6, g0, 8), tmcol(6, g0, 8),
                                       tmcol(7, g0, 8), Alu.mult)
        EM("v", "reciprocal", tmcol(7, g0, 8), tmcol(7, g0, 8))
        build_bcast(tmcol(6, g0, 8), Bw, 8)
        for ft in range(8):
            EM("v", "tensor_tensor", 
                tA[:, 0:HALF], h1h[:, ft * HALF:(ft + 1) * HALF], Bw[:], Alu.mult)
            rnd(xq1[:, ft * T + t0: ft * T + t0 + HALF], tA[:, 0:HALF])
        gadd("tAx", ("v", S.cnt["v"]))
    ckpt("p4logits")
    S.wait("v", "a")
    EM("v", "tensor_tensor", wselB[:], wselB[:], selB[:], Alu.mult)
    EM("v", "tensor_copy", selBb[:], selB[:])
    S.wait("t", "v")
    S.wait("t", "a")
    mm(ps_misc[0:1, 0:128], ones_m1b[:], selBb[:])
    S.wait("a", "t")
    EM("a", "activation", offBrow[0:1, :], ps_misc[0:1, 0:128], Act.Identity)
    S.wait("v", "a")
    oB3 = offBrow[0:1, :].rearrange("o (g e) -> o e g", e=E)
    scanrow = trow[0:1, 256:256 + NG * E]
    cB3 = scanrow.rearrange("o (g e) -> o e g", e=E)
    for e in range(E):
        EM("v", "tensor_tensor_scan", cB3[:, e, :], oB3[:, e, :],
                                                 oB3[:, e, :], 0.0, Alu.add,
                                                 Alu.bypass)
    EM("v", "tensor_tensor", scanrow, scanrow, offBrow[0:1, :],
                                   Alu.subtract)
    EM("v", "tensor_tensor", 
        scanrow.rearrange("o (g e) -> o g e", e=E),
        scanrow.rearrange("o (g e) -> o g e", e=E),
        offrow[0:1, :].unsqueeze(1).broadcast_to([1, NG, E]), Alu.add)
    EM("v", "tensor_copy", offBb[0:1, :], scanrow)
    S.wait("t", "v")
    S.wait("t", "a")
    for g in range(NG):
        mm(ps_misc[0:128, g * 8:(g + 1) * 8], tribf[:], selBb[:, g * 8:(g + 1) * 8],
           start=(g == 0), stop=False)
    mm(ps_misc[0:128, 0:NG * E], ones_k1b[0:1, :], offBb[0:1, :], start=False,
       stop=True)
    S.wait("v", "t")
    EM("v", "tensor_scalar", cumsb[:], ps_misc[0:128, 0:NG * E], CAP, None,
                                   Alu.is_le)
    EM("v", "tensor_tensor", wkeep[:], cumsb[:], wselB[:], Alu.mult)
    EM("v", "tensor_copy", wkeepb[:], wkeep[:])
    for c in range(4):
        S.wait("t", "v")
        S.wait("t", "a")
        for s4 in range(4):
            g = c * 4 + s4
            tr(ps_stat[0:8, 0:512].bitcast(BF16)[:, s4 * 128:(s4 + 1) * 128],
               wkeepb[:, g * 8:(g + 1) * 8], idbf[:])
        S.wait("a", "t")
        EM("a", "activation", 
            wkT[:, c * 512:(c + 1) * 512],
            ps_stat[0:8, 0:512].bitcast(BF16)[:, 0:512], Act.Identity)
    build_bcast(tmcol(7, 0, 8), Bu, 8)
    build_bcast(tmcol(7, 8, 8), Bv, 8)
    S.wait("v", "a")
    EM("v", "tensor_copy", Bs1F[:, 0:HALF], Bu[:])
    EM("v", "tensor_copy", Bs1F[:, HALF:T], Bv[:])

    ckpt("p4")
    S.barrier()
    for e in range(9):
        gsrc = sgw_d if e == 0 else egw_d[e - 1]
        dsrc = sdw_d if e == 0 else edw_d[e - 1]
        mi_g = 0 if e == 0 else 2 + (e - 1)
        mi_d = 1 if e == 0 else 10 + (e - 1)
        for half in range(2):
            g0 = half * 8
            t0 = half * HALF
            def b1_load_quant(j):
                bi = j % 2
                if half == 0:
                    d1 = dma(wst[bi][:, 0:1024], gsrc[j], gbuf=f"wst{bi}")
                    d2 = dma(wst[bi][:, 1024:2048], gsrc[j + 16])
                    S.wait_tok("a", d1)
                    S.wait_tok("a", d2)
                    for tok in guard.get(f"wqs{bi}", []):
                        S.wait_tok("v", tok)
                    for tok in guard.get(f"wqb{bi}", []):
                        S.wait_tok("v", tok)
                    quant_strip(wst[bi], wqb[bi], mi_g, 2048)
                    gset(f"wst{bi}", ("v", S.cnt["v"]))
                    qd = S.cnt["v"]
                    S.wait("a", "v", qd)
                    gset(f"wqs{bi}", dma_w(scr_g_d[e, j], wqb[bi][:]))
                    return ("v", qd)
                d1 = dma(wqb[bi][:, 0:1024], scr_g_d[e, j][:, 0:1024],
                         gbuf=f"wqb{bi}")
                d2 = dma(wqb[bi][:, 1024:2048], scr_g_d[e, j][:, 1024:2048])
                return [d1, d2]

            def b1_mms(j, qtok):
                bi = j % 2
                r0 = (2 * j) % 3
                r1 = (2 * j + 1) % 3
                for tk in (qtok if isinstance(qtok, list) else [qtok]):
                    S.wait_tok("t", tk)
                for r in (r0, r1):
                    for tok in guard.get(f"psa{r}", []):
                        S.wait_tok("t", tok)
                    for tok in guard.get(f"psv{r}", []):
                        S.wait_tok("t", tok)
                for k in range(8):
                    for c in range(2):
                        mm(psR[r0][:, c * 512:(c + 1) * 512],
                           wqb[bi][:, k * 128:(k + 1) * 128],
                           xq1[:, k * T + t0 + c * 512: k * T + t0 + (c + 1) * 512],
                           start=(k == 0), stop=(k == 7))
                m0 = ("t", S.cnt["t"])
                for k in range(8):
                    for c in range(2):
                        mm(psR[r1][:, c * 512:(c + 1) * 512],
                           wqb[bi][:, 1024 + k * 128:1024 + (k + 1) * 128],
                           xq1[:, k * T + t0 + c * 512: k * T + t0 + (c + 1) * 512],
                           start=(k == 0), stop=(k == 7))
                gset(f"wqb{bi}", ("t", S.cnt["t"]))
                return (m0, ("t", S.cnt["t"]))

            def b1_evac(j, mtoks):
                r0 = (2 * j) % 3
                r1 = (2 * j + 1) % 3
                S.wait_tok("v", mtoks[0])
                EM("v", "scalar_tensor_tensor",
                    tA[:, 0:HALF], psR[r0][:], mwbc[:, mi_g:mi_g + 1],
                    Bs1F[:, t0:t0 + HALF], Alu.mult, Alu.mult)
                gset(f"psv{r0}", ("v", S.cnt["v"]))
                S.wait("a", "v")
                EM("a", "activation", tC[:, 0:HALF], tA[:, 0:HALF], Act.Silu)
                S.wait_tok("a", mtoks[1])
                EM("a", "activation", tB[:, 0:HALF], psR[r1][:], Act.Identity)
                gset(f"psa{r1}", ("a", S.cnt["a"]))
                S.wait("v", "a")
                EM("v", "tensor_tensor", zt[:, j * HALF:(j + 1) * HALF],
                                                    tC[:, 0:HALF], tB[:, 0:HALF],
                                                    Alu.mult)

            pend = None
            for j in range(17):
                if j < 16:
                    qtok = b1_load_quant(j)
                if pend is not None:
                    b1_evac(*pend)
                if j < 16:
                    pend = (j, b1_mms(j, qtok))
                else:
                    pend = None
            if e == 0 and half == 0:
                ckpt("p5b1")
            for j in range(8):
                S.wait("a", "t")
                S.wait("a", "v")
                EM("a", "activation",
                    tC[:, 0:HALF], zt[:, 2 * j * HALF:(2 * j + 1) * HALF],
                    Act.Square)
                EM("a", "activation",
                    tD[:, 0:HALF], zt[:, (2 * j + 1) * HALF:(2 * j + 2) * HALF],
                    Act.Square)
                S.wait("t", "a")
                mm(ps_stat[0:1, 0:512], ones_m1b[:], tC[:, 0:512],
                   start=(j == 0), stop=False)
                mm(ps_misc[0:1, 0:512], ones_m1b[:], tC[:, 512:1024],
                   start=(j == 0), stop=False)
                mm(ps_stat[0:1, 0:512], ones_m1b[:], tD[:, 0:512],
                   start=False, stop=(j == 7))
                mm(ps_misc[0:1, 0:512], ones_m1b[:], tD[:, 512:1024],
                   start=False, stop=(j == 7))
                S.wait("v", "a")
                S.wait("v", "t")
                EM("v", "tensor_tensor", tD[:, 0:HALF], tC[:, 0:HALF],
                                               tD[:, 0:HALF], Alu.max)
                if j == 0:
                    EM("v", "tensor_copy", tE_[:, 0:HALF], tD[:, 0:HALF])
                else:
                    EM("v", "tensor_tensor", tE_[:, 0:HALF], tE_[:, 0:HALF],
                                                   tD[:, 0:HALF], Alu.max)
            two_rows_to_tm(10, g0, 10, g0 + 4)
            S.wait("v", "a")
            maxtree_to_tm(tE_, HALF, 1, 11, g0, 8)
            EM("v", "tensor_scalar", tmcol(9, g0, 8), tmcol(7, g0, 8),
                                           mwbc[:, mi_g:mi_g + 1], None, Alu.mult)
            EM("v", "tensor_scalar", tmcol(12, g0, 8), tmcol(10, g0, 8),
                                           1.0 / IM, None, Alu.mult)
            EM("v", "tensor_tensor", tmcol(12, g0, 8), tmcol(12, g0, 8),
                                           tmcol(9, g0, 8), Alu.mult)
            EM("v", "tensor_tensor", tmcol(12, g0, 8), tmcol(12, g0, 8),
                                           tmcol(9, g0, 8), Alu.mult)
            EM("v", "tensor_scalar", tmcol(12, g0, 8), tmcol(12, g0, 8),
                                           1e-8, None, Alu.add)
            EM("v", "reciprocal", tmcol(12, g0, 8), tmcol(12, g0, 8))
            S.wait("a", "v")
            EM("a", "activation", tmcol(12, g0, 8), tmcol(12, g0, 8),
                                        Act.Sqrt)
            EM("a", "activation", tmcol(11, g0, 8), tmcol(11, g0, 8),
                                        Act.Sqrt)
            S.wait("v", "a")
            EM("v", "tensor_tensor", tmcol(11, g0, 8), tmcol(11, g0, 8),
                                           tmcol(9, g0, 8), Alu.mult)
            EM("v", "tensor_tensor", tmcol(11, g0, 8), tmcol(11, g0, 8),
                                           tmcol(12, g0, 8), Alu.mult)
            EM("v", "tensor_scalar", tmcol(11, g0, 8), tmcol(11, g0, 8),
                                           1e-5, None, Alu.max)
            EM("v", "reciprocal", tmcol(13, g0, 8), tmcol(11, g0, 8))
            EM("v", "tensor_scalar", tmcol(13, g0, 8), tmcol(13, g0, 8),
                                           127.0, None, Alu.mult)
            EM("v", "tensor_tensor", tmcol(12, g0, 8), tmcol(12, g0, 8),
                                           tmcol(13, g0, 8), Alu.mult)
            EM("v", "reciprocal", tmcol(13, g0, 8), tmcol(13, g0, 8))
            EM("v", "tensor_tensor", tmcol(12, g0, 8), tmcol(12, g0, 8),
                                           tmcol(9, g0, 8), Alu.mult)
            build_bcast(tmcol(12, g0, 8), Bw, 8)
            build_bcast(tmcol(13, g0, 8), Bx, 8)
            if e == 0:
                EM("v", "tensor_copy", tF[:, 0:HALF], Bx[:])
            else:
                S.wait("g", "t")
                EM("g", "memset", e8sel[:], 1.0)
                S.wait("g", "g")
                EM("g", "affine_select",
                    e8sel[:], e8sel[:], [[0, 128]], Alu.is_equal, 0.0,
                    base=-(e - 1), channel_multiplier=1)
                S.wait("t", "g")
                for c in range(2):
                    S.wait("t", "v")
                    S.wait("t", "a")
                    mm(ps_misc[0:128, 0:512], e8sel[:],
                       wkT[:, t0 + c * 512: t0 + (c + 1) * 512])
                    S.wait("v", "t")
                    EM("v", "tensor_tensor",
                        tF[:, c * 512:(c + 1) * 512], ps_misc[0:128, 0:512],
                        Bx[:, c * 512:(c + 1) * 512], Alu.mult)
            S.wait("v", "t")
            for j in range(16):
                EM("v", "tensor_tensor", tA[:, 0:HALF],
                                                    zt[:, j * HALF:(j + 1) * HALF],
                                                    Bw[:], Alu.mult)
                rnd(zt[:, j * HALF:(j + 1) * HALF], tA[:, 0:HALF])
            zq_tok = ("v", S.cnt["v"])
            def b2_load_quant(m):
                bi = m % 2
                if half == 0:
                    d1 = dma(wst[bi][:], dsrc[m], gbuf=f"wst{bi}")
                    S.wait_tok("a", d1)
                    for tok in guard.get(f"wqs{bi}", []):
                        S.wait_tok("v", tok)
                    for tok in guard.get(f"wqb{bi}", []):
                        S.wait_tok("v", tok)
                    quant_strip(wst[bi], wqb[bi], mi_d, 2048)
                    gset(f"wst{bi}", ("v", S.cnt["v"]))
                    qd = S.cnt["v"]
                    S.wait("a", "v", qd)
                    gset(f"wqs{bi}", dma_w(scr_d_d[e, m], wqb[bi][:]))
                    return ("v", qd)
                d1 = dma(wqb[bi][:, 0:1024], scr_d_d[e, m][:, 0:1024],
                         gbuf=f"wqb{bi}")
                d2 = dma(wqb[bi][:, 1024:2048], scr_d_d[e, m][:, 1024:2048])
                return [d1, d2]

            def b2_mms(m, qtok):
                bi = m % 2
                r0 = m % 3
                for tk in (qtok if isinstance(qtok, list) else [qtok]):
                    S.wait_tok("t", tk)
                S.wait_tok("t", zq_tok)
                for tok in guard.get(f"psa{r0}", []):
                    S.wait_tok("t", tok)
                for tok in guard.get(f"psv{r0}", []):
                    S.wait_tok("t", tok)
                for k in range(16):
                    for c in range(2):
                        mm(psR[r0][:, c * 512:(c + 1) * 512],
                           wqb[bi][:, k * 128:(k + 1) * 128],
                           zt[:, k * HALF + c * 512: k * HALF + (c + 1) * 512],
                           start=(k == 0), stop=(k == 15))
                gset(f"wqb{bi}", ("t", S.cnt["t"]))
                return ("t", S.cnt["t"])

            def b2_evac(m, mtok):
                r0 = m % 3
                S.wait_tok("v", mtok)
                EM("v", "scalar_tensor_tensor",
                    tA[:, 0:HALF], psR[r0][:], mwbc[:, mi_d:mi_d + 1],
                    tF[:, 0:HALF], Alu.mult, Alu.mult)
                gset(f"psv{r0}", ("v", S.cnt["v"]))
                if e == 0:
                    EM("v", "tensor_copy",
                        acc[:, m * T + t0: m * T + t0 + HALF],
                        tA[:, 0:HALF])
                else:
                    EM("v", "tensor_tensor",
                        acc[:, m * T + t0: m * T + t0 + HALF],
                        acc[:, m * T + t0: m * T + t0 + HALF],
                        tA[:, 0:HALF], Alu.add)

            pend = None
            for m in range(9):
                if m < 8:
                    qtok = b2_load_quant(m)
                if pend is not None:
                    b2_evac(*pend)
                if m < 8:
                    pend = (m, b2_mms(m, qtok))
                else:
                    pend = None
            if half == 1:
                ckpt(f"p5e{e}")

    ckpt("p5")
    S.barrier()
    xbh2_32 = xbh2[:].bitcast(F32)
    for half in range(2):
        t0 = half * HALF
        for s8 in range(8):
            xb_ = s8 % 4
            xstg = [tB[:, 0:1024], xbh2_32[:, 0:1024], xbh2_32[:, 1024:2048],
                    xbh2_32[:, 2048:3072]][xb_]
            dd = dma(xstg, x_own_d[t0 + s8 * 128: t0 + (s8 + 1) * 128, :],
                     gbuf=f"p6x{xb_}", pool="fast")
            S.wait_tok("t", dd)
            S.wait("t", "a")
            for ft in range(4):
                tr(ps_misc[:, ft * 128:(ft + 1) * 128],
                   xstg[:, ft * 128:(ft + 1) * 128], id128[:])
            S.wait("a", "t")
            EM("a", "activation", tA[:, 0:512], ps_misc[:, 0:512],
                                        Act.Identity)
            S.wait("t", "a")
            for ft in range(4):
                tr(ps_misc[:, ft * 128:(ft + 1) * 128],
                   xstg[:, 512 + ft * 128: 512 + (ft + 1) * 128], id128[:])
            gset(f"p6x{xb_}", ("t", S.cnt["t"]))
            S.wait("a", "t")
            EM("a", "activation", tA[:, 512:1024], ps_misc[:, 0:512],
                                        Act.Identity)
            S.wait("v", "a")
            for ft in range(8):
                av = vecs[:, 16 + ft:17 + ft]
                EM("v", "scalar_tensor_tensor", 
                    acc[:, ft * T + t0 + s8 * 128: ft * T + t0 + s8 * 128 + 128],
                    acc[:, ft * T + t0 + s8 * 128: ft * T + t0 + s8 * 128 + 128],
                    av, tA[:, ft * 128:(ft + 1) * 128], Alu.mult, Alu.add)

    ckpt("p6")
    for half in range(2):
        g0 = half * 8
        t0 = half * HALF
        S.wait("v", "t")
        for ft in range(8):
            EM("v", "tensor_copy", 
                xbh2[:, ft * HALF:(ft + 1) * HALF],
                acc[:, ft * T + t0: ft * T + t0 + HALF])
        S.wait("a", "v")
        S.wait("a", "t")
        EM("a", "activation", h1h2[:], xbh2[:], Act.Square)
        for c in range(2):
            S.wait("t", "v")
            S.wait("t", "a")
            for ft in range(8):
                mm(ps_stat[0:1, 0:512], ones_m1b[:],
                   xbh2[:, ft * HALF + c * 512: ft * HALF + (c + 1) * 512],
                   start=(ft == 0), stop=(ft == 7))
            for ft in range(8):
                mm(ps_misc[0:1, 0:512], ones_m1b[:],
                   h1h2[:, ft * HALF + c * 512: ft * HALF + (c + 1) * 512],
                   start=(ft == 0), stop=(ft == 7))
            two_rows_to_tm(14, g0 + c * 4, 15, g0 + c * 4)
        S.wait("v", "a")
        EM("v", "tensor_scalar", tmcol(16, g0, 8), tmcol(14, g0, 8), 1.0 / D,
                                       None, Alu.mult)
        EM("v", "tensor_tensor", tmcol(17, g0, 8), tmcol(16, g0, 8),
                                       tmcol(16, g0, 8), Alu.mult)
        EM("v", "scalar_tensor_tensor", tmcol(15, g0, 8), tmcol(15, g0, 8),
                                              1.0 / D, tmcol(17, g0, 8), Alu.mult,
                                              Alu.subtract)
        EM("v", "tensor_scalar", tmcol(15, g0, 8), tmcol(15, g0, 8), 1e-5,
                                       None, Alu.add)
        EM("v", "reciprocal", tmcol(15, g0, 8), tmcol(15, g0, 8))
        S.wait("a", "v")
        EM("a", "activation", tmcol(15, g0, 8), tmcol(15, g0, 8), Act.Sqrt)
        S.wait("v", "a")
        EM("v", "tensor_tensor", tmcol(16, g0, 8), tmcol(16, g0, 8),
                                       tmcol(15, g0, 8), Alu.mult)
        build_bcast(tmcol(15, g0, 8), Bu, 8)
        build_bcast(tmcol(16, g0, 8), Bv, 8)
        S.wait("v", "t")
        for ft in range(8):
            av = vecs[:, 32 + ft:33 + ft]
            bv = vecs[:, 24 + ft:25 + ft]
            EM("v", "tensor_tensor", 
                tC[:, 0:HALF], xbh2[:, ft * HALF:(ft + 1) * HALF], Bu[:], Alu.mult)
            EM("v", "tensor_tensor", tC[:, 0:HALF], tC[:, 0:HALF], Bv[:],
                                           Alu.subtract)
            EM("v", "tensor_scalar", 
                h1h2[:, ft * HALF:(ft + 1) * HALF], tC[:, 0:HALF], av, bv,
                Alu.mult, Alu.add)
        S.wait("a", "v")
        S.wait("a", "t")
        EM("a", "activation", xbh2[:], h1h2[:], Act.Square)
        S.wait("v", "a")
        maxtree_to_tm(xbh2, HALF, 8, 17, g0, 8)
        S.wait("a", "v")
        EM("a", "activation", tmcol(17, g0, 8), tmcol(17, g0, 8), Act.Sqrt)
        S.wait("v", "a")
        EM("v", "tensor_scalar", tmcol(17, g0, 8), tmcol(17, g0, 8), 1e-5,
                                       None, Alu.max)
        EM("v", "reciprocal", tmcol(18, g0, 8), tmcol(17, g0, 8))
        EM("v", "tensor_scalar", tmcol(18, g0, 8), tmcol(18, g0, 8), 127.0,
                                       None, Alu.mult)
        EM("v", "reciprocal", tmcol(19, g0, 8), tmcol(18, g0, 8))
        build_bcast(tmcol(18, g0, 8), Bw, 8)
        for ft in range(8):
            EM("v", "tensor_tensor", 
                tA[:, 0:HALF], h1h2[:, ft * HALF:(ft + 1) * HALF], Bw[:], Alu.mult)
            rnd(xq3[:, ft * T + t0: ft * T + t0 + HALF], tA[:, 0:HALF])
        build_bcast(tmcol(19, g0, 8), Bu, 8)
        EM("v", "tensor_copy", 
            Bs1F[:, half * HALF:(half + 1) * HALF], Bu[:])

    ckpt("p7")
    S.barrier()
    for q in range(4):
        t0 = q * QTR
        g0q = q * 4
        half = t0 // HALF
        hoff = t0 % HALF
        if q > 0:
            S.wait("a", "w")
        def w1_load_quant(m):
            bi = m % 2
            if q == 0:
                d1 = dma(wst[bi][:, 0:1024], w1_d[m], gbuf=f"wst{bi}")
                S.wait_tok("a", d1)
                for tok in guard.get(f"wqs{bi}", []):
                    S.wait_tok("v", tok)
                for tok in guard.get(f"wqb{bi}", []):
                    S.wait_tok("v", tok)
                quant_strip(wst[bi], wqb[bi], 18, 1024)
                gset(f"wst{bi}", ("v", S.cnt["v"]))
                qd = S.cnt["v"]
                S.wait("a", "v", qd)
                gset(f"wqs{bi}", dma_w(scr_w1_d[m], wqb[bi][:, 0:1024]))
                return ("v", qd)
            return dma(wqb[bi][:, 0:1024], scr_w1_d[m], gbuf=f"wqb{bi}")

        def w1_mms(m, qtok):
            bi = m % 2
            r0 = m % 3
            S.wait_tok("t", qtok)
            for tok in guard.get(f"psa{r0}", []):
                S.wait_tok("t", tok)
            for tok in guard.get(f"psv{r0}", []):
                S.wait_tok("t", tok)
            for k in range(8):
                mm(psR[r0][:, 0:512], wqb[bi][:, k * 128:(k + 1) * 128],
                   xq3[:, k * T + t0: k * T + t0 + QTR], start=(k == 0),
                   stop=(k == 7))
            gset(f"wqb{bi}", ("t", S.cnt["t"]))
            return ("t", S.cnt["t"])

        def w1_evac(m, mtok):
            r0 = m % 3
            S.wait_tok("v", mtok)
            EM("v", "scalar_tensor_tensor",
                tA[:, 0:QTR], psR[r0][:, 0:512], mwbc[:, 18:19],
                Bs1F[:, half * HALF + hoff: half * HALF + hoff + QTR],
                Alu.mult, Alu.mult)
            gset(f"psv{r0}", ("v", S.cnt["v"]))
            S.wait("a", "v")
            EM("a", "activation",
                hft[:, m * QTR:(m + 1) * QTR], tA[:, 0:QTR], Act.Gelu)

        pend = None
        for m in range(33):
            if m < 32:
                qtok = w1_load_quant(m)
            if pend is not None:
                w1_evac(*pend)
            if m < 32:
                pend = (m, w1_mms(m, qtok))
            else:
                pend = None
        S.wait("v", "a")
        for j in range(16):
            S.wait("a", "v")
            EM("a", "activation",
                tC[:, 0:QTR], hft[:, 2 * j * QTR:(2 * j + 1) * QTR],
                Act.Square)
            EM("a", "activation",
                tD[:, 0:QTR], hft[:, (2 * j + 1) * QTR:(2 * j + 2) * QTR],
                Act.Square)
            S.wait("v", "a")
            EM("v", "tensor_tensor", tD[:, 0:QTR], tC[:, 0:QTR],
                                           tD[:, 0:QTR], Alu.max)
            if j == 0:
                EM("v", "tensor_copy", tE_[:, 0:QTR], tD[:, 0:QTR])
            else:
                EM("v", "tensor_tensor", tE_[:, 0:QTR], tE_[:, 0:QTR],
                                               tD[:, 0:QTR], Alu.max)
        maxtree_to_tm(tE_, QTR, 1, 20, g0q, 4)
        S.wait("a", "v")
        EM("a", "activation", tmcol(20, g0q, 4), tmcol(20, g0q, 4), Act.Sqrt)
        S.wait("v", "a")
        EM("v", "tensor_scalar", tmcol(20, g0q, 4), tmcol(20, g0q, 4), 1e-5,
                                       None, Alu.max)
        EM("v", "reciprocal", tmcol(21, g0q, 4), tmcol(20, g0q, 4))
        EM("v", "tensor_scalar", tmcol(21, g0q, 4), tmcol(21, g0q, 4), 127.0,
                                       None, Alu.mult)
        EM("v", "reciprocal", tmcol(22, g0q, 4), tmcol(21, g0q, 4))
        build_bcast(tmcol(21, g0q, 4), Bw, 4)
        build_bcast(tmcol(22, g0q, 4), Bx, 4)
        for j in range(32):
            EM("v", "tensor_tensor", tA[:, 0:QTR],
                                                hft[:, j * QTR:(j + 1) * QTR],
                                                Bw[:, 0:QTR], Alu.mult)
            rnd(hft[:, j * QTR:(j + 1) * QTR], tA[:, 0:QTR])
        hq_tok = ("v", S.cnt["v"])

        def w2_load_quant(m):
            if q == 0:
                for hk in range(2):
                    d1 = dma(wst[hk][:],
                             w2_d[m][:, hk * 2048:(hk + 1) * 2048],
                             gbuf=f"wst{hk}")
                    S.wait_tok("a", d1)
                    for tok in guard.get("wq2s", []):
                        S.wait_tok("v", tok)
                    for tok in guard.get("wq20", []):
                        S.wait_tok("v", tok)
                    EM("a", "activation", wst[hk][:], wst[hk][:],
                                                Act.Identity,
                                                scale=rmwbc[:, 19:20],
                                                bias=magic_col[:, 0:1])
                    S.wait("v", "a")
                    EM("v", "tensor_scalar",
                        wst[hk][:], wst[hk][:], MAGIC, -1.0, Alu.subtract,
                        Alu.max)
                    EM("v", "tensor_scalar",
                        wq2[0][:, hk * 2048:(hk + 1) * 2048], wst[hk][:],
                        1.0, None, Alu.min)
                    gset(f"wst{hk}", ("v", S.cnt["v"]))
                qd = S.cnt["v"]
                S.wait("a", "v", qd)
                gset("wq2s", dma_w(scr_w2_d[m], wq2[0][:]))
                return ("v", qd)
            dd = [dma(wq2[0][:, i * 1024:(i + 1) * 1024],
                      scr_w2_d[m][:, i * 1024:(i + 1) * 1024],
                      gbuf=("wq20" if i == 0 else None)) for i in range(4)]
            return dd

        def w2_mms(m, qtok):
            r0 = m % 3
            for tk in (qtok if isinstance(qtok, list) else [qtok]):
                S.wait_tok("t", tk)
            S.wait_tok("t", hq_tok)
            for tok in guard.get(f"psa{r0}", []):
                S.wait_tok("t", tok)
            for tok in guard.get(f"psv{r0}", []):
                S.wait_tok("t", tok)
            for k in range(32):
                mm(psR[r0][:, 0:512], wq2[0][:, k * 128:(k + 1) * 128],
                   hft[:, k * QTR:(k + 1) * QTR], start=(k == 0), stop=(k == 31))
            gset("wq20", ("t", S.cnt["t"]))
            return ("t", S.cnt["t"])

        def w2_evac(m, mtok):
            r0 = m % 3
            S.wait_tok("v", mtok)
            EM("v", "scalar_tensor_tensor",
                tA[:, 0:QTR], psR[r0][:, 0:512], mwbc[:, 19:20], Bx[:, 0:QTR],
                Alu.mult, Alu.mult)
            gset(f"psv{r0}", ("v", S.cnt["v"]))
            gv = vecs[:, 40 + m:41 + m]
            EM("v", "scalar_tensor_tensor",
                tB[:, 0:QTR], tA[:, 0:QTR], gv,
                acc[:, m * T + t0: m * T + t0 + QTR], Alu.mult, Alu.add)
            obuf = tA[:, 512:1024] if m % 2 == 0 else tB[:, 512:1024]
            for tok in guard.get(f"ob{m % 2}", []):
                S.wait_tok("a", tok)
            for s4 in range(4):
                S.wait("t", "v")
                S.wait("t", "a")
                tr(ps_misc[:, 0:128], tB[:, s4 * 128:(s4 + 1) * 128], id128[:])
                S.wait("a", "t")
                EM("a", "activation",
                    obuf[:, s4 * 128:(s4 + 1) * 128],
                    ps_misc[:, 0:128], Act.Identity)
            wtok = dma_w(
                out_d[t0:t0 + QTR, m * 128:(m + 1) * 128]
                .rearrange("(s p) d -> p s d", p=128),
                obuf[:].rearrange("p (s d) -> p s d", d=128))
            gset(f"ob{m % 2}", wtok)

        pend = None
        for m in range(9):
            if m < 8:
                qtok = w2_load_quant(m)
            if pend is not None:
                w2_evac(*pend)
            if m < 8:
                pend = (m, w2_mms(m, qtok))
            else:
                pend = None

    S.stopped = False
    tap_srcs = {
        "t_vecs": (vecs, [128, 48]), "t_offrow": (offrow, [1, 8]),
        "t_wkeep": (wkeep, [128, 128]), "t_mwrow": (mwrow, [1, 32]),
        "t_rows": (rows_tm, [128, NG * NQ]), "t_acc": (acc, [128, FT * T]),
        "t_selB": (selB, [128, 128]), "t_wselB": (wselB, [128, 128]),
        "t_cum": (cumsb, [128, 128]),
        "t_xq1": (xq1, [128, FT * T]),
        "t_h1": (h1h, [128, FT * HALF]),
        "t_xbh": (xbh, [128, FT * HALF]),
        "t_cbc": (cbc, [128, 32]),
        "t_statsA": (statsA, [128, NGA * 13]),
        "t_statsB": (statsB, [128, NG * 13]),
        "t_probsA": (probsA, [128, NGA * E]),
        "t_cbco": (cbco, [128, 32]),
        "t_tmA": (tmA, [128, NGA]), "t_tmB": (tmB, [128, NGA]),
        "t_tmC": (tmC, [128, NGA]), "t_tmD": (tmD, [128, NGA]),
        "t_offB": (offBrow, [1, NG * E]),
        "t_zt": (zt, [128, IMT * HALF]),
        "t_pafa": (pafa, [128, FT * 512]),
        "t_past": (past, [16, 512]),
        "t_sxa": (sxa, [128, FT * 11]),
        "t_vecsd": (vecs, [128, 48]),
        "t_idbf": (idbf, [128, 128]),
        "t_id128": (id128, [128, 128]),
        "t_tB": (tB, [128, 1024]),
    }
    S.barrier(engs="ag")
    for name in dbg:
        tsrc, _shape = tap_srcs[name]
        shape = list(dbg[name].shape)
        ap = tsrc[0:shape[0], 0:shape[1]]
        S.cnt.setdefault("dbg", 0)
        S.mark("g", lambda E_, d=dbg[name], a=ap: E_.dma_start(d, a),
               sem_key="dbg", dma=True)
    if dbg:
        S.items.append(("a", lambda E_: E_.wait_ge(sems["dbg"], S.cnt["dbg"])))

    for i in range(Sched.NST):
        fw = S.cnt[f"w{i}"]
        if fw > 0:
            S.items.append(
                ("a", lambda E_, i=i, fw=fw: E_.wait_ge(sems[f"w{i}"], fw)))

    with es:
        with nc.Block() as block:
            @block.sync
            def _(sync):
                S.replay("s", sync)

            @block.tensor
            def _(t_):
                S.replay("t", t_)

            @block.vector
            def _(v):
                S.replay("v", v)

            @block.scalar
            def _(a):
                S.replay("a", a)

            @block.gpsimd
            def _(g):
                S.replay("g", g)

    return nc, list(dbg.keys())


def _strips(w, nstrip, nk):
    a = np.asarray(w, dtype=np.float32)
    lead = a.shape[:-2]
    a = a.reshape(lead + (nstrip, 128, nk, 128))
    ax = list(range(len(lead))) + [len(lead), len(lead) + 3, len(lead) + 2,
                                   len(lead) + 1]
    a = np.ascontiguousarray(a.transpose(ax))
    return a.reshape(lead + (nstrip, 128, nk * 128))


def _host_common(inputs):
    x = np.ascontiguousarray(np.asarray(inputs["L0_attn_x"], dtype=np.float32))
    xf = x.reshape(NT, D)
    common = {
        "x_all": xf,
        "cin": np.ascontiguousarray(np.asarray(inputs["L0_c"], dtype=np.float32)),
        "wmod": _strips(np.asarray(inputs["L0_wmod"],
                                   dtype=np.float32).T, 48, 8),
        "gwt": np.ascontiguousarray(np.asarray(inputs["L0_attn_gate_w"],
                                               dtype=np.float32).T),
        "gnwt": np.ascontiguousarray(np.asarray(inputs["L0_attn_gate_norm_w"],
                                                dtype=np.float32).reshape(D, 1)),
        "sgw_t": _strips(inputs["L0_attn_shared_gate_w"], 32, 8),
        "sdw_t": _strips(inputs["L0_attn_shared_down_w"], 8, 16),
        "egw_t": _strips(inputs["L0_attn_expert_gate_w"], 32, 8),
        "edw_t": _strips(inputs["L0_attn_expert_down_w"], 8, 16),
        "w1_t": _strips(inputs["L0_ffn_w1"], 32, 8),
        "w2_t": _strips(inputs["L0_ffn_w2"], 8, 32),
    }
    return common, xf


def kernel(**inputs):
    common, xf = _host_common(inputs)
    in_maps = []
    for core in range(NCORE):
        m = dict(common)
        m["x_own"] = np.ascontiguousarray(xf[core * T:(core + 1) * T])
        mk = np.zeros((1, NGA), dtype=np.float32)
        mk[0, : core * NG] = 1.0
        m["maskrow"] = mk
        bs = np.zeros((1, 2), dtype=np.float32)
        bs[0, core // 4] = 1.0
        m["bselrow"] = bs
        in_maps.append(m)

    nc, _ = build_nc()
    res = run_bass_kernel_spmd(nc, in_maps, core_ids=list(range(NCORE)))
    out = np.concatenate([np.asarray(res.results[i]["out"]) for i in range(NCORE)],
                         axis=0)
    return out.reshape(2, 8192, D)


# version 14 (final)
# speedup vs baseline: 1.0118x; 1.0044x over previous
import contextlib
import numpy as np

import concourse.bass as bass
import concourse.mybir as mybir
from concourse.bass_utils import run_bass_kernel_spmd

F32 = mybir.dt.float32
BF16 = mybir.dt.bfloat16
Alu = mybir.AluOpType
Act = mybir.ActivationFunctionType
AX = mybir.AxisListType

MAGIC = 12582912.0
NCORE = 8
T = 2048
NT = 16384
D = 1024
FT = 8
IM = 2048
IMT = 16
CAP = 2560.0
E = 8
HALF = 1024
QTR = 512
NG = 16
NGA = 128
NQ = 24

MEAN_N = {i: 4096 * 1024 for i in range(20)}
MEAN_N[1] = 2048 * 1024
for _e in range(8):
    MEAN_N[10 + _e] = 2048 * 1024


class Sched:
    NDMA = 16
    NST = 4

    def __init__(self, sems):
        self.stopped = False
        self.items = []
        self.cnt = {e: 0 for e in "tvag"}
        for i in range(self.NDMA):
            self.cnt[f"d{i}"] = 0
        for i in range(self.NST):
            self.cnt[f"w{i}"] = 0
        self.sem = sems
        self.dseq = 0
        self.dseq_f = 0
        self.dseq_b = 0
        self.wseq = 0
        self.seen = {e: {} for e in "stvag"}

    def mark(self, eng, builder, sem_key=None, dma=False):
        if self.stopped:
            return (sem_key or eng, self.cnt[sem_key or eng])
        se = sem_key or eng
        n = 16 if dma else 1
        self.cnt[se] += n
        sem = self.sem

        def fn(E, b=builder, s=se, k=n):
            b(E).then_inc(sem[s], k)
        self.items.append((eng, fn))
        return (se, self.cnt[se])

    def wait_tok(self, eng, tok):
        if self.stopped:
            return
        key, c = tok
        if c <= 0 or self.seen[eng].get(key, 0) >= c:
            return
        self.seen[eng][key] = c
        sem = self.sem
        self.items.append((eng, lambda E, p=key, cc=c: E.wait_ge(sem[p], cc)))

    def wait(self, eng, prod, count=None):
        if prod == "s":
            for i in range(self.NDMA):
                self.wait_tok(eng, (f"d{i}", self.cnt[f"d{i}"]))
            return
        if prod == "w":
            for i in range(self.NST):
                self.wait_tok(eng, (f"w{i}", self.cnt[f"w{i}"]))
            return
        c = self.cnt[prod] if count is None else count
        self.wait_tok(eng, (prod, c))

    def dma_sem(self, pool="any"):
        if pool == "fast":
            key = f"d{self.dseq_f % 4}"
            self.dseq_f += 1
        elif pool == "bulk":
            key = f"d{4 + self.dseq_b % 12}"
            self.dseq_b += 1
        else:
            key = f"d{self.dseq % self.NDMA}"
            self.dseq += 1
        return key

    def st_sem(self):
        key = f"w{self.wseq % self.NST}"
        self.wseq += 1
        return key

    def barrier(self, engs="stvag", prods="stvag"):
        for e in engs:
            for p in prods:
                if p == e:
                    continue
                if p == "s":
                    continue
                self.wait(e, p)
        for e in engs:
            self.wait(e, "s")
            self.wait(e, "w")

    def replay(self, eng, engobj):
        for e, fn in self.items:
            if e == eng:
                fn(engobj)


def build_nc(taps=(), stop_at=None):
    nc = bass.Bass("TRN2", target_bir_lowering=False, detect_race_conditions=False)

    def din(name, shape, dt=F32):
        return nc.dram_tensor(name, shape, dt, kind="ExternalInput").ap()

    x_own_d = din("x_own", [T, D])
    x_all_d = din("x_all", [NT, D])
    maskrow_d = din("maskrow", [1, NGA])
    bsel_d = din("bselrow", [1, 2])
    cin_d = din("cin", [2, D])
    wmod_d = din("wmod", [48, 128, 1024])
    gwt_d = din("gwt", [D, E])
    gnwt_d = din("gnwt", [D, 1])
    sgw_d = din("sgw_t", [32, 128, 1024])
    sdw_d = din("sdw_t", [8, 128, 2048])
    egw_d = din("egw_t", [E, 32, 128, 1024])
    edw_d = din("edw_t", [E, 8, 128, 2048])
    w1_d = din("w1_t", [32, 128, 1024])
    w2_d = din("w2_t", [8, 128, 4096])

    out_d = nc.dram_tensor("out", [T, D], F32, kind="ExternalOutput").ap()
    dbg = {}
    for name, shape in taps:
        dbg[name] = nc.dram_tensor(name, shape, F32, kind="ExternalOutput").ap()

    scr_g_d = nc.dram_tensor("scr_g", [9, 16, 128, 2048], BF16, kind="Internal").ap()
    scr_d_d = nc.dram_tensor("scr_d", [9, 8, 128, 2048], BF16, kind="Internal").ap()
    scr_w1_d = nc.dram_tensor("scr_w1", [32, 128, 1024], BF16, kind="Internal").ap()
    scr_w2_d = nc.dram_tensor("scr_w2", [8, 128, 4096], BF16, kind="Internal").ap()

    es = contextlib.ExitStack()

    def sb(n, shape, d=F32):
        return es.enter_context(nc.sbuf_tensor(n, shape, d))

    def psb(n, shape, d=F32):
        return es.enter_context(nc.psum_tensor(n, shape, d))

    id128 = sb("id128", [128, 128])
    idbf = sb("idbf", [128, 128], BF16)
    tribf = sb("tribf", [128, 128], BF16)
    onesP = sb("onesP", [128, 128], BF16)
    ones_m1b = sb("ones_m1b", [128, 1], BF16)
    ones_k1b = sb("ones_k1b", [1, 128], BF16)
    e8sel = sb("e8sel", [8, 128], BF16)
    vecs = sb("vecs", [128, 48])
    mod_sb = sb("mod_sb", [128, 96])
    csT = sb("csT", [128, 16])
    bsel_bc = sb("bsel_bc", [128, 2])
    bsb = sb("bsb", [1, 2], BF16)
    gw2 = sb("gw2", [128, FT * E], BF16)
    sxa = sb("sxa", [128, 2 * FT * 11], BF16)
    sx2 = sb("sx2", [128, 2 * FT * 2], BF16)
    crow = sb("crow", [1, 8])
    agrow = sb("agrow", [1, E])
    c4row = sb("c4row", [1, E])
    cb_src = sb("cb_src", [1, 32], BF16)
    cbc = sb("cbc", [128, 64])
    offrow = sb("offrow", [1, E])
    maskb = sb("maskb", [1, NGA], BF16)
    rows_tm = sb("rows_tm", [128, NG * NQ])
    magic_col = sb("magicc", [128, 1])
    rowst = sb("rowst", [16, 512])
    past = sb("past", [16, 512])
    diagt = sb("diagt", [128, 128], BF16)
    mwcol = sb("mwcol", [128, 32])
    mwrow = sb("mwrow", [1, 32])
    mwc32 = sb("mwc32", [32, 1])
    mwbc = sb("mwbc", [128, 32])
    rmwbc = sb("rmwbc", [128, 32])
    cntinv = sb("cntinv", [1, 32])
    mwrow_b = sb("mwrow_b", [128, 32], BF16)
    xq1 = sb("xq1", [128, FT * T], BF16)
    acc = sb("acc", [128, FT * T])
    wkT = sb("wkT", [8, T], BF16)
    selB = sb("selB", [128, NG * E])
    selBb = sb("selBb", [128, NG * E], BF16)
    wselB = sb("wselB", [128, NG * E])
    wkeep = sb("wkeep", [128, NG * E])
    wkeepb = sb("wkeepb", [128, NG * E], BF16)
    offBrow = sb("offBrow", [1, NG * E])
    offBb = sb("offBb", [1, NG * E], BF16)
    cumsb = sb("cumsb", [128, NG * E])
    Bs1F = sb("Bs1F", [128, T], BF16)
    tA = sb("tA", [128, 1024])
    tB = sb("tB", [128, 1024])
    tC = sb("tC", [128, 1024], BF16)
    tD = sb("tD", [128, 1024], BF16)
    paxb = tD
    tE_ = sb("tE_", [128, 1024], BF16)
    tF = sb("tF", [128, 1024], BF16)
    trow = sb("trow", [1, 512])
    wst = [sb("wstA", [128, 2048]), sb("wstB", [128, 2048])]
    wqb = [sb("wqbA", [128, 2048], BF16), sb("wqbB", [128, 2048], BF16)]
    Bu = sb("Bu", [128, HALF], BF16)
    Bv = sb("Bv", [128, HALF], BF16)
    Bw = Bu
    Bx = Bv
    xq3 = xq1
    es1 = contextlib.ExitStack()

    def sb1(n, shape, d=F32):
        return es1.enter_context(nc.sbuf_tensor(n, shape, d))
    pafa = sb1("pafa", [128, FT * 512], BF16)
    paf2 = sb1("paf2", [128, 512], BF16)
    paflo = sb1("paflo", [128, FT * 512], BF16)
    statsB = sb1("statsB", [128, NG * 13])
    sxo = sb1("sxo", [128, FT * 11], BF16)
    sx2o = sb1("sx2o", [128, FT * 2], BF16)
    cbco = sb1("cbco", [128, 32])
    statsA = sb1("statsA", [128, NGA * 13], BF16)
    probsA = sb1("probsA", [128, NGA * E])
    selAb = sb1("selAb", [128, NGA * E], BF16)
    cntrow = sb1("cntrow", [1, NGA * E], BF16)
    tmA = sb1("tmA", [128, NGA])
    tmB = sb1("tmB", [128, NGA])
    tmC = sb1("tmC", [128, NGA])
    tmD = sb1("tmD", [128, NGA])
    tmE = sb1("tmE", [128, NGA])
    es1.close()
    es2 = contextlib.ExitStack()
    xbh = es2.enter_context(nc.sbuf_tensor("xbh", [128, FT * HALF], BF16))
    h1h = es2.enter_context(nc.sbuf_tensor("h1h", [128, FT * HALF], BF16))
    es2.close()
    es3 = contextlib.ExitStack()
    zt = es3.enter_context(nc.sbuf_tensor("zt", [128, IMT * HALF], BF16))
    es3.close()
    es4 = contextlib.ExitStack()
    xbh2 = es4.enter_context(nc.sbuf_tensor("xbh2", [128, FT * HALF], BF16))
    h1h2 = es4.enter_context(nc.sbuf_tensor("h1h2", [128, FT * HALF], BF16))
    es4.close()
    _wq20 = es.enter_context(nc.sbuf_tensor("wq20", [128, 4096], BF16))
    wq2 = [_wq20, _wq20]
    hft = es.enter_context(nc.sbuf_tensor("hft", [128, 32 * QTR], BF16))

    psR = [psb(f"psR{i}", [128, 1024]) for i in range(3)]
    ps_stat = psb("ps_stat", [16, 512])
    ps_misc = psb("ps_misc", [128, 512])

    sems = {}
    for key, nm in [("t", "Stn"), ("v", "Sve"), ("a", "Sac"), ("g", "Sgp")]:
        sems[key] = es.enter_context(nc.semaphore(nm))
    for i in range(Sched.NDMA):
        sems[f"d{i}"] = es.enter_context(nc.semaphore(f"Sd{i}"))
    for i in range(Sched.NST):
        sems[f"w{i}"] = es.enter_context(nc.semaphore(f"Sw{i}"))
    sems["dbg"] = es.enter_context(nc.semaphore("Sdbg"))

    S = Sched(sems)
    S.v_need_drain = False

    def ckpt(name):
        if stop_at is not None and name == stop_at:
            S.stopped = True
    guard = {}

    def dguard(buf):
        for tok in guard.get(buf, []):
            S.wait_tok("s", tok)

    def gset(buf, *toks):
        guard[buf] = list(toks)

    def gadd(buf, tok):
        guard.setdefault(buf, []).append(tok)

    def dma(dst, src, gbuf=None, slow=False, pool="any"):
        if gbuf is not None:
            dguard(gbuf)
        key = S.dma_sem(pool)
        S.wait_tok("s", (key, S.cnt[key]))
        return S.mark("s", lambda E_: E_.dma_start(
            dst, src, allow_slow_non_contiguous=slow), sem_key=key, dma=True)

    def dma_w(dst, src):
        key = S.st_sem()
        S.wait_tok("a", (key, S.cnt[key]))
        return S.mark("a", lambda E_: E_.dma_start(dst, src), sem_key=key, dma=True)

    def mm(out, lhsT, rhs, start=True, stop=True):
        return S.mark("t", lambda E_: E_.matmul(out, lhsT, rhs, start=start, stop=stop))

    def tr(out, in_, ident):
        return S.mark("t", lambda E_: E_.transpose(out, in_, ident))

    def v_(b):
        return S.mark("v", b)

    def a_(b):
        return S.mark("a", b)

    def g_(b):
        S.wait("g", "g")
        return S.mark("g", b)

    def _free_elems(ap):
        try:
            shp = list(ap.shape)
        except Exception:
            return 1 << 30
        n = 1
        for s in shp[1:]:
            n *= s
        return n

    def EM(eng, method, *args, **kwargs):
        if eng == "g":
            S.wait("g", "g")
        if eng == "v":
            if S.v_need_drain:
                S.wait("v", "v")
                S.v_need_drain = False
            if _free_elems(args[0]) < 128:
                S.v_need_drain = True

        def fn(E_, m=method, a=args, k=kwargs):
            return getattr(E_, m)(*a, **k)
        return S.mark(eng, fn)

    def rnd(out, in_):
        return EM("v", "tensor_scalar", out, in_, MAGIC, MAGIC, Alu.add,
                                              Alu.subtract)

    def tmcol(q, g0=0, ng=NG):
        return rows_tm[:, q * NG + g0: q * NG + g0 + ng]

    tmview = rows_tm[:].rearrange("p (q g) -> p q g", g=NG)

    def build_bcast(tmslice, dst, ng):
        for g in range(ng):
            S.wait("v", "t")
            EM("v", "tensor_scalar", diagt[:], idbf[:],
                                                tmslice[:, g:g + 1], None, Alu.mult)
            S.wait("t", "v")
            S.wait("t", "a")
            mm(ps_misc[:, 0:128], onesP[:], diagt[:])
            S.wait("a", "t")
            EM("a", "activation", dst[:, g * 128:(g + 1) * 128],
                                             ps_misc[:, 0:128], Act.Identity)
        S.wait("v", "a")

    def rows_to_tm(k, qglist, g0):
        for s in range(4):
            S.wait("t", "a")
            S.wait("t", "v")
            tr(ps_misc[0:128, 0:k], rowst[0:k, s * 128:(s + 1) * 128],
               id128[0:k, 0:k])
            S.wait("a", "t")
            for r, (q, gd) in enumerate(qglist):
                EM("a", "activation", 
                    tmview[:, q, g0 + gd + s: g0 + gd + s + 1],
                    ps_misc[0:128, r:r + 1], Act.Identity)

    def two_rows_to_tm(qa, g0a, qb, g0b):
        S.wait("a", "t")
        EM("a", "activation", rowst[0:1, :], ps_stat[0:1, 0:512],
                                    Act.Identity)
        EM("a", "activation", past[0:1, :], ps_misc[0:1, 0:512],
                                    Act.Identity)
        rows_to_tm(1, [(qa, 0)], g0a)
        S.wait("v", "a")
        EM("v", "tensor_copy", rowst[0:1, 0:512], past[0:1, 0:512])
        S.wait("a", "v")
        rows_to_tm(1, [(qb, 0)], g0b)

    def quant_strip(wstrip, wq_out, mi, ncol):
        EM("a", "activation", wstrip[:, 0:ncol], wstrip[:, 0:ncol],
                                    Act.Identity, scale=rmwbc[:, mi:mi + 1],
                                    bias=magic_col[:, 0:1])
        S.wait("v", "a")
        EM("v", "tensor_scalar", wstrip[:, 0:ncol], wstrip[:, 0:ncol],
                                       MAGIC, -1.0, Alu.subtract, Alu.max)
        EM("v", "tensor_scalar", wq_out[:, 0:ncol], wstrip[:, 0:ncol],
                                       1.0, None, Alu.min)

    def maxtree_to_tm(buf, width, ntile, qdst, g0, ng):
        n = ntile // 2
        while n >= 1:
            for j in range(n):
                EM("v", "tensor_tensor",
                    buf[:, j * width:(j + 1) * width],
                    buf[:, j * width:(j + 1) * width],
                    buf[:, (j + n) * width:(j + n + 1) * width], Alu.max)
            n //= 2
        S.wait("v", "t")
        EM("v", "tensor_copy", tC[0:64, 0:width], buf[64:128, 0:width])
        EM("v", "tensor_tensor", buf[0:64, 0:width], buf[0:64, 0:width],
                                       tC[0:64, 0:width], Alu.max)
        EM("v", "tensor_copy", tC[0:32, 0:width], buf[32:64, 0:width])
        EM("v", "tensor_tensor", buf[0:32, 0:width], buf[0:32, 0:width],
                                       tC[0:32, 0:width], Alu.max)
        nsub = width // 128
        for s in range(nsub):
            S.wait("t", "v")
            S.wait("t", "a")
            tr(ps_misc[:, 0:128].bitcast(BF16)[:, 0:32],
               buf[0:32, s * 128:(s + 1) * 128], idbf[0:32, 0:32])
            S.wait("a", "t")
            EM("a", "activation", tA[:, s * 32:(s + 1) * 32],
                                             ps_misc[:, 0:128].bitcast(BF16)[:, 0:32],
                                             Act.Identity)
        S.wait("v", "a")
        gadd("tAx", ("a", S.cnt["a"]))
        EM("v", "reduce_max", 
            tmcol(qdst, g0, ng),
            tA[:, 0:nsub * 32].rearrange("p (g k) -> p g k", k=32), axis=AX.X)

    EM("g", "memset", id128[:], 1.0)
    S.wait("g", "g")
    EM("g", "affine_select", id128[:], id128[:], [[1, 128]], Alu.is_equal,
                                   0.0, base=0, channel_multiplier=-1)
    EM("g", "memset", tribf[:], 1.0)
    S.wait("g", "g")
    EM("g", "affine_select", tribf[:], tribf[:], [[1, 128]], Alu.is_ge,
                                   0.0, base=0, channel_multiplier=-1)
    EM("g", "memset", onesP[:], 1.0)
    EM("g", "memset", ones_m1b[:], 1.0)
    EM("g", "memset", ones_k1b[:], 1.0)
    EM("g", "memset", mwcol[:], 0.0)
    EM("g", "memset", magic_col[:], MAGIC)

    EM("g", "memset", cntinv[:], 0.0)
    for mi, n in MEAN_N.items():
        EM("g", "memset", cntinv[0:1, mi:mi + 1], 1.0 / n)
    if taps:
        _tap_init = {
            "t_pafa": pafa, "t_past": past, "t_rows": rows_tm, "t_statsA": statsA,
            "t_zt": zt, "t_h1": h1h, "t_xbh": xbh, "t_xq1": xq1, "t_acc": acc,
            "t_wkeep": wkeep, "t_selB": selB, "t_wselB": wselB, "t_cum": cumsb,
        }
        for nm, _shape in taps:
            if nm in _tap_init:
                EM("g", "memset", _tap_init[nm][:], 0.0)
    S.wait("v", "g")
    EM("v", "tensor_copy", idbf[:], id128[:])

    d_c = dma(tA[0:2, 0:1024], cin_d[:])
    d_b = dma(trow[0:1, 0:2], bsel_d[:])
    d_m = dma(trow[0:1, 16:16 + NGA], maskrow_d[:])
    S.wait("a", "s")
    EM("a", "activation", tA[0:2, 0:1024], tA[0:2, 0:1024], Act.Silu)
    for k in range(8):
        S.wait("t", "a")
        tr(ps_misc[:, 0:2], tA[0:2, k * 128:(k + 1) * 128], id128[0:2, 0:2])
        S.wait("a", "t")
        EM("a", "activation", csT[:, 2 * k:2 * k + 2], ps_misc[:, 0:2],
                                         Act.Identity)
    S.wait("v", "s")
    EM("v", "tensor_copy", maskb[:], trow[0:1, 16:16 + NGA])
    for m in range(48):
        dd = dma(wst[m % 2][:, 0:1024], wmod_d[m], gbuf=f"mst{m % 2}")
        S.wait("t", "s")
        S.wait("t", "a")
        for k in range(8):
            mm(ps_misc[:, 0:2], wst[m % 2][:, k * 128:(k + 1) * 128],
               csT[:, 2 * k:2 * k + 2], start=(k == 0), stop=(k == 7))
        gset(f"mst{m % 2}", ("t", S.cnt["t"]))
        S.wait("a", "t")
        EM("a", "activation", mod_sb[:, 2 * m:2 * m + 2], ps_misc[:, 0:2],
                                         Act.Identity)
    S.wait("v", "s")
    EM("v", "tensor_copy", bsb[:], trow[0:1, 0:2])
    S.wait("t", "v")
    S.wait("t", "a")
    mm(ps_misc[:, 0:2], ones_k1b[0:1, :], bsb[:])
    S.wait("a", "t")
    EM("a", "activation", bsel_bc[:], ps_misc[:, 0:2], Act.Identity)
    S.wait("v", "a")
    mod3 = mod_sb[:].rearrange("p (m two) -> p m two", two=2)
    EM("v", "tensor_scalar", tA[:, 0:48], mod3[:, :, 1], bsel_bc[:, 1:2],
                                   None, Alu.mult)
    EM("v", "scalar_tensor_tensor", vecs[:], mod3[:, :, 0], bsel_bc[:, 0:1],
                                          tA[:, 0:48], Alu.mult, Alu.add)
    EM("v", "tensor_scalar", vecs[:, 8:16], vecs[:, 8:16], 1.0, None, Alu.add)
    EM("v", "tensor_scalar", vecs[:, 32:40], vecs[:, 32:40], 1.0, None,
                                   Alu.add)
    for c0 in (16, 40):
        S.wait("a", "v")
        EM("a", "activation", tA[:, 0:8], vecs[:, c0:c0 + 8], Act.Sign)
        EM("a", "activation", tA[:, 8:16], vecs[:, c0:c0 + 8], Act.Abs)
        S.wait("v", "a")
        EM("v", "tensor_scalar", tA[:, 8:16], tA[:, 8:16], 0.25, None, Alu.max)
        EM("v", "tensor_tensor", vecs[:, c0:c0 + 8], tA[:, 0:8],
                                              tA[:, 8:16], Alu.mult)
    S.wait("s", "v")
    S.wait("s", "a")
    d_gw = dma(tA[:, 0:64].rearrange("p (ft e) -> p ft e", e=8),
               gwt_d[:].rearrange("(ft p) e -> p ft e", p=128))
    d_gn = dma(tA[:, 64:72].rearrange("p (ft o) -> p ft o", o=1),
               gnwt_d[:].rearrange("(ft p) o -> p ft o", p=128), slow=True)
    S.wait("v", "s")
    for ft in range(8):
        EM("v", "tensor_scalar", gw2[:, ft * 8:ft * 8 + 8],
                                              tA[:, ft * 8:ft * 8 + 8],
                                              tA[:, 64 + ft:65 + ft], None, Alu.mult)
    for b in range(3):
        if b < 2:
            sxT, sxoff = sxa, b * 88
            sx2T, sx2off = sx2, b * 16
            cbcT, cbcoff = cbc, b * 32
        else:
            sxT, sxoff = sxo, 0
            sx2T, sx2off = sx2o, 0
            cbcT, cbcoff = cbco, 0
        for ft in range(8):
            if b < 2:
                av = tB[:, 40 + ft:41 + ft]
                EM("v", "tensor_scalar", av, mod3[:, 8 + ft, b:b + 1], 1.0, None,
                   Alu.add)
                bv = mod3[:, 0 + ft, b:b + 1]
            else:
                av = vecs[:, 8 + ft:9 + ft]
                bv = vecs[:, 0 + ft:1 + ft]
            EM("v", "tensor_scalar", 
                sxT[:, sxoff + ft * 11:sxoff + ft * 11 + 8],
                gw2[:, ft * 8:ft * 8 + 8], av, None, Alu.mult)
            EM("v", "memset", sxT[:, sxoff + ft * 11 + 8:sxoff + ft * 11 + 9], 1.0)
            EM("v", "tensor_tensor", 
                sxT[:, sxoff + ft * 11 + 9:sxoff + ft * 11 + 10], av, av, Alu.mult)
            EM("v", "tensor_tensor", 
                sxT[:, sxoff + ft * 11 + 10:sxoff + ft * 11 + 11], av, bv, Alu.mult)
            EM("v", "memset", sx2T[:, sx2off + ft * 2:sx2off + ft * 2 + 1], 1.0)
            EM("v", "tensor_copy", 
                sx2T[:, sx2off + ft * 2 + 1:sx2off + ft * 2 + 2],
                sxT[:, sxoff + ft * 11 + 9:sxoff + ft * 11 + 10])
        for ft in range(8):
            S.wait("v", "t")
            EM("v", "tensor_copy", tC[:, 0:1],
                sxT[:, sxoff + ft * 11 + 9:sxoff + ft * 11 + 10])
            EM("v", "tensor_copy", tC[:, 1:2],
                sxT[:, sxoff + ft * 11 + 10:sxoff + ft * 11 + 11])
            bv = (mod3[:, 0 + ft, b:b + 1] if b < 2 else vecs[:, 0 + ft:1 + ft])
            EM("v", "tensor_tensor", tA[:, 0:1], bv, bv, Alu.mult)
            EM("v", "tensor_copy", tC[:, 2:3], tA[:, 0:1])
            S.wait("t", "v")
            mm(ps_stat[0:3, 0:1], tC[:, 0:3], ones_m1b[:], start=(ft == 0),
               stop=(ft == 7))
        for ft in range(8):
            mm(ps_stat[0:8, 1:2], sxT[:, sxoff + ft * 11:sxoff + ft * 11 + 8],
               ones_m1b[:], start=(ft == 0), stop=(ft == 7))
        for ft in range(8):
            bv = (mod3[:, 0 + ft, b:b + 1] if b < 2 else vecs[:, 0 + ft:1 + ft])
            S.wait("v", "t")
            EM("v", "tensor_scalar",
                tC[:, 8:16], gw2[:, ft * 8:ft * 8 + 8], bv, None, Alu.mult)
            S.wait("t", "v")
            mm(ps_stat[0:8, 2:3], tC[:, 8:16], ones_m1b[:], start=(ft == 0),
               stop=(ft == 7))
        S.wait("a", "t")
        EM("a", "activation", rowst[0:3, 0:1], ps_stat[0:3, 0:1], Act.Identity)
        EM("a", "activation", rowst[0:8, 1:2], ps_stat[0:8, 1:2], Act.Identity)
        EM("a", "activation", rowst[0:8, 2:3], ps_stat[0:8, 2:3], Act.Identity)
        for col, nr, dst in ((0, 3, crow), (1, 8, agrow), (2, 8, c4row)):
            S.wait("t", "a")
            tr(ps_misc[0:1, 0:nr], rowst[0:nr, col:col + 1], id128[0:nr, 0:nr])
            S.wait("a", "t")
            EM("a", "activation", dst[0:1, 0:nr], ps_misc[0:1, 0:nr], Act.Identity)
        S.wait("v", "a")
        EM("v", "memset", cb_src[:], 0.0)
        EM("v", "tensor_copy", cb_src[0:1, 0:3], crow[0:1, 0:3])
        EM("v", "tensor_copy", cb_src[0:1, 8:16], agrow[0:1, :])
        EM("v", "tensor_copy", cb_src[0:1, 16:24], c4row[0:1, :])
        S.wait("t", "v")
        S.wait("t", "a")
        mm(ps_misc[:, 0:32], ones_k1b[0:1, :], cb_src[0:1, :])
        S.wait("a", "t")
        EM("a", "activation", cbcT[:, cbcoff:cbcoff + 32], ps_misc[:, 0:32],
           Act.Identity)

    ckpt("p1")
    flat_srcs = [(0, sgw_d.flatten()), (1, sdw_d.flatten())]
    for e in range(8):
        flat_srcs.append((2 + e, egw_d[e].flatten()))
    for e in range(8):
        flat_srcs.append((10 + e, edw_d[e].flatten()))
    flat_srcs.append((18, w1_d.flatten()))
    flat_srcs.append((19, w2_d.flatten()))

    CH = 128 * 2048
    meanitems = []
    for mi, flat in flat_srcs:
        for c in range(MEAN_N[mi] // CH):
            meanitems.append((mi, flat, c))

    meanbufs = [wst[0][:], wst[1][:]] + [
        acc[:, i * 2048:(i + 1) * 2048] for i in range(6)]

    def mean_chunk(tg):
        mi, flat, c = meanitems[tg]
        bi = tg % 8
        buf = meanbufs[bi]
        dd = dma(buf, flat[c * CH:(c + 1) * CH].rearrange("(p c) -> p c", p=128),
                 gbuf=f"mst{bi}", pool="bulk")
        if tg % 2 == 0:
            S.wait_tok("v", dd)
            EM("v", "reduce_sum", rows_tm[:, 376:377], buf, axis=AX.X,
                                                 apply_absolute_value=True)
            gset(f"mst{bi}", ("v", S.cnt["v"]))
            EM("v", "tensor_tensor",
                mwcol[:, mi:mi + 1], mwcol[:, mi:mi + 1], rows_tm[:, 376:377],
                Alu.add)
        else:
            S.wait_tok("a", dd)
            EM("a", "activation", wqb[0][:], buf, Act.Abs,
                                                 accum_out=rows_tm[:, 377:378])
            gset(f"mst{bi}", ("a", S.cnt["a"]))
            S.wait("v", "a")
            EM("v", "tensor_tensor",
                mwcol[:, mi:mi + 1], mwcol[:, mi:mi + 1], rows_tm[:, 377:378],
                Alu.add)

    def mean_final():
        S.wait("v", "a")
        EM("v", "tensor_copy", mwrow_b[:], mwcol[:])
        S.wait("t", "v")
        S.wait("t", "a")
        mm(ps_misc[0:32, 0:1], mwrow_b[:], ones_m1b[:])
        S.wait("a", "t")
        EM("a", "activation", mwc32[0:32, 0:1], ps_misc[0:32, 0:1], Act.Identity)
        S.wait("t", "a")
        tr(ps_misc[0:1, 0:32], mwc32[0:32, 0:1], id128[0:32, 0:32])
        S.wait("a", "t")
        EM("a", "activation", mwrow[0:1, :], ps_misc[0:1, 0:32], Act.Identity)
        S.wait("v", "a")
        EM("v", "tensor_tensor", mwrow[0:1, :], mwrow[0:1, :], cntinv[0:1, :],
                                       Alu.mult)
        EM("v", "tensor_scalar", mwrow[0:1, :], mwrow[0:1, :], 1e-5, None,
                                       Alu.max)
        EM("v", "tensor_copy", cb_src[0:1, 0:32], mwrow[0:1, :])
        S.wait("t", "v")
        S.wait("t", "a")
        mm(ps_misc[:, 0:32], ones_k1b[0:1, :], cb_src[0:1, :])
        S.wait("a", "t")
        EM("a", "activation", mwbc[:], ps_misc[:, 0:32], Act.Identity)
        S.wait("v", "a")
        EM("v", "reciprocal", rmwbc[:], mwbc[:])

    ckpt("p2")
    NCH = NT // 512 + T // 512
    mt = 0
    for ci in range(NCH):
        own = ci >= NT // 512
        ch = ci if not own else ci - NT // 512
        src_d = x_all_d if not own else x_own_d
        mgoal = (len(meanitems) * (ci + 1)) // NCH
        while mt < mgoal:
            mean_chunk(mt)
            mt += 1
        for s4 in range(4):
            g = ch * 4 + s4
            xb_ = g % 4
            xstg = [tA[:, 0:1024], tB[:, 0:1024],
                    acc[:, 12288:13312], acc[:, 13312:14336]][xb_]
            dd = dma(xstg, src_d[g * 128:(g + 1) * 128, :],
                     gbuf=f"xst{xb_}", pool="fast")
            S.wait_tok("v", dd)
            S.wait("v", "t")
            EM("v", "tensor_copy", paxb[:], xstg)
            EM("v", "tensor_tensor", tC[:, 0:1024], xstg, paxb[:],
               Alu.subtract)
            gset(f"xst{xb_}", ("v", S.cnt["v"]))
            pmb = ps_misc[:].bitcast(BF16)
            for src, dst in ((paxb, pafa), (tC, paflo)):
                S.wait("t", "v")
                S.wait("t", "a")
                for ft in range(8):
                    tr(pmb[:, ft * 128:(ft + 1) * 128],
                       src[:, ft * 128:(ft + 1) * 128], idbf[:])
                S.wait("a", "t")
                EM("a", "activation",
                    dst[:].rearrange("p (ft s c) -> p ft s c", s=4,
                                     c=128)[:, :, s4, :],
                    pmb[:, 0:1024].rearrange("p (ft c) -> p ft c", c=128),
                    Act.Identity)
        if own:
            sxU, sx2U = sxo, sx2o
            soff, s2off = 0, 0
        else:
            bx = ch // 16
            sxU, sx2U = sxa, sx2
            soff, s2off = bx * 88, bx * 16
        S.wait("v", "a")
        S.wait("t", "a")
        for ft in range(8):
            mm(ps_stat[0:11, 0:512], sxU[:, soff + ft * 11:soff + (ft + 1) * 11],
               pafa[:, ft * 512:(ft + 1) * 512], start=(ft == 0), stop=False)
        for ft in range(8):
            mm(ps_stat[0:11, 0:512], sxU[:, soff + ft * 11:soff + (ft + 1) * 11],
               paflo[:, ft * 512:(ft + 1) * 512], start=False, stop=(ft == 7))
        for ft in range(8):
            EM("v", "tensor_tensor", 
                paf2[:], pafa[:, ft * 512:(ft + 1) * 512],
                pafa[:, ft * 512:(ft + 1) * 512], Alu.mult)
            S.wait("t", "v")
            mm(ps_misc[0:2, 0:512], sx2U[:, s2off + ft * 2:s2off + (ft + 1) * 2],
               paf2[:], start=(ft == 0), stop=(ft == 7))
            if ft < 7:
                S.wait("v", "t")
        S.wait("a", "t")
        EM("a", "activation", past[0:11, :], ps_stat[0:11, 0:512], Act.Identity)
        EM("a", "activation", rowst[0:2, 0:512], ps_misc[0:2, 0:512], Act.Identity)
        stT = (statsA if not own else statsB)[:].rearrange("p (g q) -> p g q", q=13)
        for s4 in range(4):
            g = ch * 4 + s4
            S.wait("t", "a")
            tr(ps_misc[0:128, 0:11], past[0:11, s4 * 128:(s4 + 1) * 128],
               id128[0:11, 0:11])
            S.wait("a", "t")
            EM("a", "activation", stT[:, g, 0:11], ps_misc[0:128, 0:11],
               Act.Identity)
            S.wait("t", "a")
            tr(ps_misc[0:128, 0:2], rowst[0:2, s4 * 128:(s4 + 1) * 128],
               id128[0:2, 0:2])
            S.wait("a", "t")
            EM("a", "activation", stT[:, g, 11:13], ps_misc[0:128, 0:2],
               Act.Identity)
    mean_final()
    ckpt("pAstats")
    stA = statsA[:].rearrange("p (g q) -> p g q", q=13)
    stO = statsB[:].rearrange("p (g q) -> p g q", q=13)
    pA3 = probsA[:].rearrange("p (g e) -> p g e", e=E)
    pO3 = wselB[:].rearrange("p (g e) -> p g e", e=E)
    S.wait("v", "a")
    for b in range(3):
        if b < 2:
            stB = stA[:, b * 64:(b + 1) * 64, :]
            out3 = pA3[:, b * 64:(b + 1) * 64, :]
            cb_ap = cbc
            co = b * 32
            ngz = 64
        else:
            stB = stO[:, 0:16, :]
            out3 = pO3
            cb_ap = cbco
            co = 0
            ngz = 16
        tmAs = tmA[:, 0:ngz]; tmBs = tmB[:, 0:ngz]; tmCs = tmC[:, 0:ngz]
        tmDs = tmD[:, 0:ngz]; tmEs = tmE[:, 0:ngz]
        EM("v", "tensor_scalar", tmAs, stB[:, :, 8], 1.0 / D, None, Alu.mult)
        EM("v", "tensor_tensor", tmBs, tmAs, tmAs, Alu.mult)
        EM("v", "scalar_tensor_tensor", tmCs, stB[:, :, 11], 1.0 / D, tmBs,
            Alu.mult, Alu.subtract)
        EM("v", "tensor_scalar", tmCs, tmCs, 1e-5, None, Alu.add)
        EM("v", "reciprocal", tmCs, tmCs)
        S.wait("a", "v")
        EM("a", "activation", tmCs, tmCs, Act.Sqrt)
        S.wait("v", "a")
        EM("v", "tensor_tensor", tmDs, tmAs, stB[:, :, 9], Alu.mult)
        EM("v", "scalar_tensor_tensor", tmDs, tmDs, -2.0, stB[:, :, 12],
            Alu.mult, Alu.add)
        EM("v", "tensor_scalar", tmEs, tmBs, cb_ap[:, co + 0:co + 1], None, Alu.mult)
        EM("v", "tensor_tensor", tmDs, tmDs, tmEs, Alu.add)
        EM("v", "tensor_tensor", tmEs, tmCs, tmCs, Alu.mult)
        EM("v", "tensor_tensor", tmDs, tmDs, tmEs, Alu.mult)
        EM("v", "tensor_scalar", tmEs, tmAs, cb_ap[:, co + 1:co + 2], None, Alu.mult)
        EM("v", "tensor_tensor", tmEs, tmEs, stB[:, :, 10], Alu.subtract)
        EM("v", "tensor_tensor", tmEs, tmEs, tmCs, Alu.mult)
        EM("v", "scalar_tensor_tensor", tmDs, tmEs, -2.0, tmDs, Alu.mult, Alu.add)
        EM("v", "tensor_scalar", tmDs, tmDs, cb_ap[:, co + 2:co + 3], None, Alu.add)
        EM("v", "tensor_scalar", tmDs, tmDs, 1.0 / D, 1e-6, Alu.mult, Alu.add)
        EM("v", "reciprocal", tmDs, tmDs)
        S.wait("a", "v")
        EM("a", "activation", tmDs, tmDs, Act.Sqrt)
        S.wait("v", "a")
        for e in range(E):
            EM("v", "tensor_scalar", tmEs, tmAs, cb_ap[:, co + 8 + e:co + 9 + e],
                None, Alu.mult)
            EM("v", "tensor_tensor", tmEs, stB[:, :, e], tmEs, Alu.subtract)
            EM("v", "tensor_tensor", tmEs, tmEs, tmCs, Alu.mult)
            EM("v", "tensor_scalar", tmEs, tmEs, cb_ap[:, co + 16 + e:co + 17 + e],
                None, Alu.add)
            EM("v", "tensor_tensor", out3[:, :, e], tmEs, tmDs, Alu.mult)
    ckpt("p3logits")
    def soft_top2(p3, ng_, sel_dst):
        tm1 = tmA[:, 0:ng_]
        tm2 = tmB[:, 0:ng_]
        EM("v", "reduce_max", tm1, p3, axis=AX.X)
        EM("v", "tensor_tensor", p3, p3, tm1.unsqueeze(2).broadcast_to(
            [128, ng_, E]), Alu.subtract)
        S.wait("a", "v")
        EM("a", "activation", p3, p3, Act.Exp)
        S.wait("v", "a")
        EM("v", "reduce_sum", tm1, p3, axis=AX.X)
        EM("v", "reciprocal", tm1, tm1)
        EM("v", "tensor_tensor", p3, p3, tm1.unsqueeze(2).broadcast_to(
            [128, ng_, E]), Alu.mult)
        EM("v", "reduce_max", tm1, p3, axis=AX.X)
        EM("v", "tensor_tensor", sel_dst, p3, tm1.unsqueeze(2).broadcast_to(
            [128, ng_, E]), Alu.is_equal)
        EM("v", "scalar_tensor_tensor", sel_dst, sel_dst, -1e30, p3, Alu.mult,
            Alu.add)
        EM("v", "reduce_max", tm2, sel_dst, axis=AX.X)
        EM("v", "tensor_tensor", sel_dst, p3, tm2.unsqueeze(2).broadcast_to(
            [128, ng_, E]), Alu.is_ge)
    scrA = statsA[:, 0:NGA * E].rearrange("p (g e) -> p g e", e=E)
    soft_top2(pA3, NGA, scrA)
    EM("v", "tensor_copy", selAb[:], statsA[:, 0:NGA * E])
    sB3o = selB[:].rearrange("p (g e) -> p g e", e=E)
    soft_top2(pO3, NG, sB3o)
    S.wait("t", "v")
    S.wait("t", "a")
    mm(ps_misc[0:1, 0:512], ones_m1b[:], selAb[:, 0:512])
    mm(ps_stat[0:1, 0:512], ones_m1b[:], selAb[:, 512:1024])
    S.wait("a", "t")
    EM("a", "activation", cntrow[0:1, 0:512], ps_misc[0:1, 0:512], Act.Identity)
    EM("a", "activation", cntrow[0:1, 512:1024], ps_stat[0:1, 0:512], Act.Identity)
    S.wait("v", "a")
    EM("v", "tensor_tensor", 
        cntrow[0:1, :].rearrange("o (g e) -> o g e", e=E),
        cntrow[0:1, :].rearrange("o (g e) -> o g e", e=E),
        maskb[0:1, :].unsqueeze(2).broadcast_to([1, NGA, E]), Alu.mult)
    EM("v", "reduce_sum", offrow[0:1, :],
        cntrow[0:1, :].rearrange("o (g e) -> o e g", e=E), axis=AX.X)

    ckpt("p3")
    S.barrier()
    for half in range(2):
        g0 = half * 8
        t0 = half * HALF
        for s8 in range(8):
            xb_ = s8 % 4
            xstg = [tA[:, 0:1024], tB[:, 0:1024],
                    acc[:, 12288:13312], acc[:, 13312:14336]][xb_]
            dd = dma(xstg, x_own_d[t0 + s8 * 128: t0 + (s8 + 1) * 128, :],
                     gbuf=("tAx" if xb_ == 0 else f"p4x{xb_}"), pool="fast")
            S.wait_tok("v", dd)
            S.wait("v", "t")
            EM("v", "tensor_copy", paxb[:], xstg)
            gset("tAx" if xb_ == 0 else f"p4x{xb_}", ("v", S.cnt["v"]))
            S.wait("t", "v")
            S.wait("t", "a")
            pmb = ps_misc[:].bitcast(BF16)
            for ft in range(8):
                tr(pmb[:, ft * 128:(ft + 1) * 128],
                   paxb[:, ft * 128:(ft + 1) * 128], idbf[:])
            S.wait("a", "t")
            EM("a", "activation",
                xbh[:].rearrange("p (ft s c) -> p ft s c", s=8,
                                 c=128)[:, :, s8, :],
                pmb[:, 0:1024].rearrange("p (ft c) -> p ft c", c=128),
                Act.Identity)
        S.wait("v", "a")
        S.wait("v", "t")
        EM("a", "activation", h1h[:], xbh[:], Act.Square)
        for c in range(2):
            S.wait("t", "v")
            S.wait("t", "a")
            for ft in range(8):
                mm(ps_stat[0:1, 0:512], ones_m1b[:],
                   xbh[:, ft * HALF + c * 512: ft * HALF + (c + 1) * 512],
                   start=(ft == 0), stop=(ft == 7))
            for ft in range(8):
                mm(ps_misc[0:1, 0:512], ones_m1b[:],
                   h1h[:, ft * HALF + c * 512: ft * HALF + (c + 1) * 512],
                   start=(ft == 0), stop=(ft == 7))
            two_rows_to_tm(0, g0 + c * 4, 1, g0 + c * 4)
        S.wait("v", "a")
        EM("v", "tensor_scalar", tmcol(2, g0, 8), tmcol(0, g0, 8), 1.0 / D,
                                       None, Alu.mult)
        EM("v", "tensor_tensor", tmcol(3, g0, 8), tmcol(2, g0, 8),
                                       tmcol(2, g0, 8), Alu.mult)
        EM("v", "scalar_tensor_tensor", tmcol(1, g0, 8), tmcol(1, g0, 8),
                                              1.0 / D, tmcol(3, g0, 8), Alu.mult,
                                              Alu.subtract)
        EM("v", "tensor_scalar", tmcol(1, g0, 8), tmcol(1, g0, 8), 1e-5,
                                       None, Alu.add)
        EM("v", "reciprocal", tmcol(1, g0, 8), tmcol(1, g0, 8))
        S.wait("a", "v")
        EM("a", "activation", tmcol(1, g0, 8), tmcol(1, g0, 8), Act.Sqrt)
        S.wait("v", "a")
        EM("v", "tensor_tensor", tmcol(3, g0, 8), tmcol(2, g0, 8),
                                       tmcol(1, g0, 8), Alu.mult)
        ckpt(f"p4stats{half}")
        build_bcast(tmcol(1, g0, 8), Bu, 8)
        build_bcast(tmcol(3, g0, 8), Bv, 8)
        S.wait("v", "t")
        for ft in range(8):
            av = vecs[:, 8 + ft:9 + ft]
            bv = vecs[:, 0 + ft:1 + ft]
            EM("v", "tensor_tensor", 
                tC[:, 0:HALF], xbh[:, ft * HALF:(ft + 1) * HALF], Bu[:], Alu.mult)
            EM("v", "tensor_tensor", tC[:, 0:HALF], tC[:, 0:HALF], Bv[:],
                                           Alu.subtract)
            EM("v", "tensor_scalar", 
                h1h[:, ft * HALF:(ft + 1) * HALF], tC[:, 0:HALF], av, bv,
                Alu.mult, Alu.add)
        S.wait("a", "t")
        S.wait("a", "v")
        EM("a", "activation", xbh[:], h1h[:], Act.Square)
        for c in range(2):
            S.wait("t", "v")
            S.wait("t", "a")
            for ft in range(8):
                mm(ps_stat[0:1, 0:512], ones_m1b[:],
                   xbh[:, ft * HALF + c * 512: ft * HALF + (c + 1) * 512],
                   start=(ft == 0), stop=(ft == 7))
            S.wait("a", "t")
            EM("a", "activation", rowst[0:1, :], ps_stat[0:1, 0:512],
                                        Act.Identity)
            rows_to_tm(1, [(4, 0)], g0 + c * 4)
        S.wait("v", "a")
        maxtree_to_tm(xbh, HALF, 8, 5, g0, 8)
        EM("v", "tensor_scalar", tmcol(6, g0, 8), tmcol(4, g0, 8), 1.0 / D,
                                       1e-8, Alu.mult, Alu.add)
        EM("v", "reciprocal", tmcol(6, g0, 8), tmcol(6, g0, 8))
        S.wait("a", "v")
        EM("a", "activation", tmcol(6, g0, 8), tmcol(6, g0, 8), Act.Sqrt)
        EM("a", "activation", tmcol(5, g0, 8), tmcol(5, g0, 8), Act.Sqrt)
        S.wait("v", "a")
        EM("v", "tensor_tensor", tmcol(5, g0, 8), tmcol(5, g0, 8),
                                       tmcol(6, g0, 8), Alu.mult)
        EM("v", "tensor_scalar", tmcol(5, g0, 8), tmcol(5, g0, 8), 1e-5,
                                       None, Alu.max)
        EM("v", "reciprocal", tmcol(7, g0, 8), tmcol(5, g0, 8))
        EM("v", "tensor_scalar", tmcol(7, g0, 8), tmcol(7, g0, 8), 127.0,
                                       None, Alu.mult)
        EM("v", "tensor_tensor", tmcol(6, g0, 8), tmcol(6, g0, 8),
                                       tmcol(7, g0, 8), Alu.mult)
        EM("v", "reciprocal", tmcol(7, g0, 8), tmcol(7, g0, 8))
        build_bcast(tmcol(6, g0, 8), Bw, 8)
        for ft in range(8):
            EM("v", "tensor_tensor", 
                tA[:, 0:HALF], h1h[:, ft * HALF:(ft + 1) * HALF], Bw[:], Alu.mult)
            rnd(xq1[:, ft * T + t0: ft * T + t0 + HALF], tA[:, 0:HALF])
        gadd("tAx", ("v", S.cnt["v"]))
    ckpt("p4logits")
    S.wait("v", "a")
    EM("v", "tensor_tensor", wselB[:], wselB[:], selB[:], Alu.mult)
    EM("v", "tensor_copy", selBb[:], selB[:])
    S.wait("t", "v")
    S.wait("t", "a")
    mm(ps_misc[0:1, 0:128], ones_m1b[:], selBb[:])
    S.wait("a", "t")
    EM("a", "activation", offBrow[0:1, :], ps_misc[0:1, 0:128], Act.Identity)
    S.wait("v", "a")
    oB3 = offBrow[0:1, :].rearrange("o (g e) -> o e g", e=E)
    scanrow = trow[0:1, 256:256 + NG * E]
    cB3 = scanrow.rearrange("o (g e) -> o e g", e=E)
    for e in range(E):
        EM("v", "tensor_tensor_scan", cB3[:, e, :], oB3[:, e, :],
                                                 oB3[:, e, :], 0.0, Alu.add,
                                                 Alu.bypass)
    EM("v", "tensor_tensor", scanrow, scanrow, offBrow[0:1, :],
                                   Alu.subtract)
    EM("v", "tensor_tensor", 
        scanrow.rearrange("o (g e) -> o g e", e=E),
        scanrow.rearrange("o (g e) -> o g e", e=E),
        offrow[0:1, :].unsqueeze(1).broadcast_to([1, NG, E]), Alu.add)
    EM("v", "tensor_copy", offBb[0:1, :], scanrow)
    S.wait("t", "v")
    S.wait("t", "a")
    for g in range(NG):
        mm(ps_misc[0:128, g * 8:(g + 1) * 8], tribf[:], selBb[:, g * 8:(g + 1) * 8],
           start=(g == 0), stop=False)
    mm(ps_misc[0:128, 0:NG * E], ones_k1b[0:1, :], offBb[0:1, :], start=False,
       stop=True)
    S.wait("v", "t")
    EM("v", "tensor_scalar", cumsb[:], ps_misc[0:128, 0:NG * E], CAP, None,
                                   Alu.is_le)
    EM("v", "tensor_tensor", wkeep[:], cumsb[:], wselB[:], Alu.mult)
    EM("v", "tensor_copy", wkeepb[:], wkeep[:])
    for c in range(4):
        S.wait("t", "v")
        S.wait("t", "a")
        for s4 in range(4):
            g = c * 4 + s4
            tr(ps_stat[0:8, 0:512].bitcast(BF16)[:, s4 * 128:(s4 + 1) * 128],
               wkeepb[:, g * 8:(g + 1) * 8], idbf[:])
        S.wait("a", "t")
        EM("a", "activation", 
            wkT[:, c * 512:(c + 1) * 512],
            ps_stat[0:8, 0:512].bitcast(BF16)[:, 0:512], Act.Identity)
    build_bcast(tmcol(7, 0, 8), Bu, 8)
    build_bcast(tmcol(7, 8, 8), Bv, 8)
    S.wait("v", "a")
    EM("v", "tensor_copy", Bs1F[:, 0:HALF], Bu[:])
    EM("v", "tensor_copy", Bs1F[:, HALF:T], Bv[:])

    ckpt("p4")
    S.barrier()
    for e in range(9):
        gsrc = sgw_d if e == 0 else egw_d[e - 1]
        dsrc = sdw_d if e == 0 else edw_d[e - 1]
        mi_g = 0 if e == 0 else 2 + (e - 1)
        mi_d = 1 if e == 0 else 10 + (e - 1)
        for half in range(2):
            g0 = half * 8
            t0 = half * HALF
            def b1_load_quant(j):
                bi = j % 2
                if half == 0:
                    d1 = dma(wst[bi][:, 0:1024], gsrc[j], gbuf=f"wst{bi}")
                    d2 = dma(wst[bi][:, 1024:2048], gsrc[j + 16])
                    S.wait_tok("a", d1)
                    S.wait_tok("a", d2)
                    for tok in guard.get(f"wqs{bi}", []):
                        S.wait_tok("v", tok)
                    for tok in guard.get(f"wqb{bi}", []):
                        S.wait_tok("v", tok)
                    quant_strip(wst[bi], wqb[bi], mi_g, 2048)
                    gset(f"wst{bi}", ("v", S.cnt["v"]))
                    qd = S.cnt["v"]
                    S.wait("a", "v", qd)
                    gset(f"wqs{bi}", dma_w(scr_g_d[e, j], wqb[bi][:]))
                    return ("v", qd)
                d1 = dma(wqb[bi][:, 0:1024], scr_g_d[e, j][:, 0:1024],
                         gbuf=f"wqb{bi}")
                d2 = dma(wqb[bi][:, 1024:2048], scr_g_d[e, j][:, 1024:2048])
                return [d1, d2]

            def b1_mms(j, qtok):
                bi = j % 2
                r0 = (2 * j) % 3
                r1 = (2 * j + 1) % 3
                for tk in (qtok if isinstance(qtok, list) else [qtok]):
                    S.wait_tok("t", tk)
                for r in (r0, r1):
                    for tok in guard.get(f"psa{r}", []):
                        S.wait_tok("t", tok)
                    for tok in guard.get(f"psv{r}", []):
                        S.wait_tok("t", tok)
                for k in range(8):
                    for c in range(2):
                        mm(psR[r0][:, c * 512:(c + 1) * 512],
                           wqb[bi][:, k * 128:(k + 1) * 128],
                           xq1[:, k * T + t0 + c * 512: k * T + t0 + (c + 1) * 512],
                           start=(k == 0), stop=(k == 7))
                m0 = ("t", S.cnt["t"])
                for k in range(8):
                    for c in range(2):
                        mm(psR[r1][:, c * 512:(c + 1) * 512],
                           wqb[bi][:, 1024 + k * 128:1024 + (k + 1) * 128],
                           xq1[:, k * T + t0 + c * 512: k * T + t0 + (c + 1) * 512],
                           start=(k == 0), stop=(k == 7))
                gset(f"wqb{bi}", ("t", S.cnt["t"]))
                return (m0, ("t", S.cnt["t"]))

            def b1_evac(j, mtoks):
                r0 = (2 * j) % 3
                r1 = (2 * j + 1) % 3
                S.wait_tok("v", mtoks[0])
                EM("v", "scalar_tensor_tensor",
                    tA[:, 0:HALF], psR[r0][:], mwbc[:, mi_g:mi_g + 1],
                    Bs1F[:, t0:t0 + HALF], Alu.mult, Alu.mult)
                gset(f"psv{r0}", ("v", S.cnt["v"]))
                S.wait("a", "v")
                EM("a", "activation", tC[:, 0:HALF], tA[:, 0:HALF], Act.Silu)
                S.wait_tok("a", mtoks[1])
                EM("a", "activation", tB[:, 0:HALF], psR[r1][:], Act.Identity)
                gset(f"psa{r1}", ("a", S.cnt["a"]))
                S.wait("v", "a")
                EM("v", "tensor_tensor", zt[:, j * HALF:(j + 1) * HALF],
                                                    tC[:, 0:HALF], tB[:, 0:HALF],
                                                    Alu.mult)

            pend = None
            for j in range(17):
                if j < 16:
                    qtok = b1_load_quant(j)
                if pend is not None:
                    b1_evac(*pend)
                if j < 16:
                    pend = (j, b1_mms(j, qtok))
                else:
                    pend = None
            if e == 0 and half == 0:
                ckpt("p5b1")
            for j in range(8):
                S.wait("a", "t")
                S.wait("a", "v")
                EM("a", "activation",
                    tC[:, 0:HALF], zt[:, 2 * j * HALF:(2 * j + 1) * HALF],
                    Act.Square)
                EM("a", "activation",
                    tD[:, 0:HALF], zt[:, (2 * j + 1) * HALF:(2 * j + 2) * HALF],
                    Act.Square)
                S.wait("t", "a")
                mm(ps_stat[0:1, 0:512], ones_m1b[:], tC[:, 0:512],
                   start=(j == 0), stop=False)
                mm(ps_misc[0:1, 0:512], ones_m1b[:], tC[:, 512:1024],
                   start=(j == 0), stop=False)
                mm(ps_stat[0:1, 0:512], ones_m1b[:], tD[:, 0:512],
                   start=False, stop=(j == 7))
                mm(ps_misc[0:1, 0:512], ones_m1b[:], tD[:, 512:1024],
                   start=False, stop=(j == 7))
                S.wait("v", "a")
                S.wait("v", "t")
                EM("v", "tensor_tensor", tD[:, 0:HALF], tC[:, 0:HALF],
                                               tD[:, 0:HALF], Alu.max)
                if j == 0:
                    EM("v", "tensor_copy", tE_[:, 0:HALF], tD[:, 0:HALF])
                else:
                    EM("v", "tensor_tensor", tE_[:, 0:HALF], tE_[:, 0:HALF],
                                                   tD[:, 0:HALF], Alu.max)
            two_rows_to_tm(10, g0, 10, g0 + 4)
            S.wait("v", "a")
            maxtree_to_tm(tE_, HALF, 1, 11, g0, 8)
            EM("v", "tensor_scalar", tmcol(9, g0, 8), tmcol(7, g0, 8),
                                           mwbc[:, mi_g:mi_g + 1], None, Alu.mult)
            EM("v", "tensor_scalar", tmcol(12, g0, 8), tmcol(10, g0, 8),
                                           1.0 / IM, None, Alu.mult)
            EM("v", "tensor_tensor", tmcol(12, g0, 8), tmcol(12, g0, 8),
                                           tmcol(9, g0, 8), Alu.mult)
            EM("v", "tensor_tensor", tmcol(12, g0, 8), tmcol(12, g0, 8),
                                           tmcol(9, g0, 8), Alu.mult)
            EM("v", "tensor_scalar", tmcol(12, g0, 8), tmcol(12, g0, 8),
                                           1e-8, None, Alu.add)
            EM("v", "reciprocal", tmcol(12, g0, 8), tmcol(12, g0, 8))
            S.wait("a", "v")
            EM("a", "activation", tmcol(12, g0, 8), tmcol(12, g0, 8),
                                        Act.Sqrt)
            EM("a", "activation", tmcol(11, g0, 8), tmcol(11, g0, 8),
                                        Act.Sqrt)
            S.wait("v", "a")
            EM("v", "tensor_tensor", tmcol(11, g0, 8), tmcol(11, g0, 8),
                                           tmcol(9, g0, 8), Alu.mult)
            EM("v", "tensor_tensor", tmcol(11, g0, 8), tmcol(11, g0, 8),
                                           tmcol(12, g0, 8), Alu.mult)
            EM("v", "tensor_scalar", tmcol(11, g0, 8), tmcol(11, g0, 8),
                                           1e-5, None, Alu.max)
            EM("v", "reciprocal", tmcol(13, g0, 8), tmcol(11, g0, 8))
            EM("v", "tensor_scalar", tmcol(13, g0, 8), tmcol(13, g0, 8),
                                           127.0, None, Alu.mult)
            EM("v", "tensor_tensor", tmcol(12, g0, 8), tmcol(12, g0, 8),
                                           tmcol(13, g0, 8), Alu.mult)
            EM("v", "reciprocal", tmcol(13, g0, 8), tmcol(13, g0, 8))
            EM("v", "tensor_tensor", tmcol(12, g0, 8), tmcol(12, g0, 8),
                                           tmcol(9, g0, 8), Alu.mult)
            build_bcast(tmcol(12, g0, 8), Bw, 8)
            build_bcast(tmcol(13, g0, 8), Bx, 8)
            if e == 0:
                EM("v", "tensor_copy", tF[:, 0:HALF], Bx[:])
            else:
                S.wait("g", "t")
                EM("g", "memset", e8sel[:], 1.0)
                S.wait("g", "g")
                EM("g", "affine_select",
                    e8sel[:], e8sel[:], [[0, 128]], Alu.is_equal, 0.0,
                    base=-(e - 1), channel_multiplier=1)
                S.wait("t", "g")
                for c in range(2):
                    S.wait("t", "v")
                    S.wait("t", "a")
                    mm(ps_misc[0:128, 0:512], e8sel[:],
                       wkT[:, t0 + c * 512: t0 + (c + 1) * 512])
                    S.wait("v", "t")
                    EM("v", "tensor_tensor",
                        tF[:, c * 512:(c + 1) * 512], ps_misc[0:128, 0:512],
                        Bx[:, c * 512:(c + 1) * 512], Alu.mult)
            S.wait("v", "t")
            for j in range(16):
                EM("v", "tensor_tensor", tA[:, 0:HALF],
                                                    zt[:, j * HALF:(j + 1) * HALF],
                                                    Bw[:], Alu.mult)
                rnd(zt[:, j * HALF:(j + 1) * HALF], tA[:, 0:HALF])
            zq_tok = ("v", S.cnt["v"])
            def b2_load_quant(m):
                bi = m % 2
                if half == 0:
                    d1 = dma(wst[bi][:], dsrc[m], gbuf=f"wst{bi}")
                    S.wait_tok("a", d1)
                    for tok in guard.get(f"wqs{bi}", []):
                        S.wait_tok("v", tok)
                    for tok in guard.get(f"wqb{bi}", []):
                        S.wait_tok("v", tok)
                    quant_strip(wst[bi], wqb[bi], mi_d, 2048)
                    gset(f"wst{bi}", ("v", S.cnt["v"]))
                    qd = S.cnt["v"]
                    S.wait("a", "v", qd)
                    gset(f"wqs{bi}", dma_w(scr_d_d[e, m], wqb[bi][:]))
                    return ("v", qd)
                d1 = dma(wqb[bi][:, 0:1024], scr_d_d[e, m][:, 0:1024],
                         gbuf=f"wqb{bi}")
                d2 = dma(wqb[bi][:, 1024:2048], scr_d_d[e, m][:, 1024:2048])
                return [d1, d2]

            def b2_mms(m, qtok):
                bi = m % 2
                r0 = m % 3
                for tk in (qtok if isinstance(qtok, list) else [qtok]):
                    S.wait_tok("t", tk)
                S.wait_tok("t", zq_tok)
                for tok in guard.get(f"psa{r0}", []):
                    S.wait_tok("t", tok)
                for tok in guard.get(f"psv{r0}", []):
                    S.wait_tok("t", tok)
                for k in range(16):
                    for c in range(2):
                        mm(psR[r0][:, c * 512:(c + 1) * 512],
                           wqb[bi][:, k * 128:(k + 1) * 128],
                           zt[:, k * HALF + c * 512: k * HALF + (c + 1) * 512],
                           start=(k == 0), stop=(k == 15))
                gset(f"wqb{bi}", ("t", S.cnt["t"]))
                return ("t", S.cnt["t"])

            def b2_evac(m, mtok):
                r0 = m % 3
                S.wait_tok("v", mtok)
                EM("v", "scalar_tensor_tensor",
                    tA[:, 0:HALF], psR[r0][:], mwbc[:, mi_d:mi_d + 1],
                    tF[:, 0:HALF], Alu.mult, Alu.mult)
                gset(f"psv{r0}", ("v", S.cnt["v"]))
                if e == 0:
                    EM("v", "tensor_copy",
                        acc[:, m * T + t0: m * T + t0 + HALF],
                        tA[:, 0:HALF])
                else:
                    EM("v", "tensor_tensor",
                        acc[:, m * T + t0: m * T + t0 + HALF],
                        acc[:, m * T + t0: m * T + t0 + HALF],
                        tA[:, 0:HALF], Alu.add)

            pend = None
            for m in range(9):
                if m < 8:
                    qtok = b2_load_quant(m)
                if pend is not None:
                    b2_evac(*pend)
                if m < 8:
                    pend = (m, b2_mms(m, qtok))
                else:
                    pend = None
            if half == 1:
                ckpt(f"p5e{e}")

    ckpt("p5")
    S.barrier()
    xbh2_32 = xbh2[:].bitcast(F32)
    for half in range(2):
        t0 = half * HALF
        for s8 in range(8):
            xb_ = s8 % 4
            xstg = [tB[:, 0:1024], xbh2_32[:, 0:1024], xbh2_32[:, 1024:2048],
                    xbh2_32[:, 2048:3072]][xb_]
            dd = dma(xstg, x_own_d[t0 + s8 * 128: t0 + (s8 + 1) * 128, :],
                     gbuf=f"p6x{xb_}", pool="fast")
            S.wait_tok("t", dd)
            S.wait("t", "a")
            for ft in range(4):
                tr(ps_misc[:, ft * 128:(ft + 1) * 128],
                   xstg[:, ft * 128:(ft + 1) * 128], id128[:])
            S.wait("a", "t")
            EM("a", "activation", tA[:, 0:512], ps_misc[:, 0:512],
                                        Act.Identity)
            S.wait("t", "a")
            for ft in range(4):
                tr(ps_misc[:, ft * 128:(ft + 1) * 128],
                   xstg[:, 512 + ft * 128: 512 + (ft + 1) * 128], id128[:])
            gset(f"p6x{xb_}", ("t", S.cnt["t"]))
            S.wait("a", "t")
            EM("a", "activation", tA[:, 512:1024], ps_misc[:, 0:512],
                                        Act.Identity)
            S.wait("v", "a")
            for ft in range(8):
                av = vecs[:, 16 + ft:17 + ft]
                EM("v", "scalar_tensor_tensor", 
                    acc[:, ft * T + t0 + s8 * 128: ft * T + t0 + s8 * 128 + 128],
                    acc[:, ft * T + t0 + s8 * 128: ft * T + t0 + s8 * 128 + 128],
                    av, tA[:, ft * 128:(ft + 1) * 128], Alu.mult, Alu.add)

    ckpt("p6")
    for half in range(2):
        g0 = half * 8
        t0 = half * HALF
        S.wait("v", "t")
        for ft in range(8):
            EM("v", "tensor_copy", 
                xbh2[:, ft * HALF:(ft + 1) * HALF],
                acc[:, ft * T + t0: ft * T + t0 + HALF])
        S.wait("a", "v")
        S.wait("a", "t")
        EM("a", "activation", h1h2[:], xbh2[:], Act.Square)
        for c in range(2):
            S.wait("t", "v")
            S.wait("t", "a")
            for ft in range(8):
                mm(ps_stat[0:1, 0:512], ones_m1b[:],
                   xbh2[:, ft * HALF + c * 512: ft * HALF + (c + 1) * 512],
                   start=(ft == 0), stop=(ft == 7))
            for ft in range(8):
                mm(ps_misc[0:1, 0:512], ones_m1b[:],
                   h1h2[:, ft * HALF + c * 512: ft * HALF + (c + 1) * 512],
                   start=(ft == 0), stop=(ft == 7))
            two_rows_to_tm(14, g0 + c * 4, 15, g0 + c * 4)
        S.wait("v", "a")
        EM("v", "tensor_scalar", tmcol(16, g0, 8), tmcol(14, g0, 8), 1.0 / D,
                                       None, Alu.mult)
        EM("v", "tensor_tensor", tmcol(17, g0, 8), tmcol(16, g0, 8),
                                       tmcol(16, g0, 8), Alu.mult)
        EM("v", "scalar_tensor_tensor", tmcol(15, g0, 8), tmcol(15, g0, 8),
                                              1.0 / D, tmcol(17, g0, 8), Alu.mult,
                                              Alu.subtract)
        EM("v", "tensor_scalar", tmcol(15, g0, 8), tmcol(15, g0, 8), 1e-5,
                                       None, Alu.add)
        EM("v", "reciprocal", tmcol(15, g0, 8), tmcol(15, g0, 8))
        S.wait("a", "v")
        EM("a", "activation", tmcol(15, g0, 8), tmcol(15, g0, 8), Act.Sqrt)
        S.wait("v", "a")
        EM("v", "tensor_tensor", tmcol(16, g0, 8), tmcol(16, g0, 8),
                                       tmcol(15, g0, 8), Alu.mult)
        build_bcast(tmcol(15, g0, 8), Bu, 8)
        build_bcast(tmcol(16, g0, 8), Bv, 8)
        S.wait("v", "t")
        for ft in range(8):
            av = vecs[:, 32 + ft:33 + ft]
            bv = vecs[:, 24 + ft:25 + ft]
            EM("v", "tensor_tensor", 
                tC[:, 0:HALF], xbh2[:, ft * HALF:(ft + 1) * HALF], Bu[:], Alu.mult)
            EM("v", "tensor_tensor", tC[:, 0:HALF], tC[:, 0:HALF], Bv[:],
                                           Alu.subtract)
            EM("v", "tensor_scalar", 
                h1h2[:, ft * HALF:(ft + 1) * HALF], tC[:, 0:HALF], av, bv,
                Alu.mult, Alu.add)
        S.wait("a", "v")
        S.wait("a", "t")
        EM("a", "activation", xbh2[:], h1h2[:], Act.Square)
        S.wait("v", "a")
        maxtree_to_tm(xbh2, HALF, 8, 17, g0, 8)
        S.wait("a", "v")
        EM("a", "activation", tmcol(17, g0, 8), tmcol(17, g0, 8), Act.Sqrt)
        S.wait("v", "a")
        EM("v", "tensor_scalar", tmcol(17, g0, 8), tmcol(17, g0, 8), 1e-5,
                                       None, Alu.max)
        EM("v", "reciprocal", tmcol(18, g0, 8), tmcol(17, g0, 8))
        EM("v", "tensor_scalar", tmcol(18, g0, 8), tmcol(18, g0, 8), 127.0,
                                       None, Alu.mult)
        EM("v", "reciprocal", tmcol(19, g0, 8), tmcol(18, g0, 8))
        build_bcast(tmcol(18, g0, 8), Bw, 8)
        for ft in range(8):
            EM("v", "tensor_tensor", 
                tA[:, 0:HALF], h1h2[:, ft * HALF:(ft + 1) * HALF], Bw[:], Alu.mult)
            rnd(xq3[:, ft * T + t0: ft * T + t0 + HALF], tA[:, 0:HALF])
        build_bcast(tmcol(19, g0, 8), Bu, 8)
        EM("v", "tensor_copy", 
            Bs1F[:, half * HALF:(half + 1) * HALF], Bu[:])

    ckpt("p7")
    S.barrier()
    for q in range(4):
        t0 = q * QTR
        g0q = q * 4
        half = t0 // HALF
        hoff = t0 % HALF
        if q > 0:
            S.wait("a", "w")
        def w1_load_quant(m):
            bi = m % 2
            if q == 0:
                d1 = dma(wst[bi][:, 0:1024], w1_d[m], gbuf=f"wst{bi}")
                S.wait_tok("a", d1)
                for tok in guard.get(f"wqs{bi}", []):
                    S.wait_tok("v", tok)
                for tok in guard.get(f"wqb{bi}", []):
                    S.wait_tok("v", tok)
                quant_strip(wst[bi], wqb[bi], 18, 1024)
                gset(f"wst{bi}", ("v", S.cnt["v"]))
                qd = S.cnt["v"]
                S.wait("a", "v", qd)
                gset(f"wqs{bi}", dma_w(scr_w1_d[m], wqb[bi][:, 0:1024]))
                return ("v", qd)
            return dma(wqb[bi][:, 0:1024], scr_w1_d[m], gbuf=f"wqb{bi}")

        def w1_mms(m, qtok):
            bi = m % 2
            r0 = m % 3
            S.wait_tok("t", qtok)
            for tok in guard.get(f"psa{r0}", []):
                S.wait_tok("t", tok)
            for tok in guard.get(f"psv{r0}", []):
                S.wait_tok("t", tok)
            for k in range(8):
                mm(psR[r0][:, 0:512], wqb[bi][:, k * 128:(k + 1) * 128],
                   xq3[:, k * T + t0: k * T + t0 + QTR], start=(k == 0),
                   stop=(k == 7))
            gset(f"wqb{bi}", ("t", S.cnt["t"]))
            return ("t", S.cnt["t"])

        def w1_evac(m, mtok):
            r0 = m % 3
            S.wait_tok("v", mtok)
            EM("v", "scalar_tensor_tensor",
                tA[:, 0:QTR], psR[r0][:, 0:512], mwbc[:, 18:19],
                Bs1F[:, half * HALF + hoff: half * HALF + hoff + QTR],
                Alu.mult, Alu.mult)
            gset(f"psv{r0}", ("v", S.cnt["v"]))
            S.wait("a", "v")
            EM("a", "activation",
                hft[:, m * QTR:(m + 1) * QTR], tA[:, 0:QTR], Act.Gelu)

        pend = None
        for m in range(33):
            if m < 32:
                qtok = w1_load_quant(m)
            if pend is not None:
                w1_evac(*pend)
            if m < 32:
                pend = (m, w1_mms(m, qtok))
            else:
                pend = None
        S.wait("v", "a")
        for j in range(16):
            S.wait("a", "v")
            EM("a", "activation",
                tC[:, 0:QTR], hft[:, 2 * j * QTR:(2 * j + 1) * QTR],
                Act.Square)
            EM("a", "activation",
                tD[:, 0:QTR], hft[:, (2 * j + 1) * QTR:(2 * j + 2) * QTR],
                Act.Square)
            S.wait("v", "a")
            EM("v", "tensor_tensor", tD[:, 0:QTR], tC[:, 0:QTR],
                                           tD[:, 0:QTR], Alu.max)
            if j == 0:
                EM("v", "tensor_copy", tE_[:, 0:QTR], tD[:, 0:QTR])
            else:
                EM("v", "tensor_tensor", tE_[:, 0:QTR], tE_[:, 0:QTR],
                                               tD[:, 0:QTR], Alu.max)
        maxtree_to_tm(tE_, QTR, 1, 20, g0q, 4)
        S.wait("a", "v")
        EM("a", "activation", tmcol(20, g0q, 4), tmcol(20, g0q, 4), Act.Sqrt)
        S.wait("v", "a")
        EM("v", "tensor_scalar", tmcol(20, g0q, 4), tmcol(20, g0q, 4), 1e-5,
                                       None, Alu.max)
        EM("v", "reciprocal", tmcol(21, g0q, 4), tmcol(20, g0q, 4))
        EM("v", "tensor_scalar", tmcol(21, g0q, 4), tmcol(21, g0q, 4), 127.0,
                                       None, Alu.mult)
        EM("v", "reciprocal", tmcol(22, g0q, 4), tmcol(21, g0q, 4))
        build_bcast(tmcol(21, g0q, 4), Bw, 4)
        build_bcast(tmcol(22, g0q, 4), Bx, 4)
        for j in range(32):
            EM("v", "tensor_tensor", tA[:, 0:QTR],
                                                hft[:, j * QTR:(j + 1) * QTR],
                                                Bw[:, 0:QTR], Alu.mult)
            rnd(hft[:, j * QTR:(j + 1) * QTR], tA[:, 0:QTR])
        hq_tok = ("v", S.cnt["v"])

        def w2_load_quant(m):
            if q == 0:
                for hk in range(2):
                    d1 = dma(wst[hk][:],
                             w2_d[m][:, hk * 2048:(hk + 1) * 2048],
                             gbuf=f"wst{hk}")
                    S.wait_tok("a", d1)
                    for tok in guard.get("wq2s", []):
                        S.wait_tok("v", tok)
                    for tok in guard.get("wq20", []):
                        S.wait_tok("v", tok)
                    EM("a", "activation", wst[hk][:], wst[hk][:],
                                                Act.Identity,
                                                scale=rmwbc[:, 19:20],
                                                bias=magic_col[:, 0:1])
                    S.wait("v", "a")
                    EM("v", "tensor_scalar",
                        wst[hk][:], wst[hk][:], MAGIC, -1.0, Alu.subtract,
                        Alu.max)
                    EM("v", "tensor_scalar",
                        wq2[0][:, hk * 2048:(hk + 1) * 2048], wst[hk][:],
                        1.0, None, Alu.min)
                    gset(f"wst{hk}", ("v", S.cnt["v"]))
                qd = S.cnt["v"]
                S.wait("a", "v", qd)
                gset("wq2s", dma_w(scr_w2_d[m], wq2[0][:]))
                return ("v", qd)
            dd = [dma(wq2[0][:, i * 1024:(i + 1) * 1024],
                      scr_w2_d[m][:, i * 1024:(i + 1) * 1024],
                      gbuf=("wq20" if i == 0 else None)) for i in range(4)]
            return dd

        def w2_mms(m, qtok):
            r0 = m % 3
            for tk in (qtok if isinstance(qtok, list) else [qtok]):
                S.wait_tok("t", tk)
            S.wait_tok("t", hq_tok)
            for tok in guard.get(f"psa{r0}", []):
                S.wait_tok("t", tok)
            for tok in guard.get(f"psv{r0}", []):
                S.wait_tok("t", tok)
            for k in range(32):
                mm(psR[r0][:, 0:512], wq2[0][:, k * 128:(k + 1) * 128],
                   hft[:, k * QTR:(k + 1) * QTR], start=(k == 0), stop=(k == 31))
            gset("wq20", ("t", S.cnt["t"]))
            return ("t", S.cnt["t"])

        def w2_evac(m, mtok):
            r0 = m % 3
            S.wait_tok("v", mtok)
            EM("v", "scalar_tensor_tensor",
                tA[:, 0:QTR], psR[r0][:, 0:512], mwbc[:, 19:20], Bx[:, 0:QTR],
                Alu.mult, Alu.mult)
            gset(f"psv{r0}", ("v", S.cnt["v"]))
            gv = vecs[:, 40 + m:41 + m]
            EM("v", "scalar_tensor_tensor",
                tB[:, 0:QTR], tA[:, 0:QTR], gv,
                acc[:, m * T + t0: m * T + t0 + QTR], Alu.mult, Alu.add)
            obuf = tA[:, 512:1024] if m % 2 == 0 else tB[:, 512:1024]
            for tok in guard.get(f"ob{m % 2}", []):
                S.wait_tok("a", tok)
            for s4 in range(4):
                S.wait("t", "v")
                S.wait("t", "a")
                tr(ps_misc[:, 0:128], tB[:, s4 * 128:(s4 + 1) * 128], id128[:])
                S.wait("a", "t")
                EM("a", "activation",
                    obuf[:, s4 * 128:(s4 + 1) * 128],
                    ps_misc[:, 0:128], Act.Identity)
            wtok = dma_w(
                out_d[t0:t0 + QTR, m * 128:(m + 1) * 128]
                .rearrange("(s p) d -> p s d", p=128),
                obuf[:].rearrange("p (s d) -> p s d", d=128))
            gset(f"ob{m % 2}", wtok)

        pend = None
        for m in range(9):
            if m < 8:
                qtok = w2_load_quant(m)
            if pend is not None:
                w2_evac(*pend)
            if m < 8:
                pend = (m, w2_mms(m, qtok))
            else:
                pend = None

    S.stopped = False
    tap_srcs = {
        "t_vecs": (vecs, [128, 48]), "t_offrow": (offrow, [1, 8]),
        "t_wkeep": (wkeep, [128, 128]), "t_mwrow": (mwrow, [1, 32]),
        "t_rows": (rows_tm, [128, NG * NQ]), "t_acc": (acc, [128, FT * T]),
        "t_selB": (selB, [128, 128]), "t_wselB": (wselB, [128, 128]),
        "t_cum": (cumsb, [128, 128]),
        "t_xq1": (xq1, [128, FT * T]),
        "t_h1": (h1h, [128, FT * HALF]),
        "t_xbh": (xbh, [128, FT * HALF]),
        "t_cbc": (cbc, [128, 32]),
        "t_statsA": (statsA, [128, NGA * 13]),
        "t_statsB": (statsB, [128, NG * 13]),
        "t_probsA": (probsA, [128, NGA * E]),
        "t_cbco": (cbco, [128, 32]),
        "t_tmA": (tmA, [128, NGA]), "t_tmB": (tmB, [128, NGA]),
        "t_tmC": (tmC, [128, NGA]), "t_tmD": (tmD, [128, NGA]),
        "t_offB": (offBrow, [1, NG * E]),
        "t_zt": (zt, [128, IMT * HALF]),
        "t_pafa": (pafa, [128, FT * 512]),
        "t_past": (past, [16, 512]),
        "t_sxa": (sxa, [128, FT * 11]),
        "t_vecsd": (vecs, [128, 48]),
        "t_idbf": (idbf, [128, 128]),
        "t_id128": (id128, [128, 128]),
        "t_tB": (tB, [128, 1024]),
    }
    S.barrier(engs="ag")
    for name in dbg:
        tsrc, _shape = tap_srcs[name]
        shape = list(dbg[name].shape)
        ap = tsrc[0:shape[0], 0:shape[1]]
        S.cnt.setdefault("dbg", 0)
        S.mark("g", lambda E_, d=dbg[name], a=ap: E_.dma_start(d, a),
               sem_key="dbg", dma=True)
    if dbg:
        S.items.append(("a", lambda E_: E_.wait_ge(sems["dbg"], S.cnt["dbg"])))

    for i in range(Sched.NST):
        fw = S.cnt[f"w{i}"]
        if fw > 0:
            S.items.append(
                ("a", lambda E_, i=i, fw=fw: E_.wait_ge(sems[f"w{i}"], fw)))

    with es:
        with nc.Block() as block:
            @block.sync
            def _(sync):
                S.replay("s", sync)

            @block.tensor
            def _(t_):
                S.replay("t", t_)

            @block.vector
            def _(v):
                S.replay("v", v)

            @block.scalar
            def _(a):
                S.replay("a", a)

            @block.gpsimd
            def _(g):
                S.replay("g", g)

    return nc, list(dbg.keys())


def _strips(w, nstrip, nk):
    a = np.asarray(w, dtype=np.float32)
    lead = a.shape[:-2]
    a = a.reshape(lead + (nstrip, 128, nk, 128))
    ax = list(range(len(lead))) + [len(lead), len(lead) + 3, len(lead) + 2,
                                   len(lead) + 1]
    a = np.ascontiguousarray(a.transpose(ax))
    return a.reshape(lead + (nstrip, 128, nk * 128))


def _host_common(inputs):
    x = np.ascontiguousarray(np.asarray(inputs["L0_attn_x"], dtype=np.float32))
    xf = x.reshape(NT, D)
    common = {
        "x_all": xf,
        "cin": np.ascontiguousarray(np.asarray(inputs["L0_c"], dtype=np.float32)),
        "wmod": _strips(np.asarray(inputs["L0_wmod"],
                                   dtype=np.float32).T, 48, 8),
        "gwt": np.ascontiguousarray(np.asarray(inputs["L0_attn_gate_w"],
                                               dtype=np.float32).T),
        "gnwt": np.ascontiguousarray(np.asarray(inputs["L0_attn_gate_norm_w"],
                                                dtype=np.float32).reshape(D, 1)),
        "sgw_t": _strips(inputs["L0_attn_shared_gate_w"], 32, 8),
        "sdw_t": _strips(inputs["L0_attn_shared_down_w"], 8, 16),
        "egw_t": _strips(inputs["L0_attn_expert_gate_w"], 32, 8),
        "edw_t": _strips(inputs["L0_attn_expert_down_w"], 8, 16),
        "w1_t": _strips(inputs["L0_ffn_w1"], 32, 8),
        "w2_t": _strips(inputs["L0_ffn_w2"], 8, 32),
    }
    return common, xf


def kernel(**inputs):
    common, xf = _host_common(inputs)
    in_maps = []
    for core in range(NCORE):
        m = dict(common)
        m["x_own"] = np.ascontiguousarray(xf[core * T:(core + 1) * T])
        mk = np.zeros((1, NGA), dtype=np.float32)
        mk[0, : core * NG] = 1.0
        m["maskrow"] = mk
        bs = np.zeros((1, 2), dtype=np.float32)
        bs[0, core // 4] = 1.0
        m["bselrow"] = bs
        in_maps.append(m)

    nc, _ = build_nc()
    res = run_bass_kernel_spmd(nc, in_maps, core_ids=list(range(NCORE)))
    out = np.concatenate([np.asarray(res.results[i]["out"]) for i in range(NCORE)],
                         axis=0)
    return out.reshape(2, 8192, D)
